# version 02
# baseline (speedup 1.0000x reference)
import numpy as np
import ml_dtypes
import concourse.bass as bass
import concourse.mybir as mybir
from concourse.bass_utils import run_bass_kernel_spmd
from contextlib import ExitStack

F32 = mybir.dt.float32
BF16 = mybir.dt.bfloat16
AF = mybir.ActivationFunctionType
ALU = mybir.AluOpType
AXX = mybir.AxisListType.X
BF16NP = ml_dtypes.bfloat16

B, S, E = 4, 4096, 1024
TOWN = 2048
H, DH = 16, 64
HID0 = 3072
DIN, NST, DTR = 2048, 16, 64
NEXP, HIDM = 8, 4096

_DMAQ = {"sync": "qsy", "gpsimd": "qgp", "scalar": "qsc"}


class Op:
    __slots__ = ("stream", "method", "kwargs", "deps", "kind", "ccsem", "snapshot")

    def __init__(self, stream, method, kwargs, deps, kind, ccsem=None):
        self.stream, self.method, self.kwargs = stream, method, kwargs
        self.deps, self.kind, self.ccsem = deps, kind, ccsem


class Buf:
    def __init__(self, pr, h, P, Fr, space):
        self.pr, self.h, self.P, self.F, self.space = pr, h, P, Fr, space
        self.regions = {}

    def _conflicts(self, rect):
        p0, p1, f0, f1 = rect
        return [(r, st) for r, st in self.regions.items()
                if r[0] < p1 and p0 < r[1] and r[2] < f1 and f0 < r[3]]

    def dep_read(self, rect):
        deps = set()
        for _, st in self._conflicts(rect):
            if st[0] is not None:
                deps.add(st[0])
        return deps

    def dep_write(self, rect):
        deps = set()
        for _, st in self._conflicts(rect):
            if st[0] is not None:
                deps.add(st[0])
            deps.update(st[1].items())
        return deps

    def note_read(self, rect, tok):
        st = self.regions.get(rect)
        if st is None:
            st = self.regions.setdefault(rect, [None, {}])
        if st[1].get(tok[0], 0) < tok[1]:
            st[1][tok[0]] = tok[1]

    def note_write(self, rect, tok):
        dead = [r for r in self.regions
                if rect[0] <= r[0] and r[1] <= rect[1] and rect[2] <= r[2] and r[3] <= rect[3]]
        for r in dead:
            del self.regions[r]
        self.regions[rect] = [tok, {}]


class V:
    __slots__ = ("buf", "rect", "ap")

    def __init__(self, buf, p0, p1, f0, f1, ap=None):
        self.buf = buf
        self.rect = (p0, p1, f0, f1)
        self.ap = buf.h[p0:p1, f0:f1] if ap is None else ap


def vw(buf, p0=0, p1=None, f0=0, f1=None):
    return V(buf, p0, buf.P if p1 is None else p1, f0, buf.F if f1 is None else f1)


class FM:

    def __init__(self, pr, name, n, T, dt=F32):
        self.buf = pr.sbuf(name, 128, n * T, dt)
        self.n, self.T = n, T
        self.off = 0

    def v(self, i, f0=0, f1=None):
        f1 = self.T if f1 is None else f1
        return V(self.buf, 0, 128, self.off + i * self.T + f0, self.off + i * self.T + f1)


class FMv:

    def __init__(self, buf, off, n, T):
        self.buf, self.off, self.n, self.T = buf, off, n, T

    def v(self, i, f0=0, f1=None):
        f1 = self.T if f1 is None else f1
        return V(self.buf, 0, 128, self.off + i * self.T + f0, self.off + i * self.T + f1)


class Sub:

    def __init__(self, buf, off, P_, width):
        self.buf, self.off, self.P, self.width = buf, off, P_, width

    def v(self, p0=0, p1=None, f0=0, f1=None):
        return V(self.buf, p0, self.P if p1 is None else p1,
                 self.off + f0, self.off + (self.width if f1 is None else f1))


class Prog:
    def __init__(self):
        self.nc = bass.Bass("TRN2", target_bir_lowering=False, dynamic_dma_scratch_size=16384)
        self.ops = {}
        self.seq = {}
        self.n_cc = 0
        self._uid = 0
        self.pb = [Buf(self, self.nc.alloc_psum_tensor("pb%d" % i, [128, 512], F32),
                       128, 512, "psum") for i in range(8)]

    def uid(self):
        self._uid += 1
        return self._uid

    def dram_in(self, name, shape, dt=F32):
        h = self.nc.dram_tensor(name, list(shape), dt, kind="ExternalInput")
        return Buf(self, h, shape[0], int(np.prod(shape[1:])), "dram")

    def dram_out(self, name, shape, dt=F32):
        h = self.nc.dram_tensor(name, list(shape), dt, kind="ExternalOutput")
        return Buf(self, h, shape[0], int(np.prod(shape[1:])), "dram")

    def dram_tmp(self, name, shape, dt=F32):
        h = self.nc.dram_tensor(name, list(shape), dt, kind="Internal")
        return Buf(self, h, shape[0], int(np.prod(shape[1:])), "dram")

    def sbuf(self, name, P, Fr, dt=F32):
        return Buf(self, self.nc.alloc_sbuf_tensor(name, [P, Fr], dt), P, Fr, "sbuf")

    def psum(self, name, P, Fr, dt=F32):
        return Buf(self, self.nc.alloc_psum_tensor(name, [P, Fr], dt), P, Fr, "psum")

    def _rec(self, stream, issue, method, kwargs, reads, writes, kind, ccsem=None):
        deps = set()
        for x in reads:
            deps |= x.buf.dep_read(x.rect)
        for x in writes:
            deps |= x.buf.dep_write(x.rect)
        self.seq[stream] = self.seq.get(stream, 0) + 1
        tok = (stream, self.seq[stream])
        for x in reads:
            x.buf.note_read(x.rect, tok)
        for x in writes:
            x.buf.note_write(x.rect, tok)
        deps.discard(tok)
        if kind == "dma" and self.seq[stream] > 1:
            deps.add((stream, self.seq[stream] - 1))
        op = Op(stream, method, kwargs, deps, kind, ccsem)
        op.snapshot = {k_: v_ for k_, v_ in self.seq.items() if k_ != stream}
        self.ops.setdefault(issue, []).append(op)
        return tok

    def comp(self, engine, method, kwargs, reads=(), writes=()):
        return self._rec(engine, engine, method, kwargs, reads, writes, "comp")

    def dma(self, q, dst, src):
        return self._rec(_DMAQ[q], q, "dma_start", dict(out=dst.ap, in_=src.ap),
                         [src], [dst], "dma")

    def cc_op(self, kind, src, dst, groups):
        self.n_cc += 1
        ccs = "cc%d" % self.n_cc
        return self._rec(ccs, "gpsimd", "collective_compute",
                         dict(kind=kind, op=ALU.bypass, replica_groups=groups,
                              ins=[src.h.ap().opt()], outs=[dst.h.ap().opt()]),
                         [vw(src)], [vw(dst)], "cc", ccsem=ccs)

    def emit(self, paranoid=False):
        nc = self.nc
        streams = sorted(self.seq)
        final_dma = {q: 16 * self.seq[q] for q in self.seq if q.startswith("q")}
        with ExitStack() as stack:
            sems = {s: stack.enter_context(nc.semaphore("s_" + s)) for s in streams}
            block = stack.enter_context(nc.Block())

            def mk(eng_name):
                def body(eng):
                    guar = {}
                    for op in self.ops.get(eng_name, ()):
                        deps = set(op.snapshot.items()) if paranoid == 1 else op.deps
                        for ds, dq in sorted(deps):
                            if ds == op.stream and ds == "tensor":
                                continue
                            thr = dq * (16 if ds.startswith("q") else 1)
                            if guar.get(ds, 0) >= thr:
                                continue
                            eng.wait_ge(sems[ds], thr)
                            guar[ds] = thr
                        if op.kind == "cc":
                            kw = op.kwargs
                            eng.collective_compute(
                                kw["kind"], kw["op"], replica_groups=kw["replica_groups"],
                                ins=kw["ins"], outs=kw["outs"]).then_inc(sems[op.ccsem])
                        else:
                            getattr(eng, op.method)(**op.kwargs).then_inc(
                                sems[op.stream], 16 if op.kind == "dma" else 1)
                    q = _DMAQ.get(eng_name)
                    if q in final_dma:
                        eng.wait_ge(sems[q], final_dma[q])
                return body

            for eng_name in ("sync", "tensor", "vector", "scalar", "gpsimd"):
                if eng_name in self.ops:
                    getattr(block, eng_name)(mk(eng_name))
        return nc

    def mm(self, out, lhsT, rhs, start=True, stop=True):
        return self.comp("tensor", "matmul",
                         dict(out=out.ap, lhsT=lhsT.ap, rhs=rhs.ap, start=start, stop=stop),
                         [lhsT, rhs], [out])

    def transp(self, out, inp, ident):
        return self.comp("tensor", "transpose",
                         dict(out=out.ap, in_=inp.ap, identity=ident.ap),
                         [inp, ident], [out])

    def act(self, out, inp, func, bias=0.0, scale=1.0, accum=None):
        kw = dict(out=out.ap, in_=inp.ap, func=func)
        reads = [inp]
        for k, val in (("bias", bias), ("scale", scale)):
            if isinstance(val, V):
                kw[k] = val.ap
                reads.append(val)
            else:
                kw[k] = float(val)
        writes = [out]
        if accum is not None:
            kw["accum_out"] = accum.ap
            writes.append(accum)
        return self.comp("scalar", "activation", kw, reads, writes)

    def vtt(self, out, a, b, op, engine="vector"):
        return self.comp(engine, "tensor_tensor",
                         dict(out=out.ap, in0=a.ap, in1=b.ap, op=op), [a, b], [out])

    def vts(self, out, a, s1, s2, op0, op1=None, engine="vector"):
        kw = dict(out=out.ap, in0=a.ap,
                  scalar1=(s1.ap if isinstance(s1, V) else float(s1)),
                  scalar2=(s2.ap if isinstance(s2, V) else (None if s2 is None else float(s2))),
                  op0=op0)
        if op1 is not None:
            kw["op1"] = op1
        reads = [a] + [x for x in (s1, s2) if isinstance(x, V)]
        return self.comp(engine, "tensor_scalar", kw, reads, [out])

    def stt(self, out, in0, scalar, in1, op0, op1, engine="vector"):
        reads = [in0, in1] + ([scalar] if isinstance(scalar, V) else [])
        return self.comp(engine, "scalar_tensor_tensor",
                         dict(out=out.ap, in0=in0.ap,
                              scalar=(scalar.ap if isinstance(scalar, V) else float(scalar)),
                              in1=in1.ap, op0=op0, op1=op1), reads, [out])

    def scan_op(self, out, d0, d1, initial):
        reads = [d0, d1] + ([initial] if isinstance(initial, V) else [])
        return self.comp("vector", "tensor_tensor_scan",
                         dict(out=out.ap, data0=d0.ap, data1=d1.ap,
                              initial=(initial.ap if isinstance(initial, V) else float(initial)),
                              op0=ALU.mult, op1=ALU.add), reads, [out])

    def copy(self, out, inp, engine="vector"):
        meth = "copy" if engine == "scalar" else "tensor_copy"
        return self.comp(engine, meth, dict(out=out.ap, in_=inp.ap), [inp], [out])

    def reduce(self, out, inp, op):
        return self.comp("vector", "tensor_reduce",
                         dict(out=out.ap, in_=inp.ap, axis=AXX, op=op), [inp], [out])

    def recip(self, out, inp):
        return self.comp("vector", "reciprocal", dict(out=out.ap, in_=inp.ap), [inp], [out])

    def memset(self, out, val, engine="vector"):
        return self.comp(engine, "memset", dict(ap=out.ap, constant=val), [], [out])


def dview(dbuf, sl_ap):
    return V(dbuf, 0, dbuf.P, 0, dbuf.F, ap=sl_ap)


def rearr_block(dh, c0, c1, p=128):
    return dh[:, c0:c1].rearrange("(a p) c -> p a c", p=p)



def ph_consts(pr, bf=False):
    cst = pr.dram_in("consts", [128, 648], F32)
    sb = pr.sbuf("cst", 128, 648)
    pr.dma("sync", vw(sb), vw(cst))
    d = dict(ones_col=V(sb, 0, 128, 0, 1), ones_row=V(sb, 0, 1, 8, 136),
             ones_row_at=lambda p: V(sb, p, p + 1, 8, 136),
             ones512_at=lambda p: V(sb, p, p + 1, 8, 520),
             eps_at=lambda p: V(sb, p, p + 1, 1, 2),
             ident=V(sb, 0, 128, 520, 648),
             identN=lambda n_: V(sb, 0, n_, 520, 520 + n_), sb=sb)
    if bf:
        sbb = pr.sbuf("cstbf", 128, 648, BF16)
        pr.copy(vw(sbb), vw(sb))
        d["ones_col_bf"] = V(sbb, 0, 128, 0, 1)
        d["ident_bf"] = V(sbb, 0, 128, 520, 648)
        d["sbb"] = sbb
    return d


def ph_ln_stream(pr, tag, getx, n, T, sink, C, banks=(0, 1, 2, 3), gamma=None,
                 beta=None, eps=1e-5, dbg=None, in_bf=False):
    u = pr.uid()
    if getattr(pr, "ln_scratch", None) is None:
        pr.ln_scratch = (pr.sbuf("lnsq%d" % u, 128, 512),
                         pr.sbuf("lnst%d" % u, 128, 2048),
                         pr.sbuf("lntm%d" % u, 128, 512))
    sq, stat, tmp = pr.ln_scratch
    ps_s = V(pr.pb[banks[0]], 0, 1, 0, 512)
    ps_q = V(pr.pb[banks[1]], 0, 1, 0, 512)
    ps_r1 = pr.pb[banks[2]]
    ps_r2 = pr.pb[banks[3]]

    def row(k):
        return V(stat, 0, 1, k * 512, (k + 1) * 512)

    for bi in range(T // 512):
        f0, f1 = bi * 512, (bi + 1) * 512
        oc = C["ones_col_bf"] if in_bf else C["ones_col"]
        for i in range(n):
            xv = getx(i, f0, f1)
            pr.mm(ps_s, oc, xv, start=(i == 0), stop=(i == n - 1))
            pr.act(vw(sq), xv, AF.Square)
            pr.mm(ps_q, C["ones_col"], vw(sq), start=(i == 0), stop=(i == n - 1))
        inv = 1.0 / (128 * n)
        pr.vts(row(0), ps_s, inv, None, ALU.mult)
        pr.vts(row(1), ps_q, inv, None, ALU.mult)
        pr.vtt(row(2), row(0), row(0), ALU.mult)
        pr.vtt(row(1), row(1), row(2), ALU.subtract)
        pr.act(row(1), row(1), AF.Ln, bias=C["eps_at"](0))
        pr.act(row(1), row(1), AF.Exp, scale=-0.5)
        pr.stt(row(3), row(0), -1.0, row(1), ALU.mult, ALU.mult)
        if dbg is not None and bi == 0:
            dbg("stat0", vw(stat))
        pr.mm(vw(ps_r1), C["ones_row"], row(1))
        pr.mm(vw(ps_r2), C["ones_row"], row(3))
        for i in range(n):
            xv = getx(i, f0, f1)
            ov = sink(i, f0, f1)
            if gamma is None:
                pr.vtt(vw(tmp), xv, vw(ps_r1), ALU.mult)
                pr.vtt(ov, vw(tmp), vw(ps_r2), ALU.add)
            else:
                pr.vtt(vw(tmp), xv, vw(ps_r1), ALU.mult)
                pr.vtt(vw(tmp), vw(tmp), vw(ps_r2), ALU.add)
                pr.vts(ov, vw(tmp), gamma(i), beta(i), ALU.mult, ALU.add)


class WLoader:

    def __init__(self, pr, tag, wide, nslots=3, conv_engine="vector"):
        self.pr = pr
        self.wide = wide
        self.nslots = nslots
        self.f32 = pr.sbuf("wl32%s" % tag, 128, wide * nslots)
        self.b16 = pr.sbuf("wlbf%s" % tag, 128, wide * nslots, BF16)
        self.k = 0
        self.conv_engine = conv_engine

    def fetch(self, dram_buf, src_ap, width, q="sync"):
        s = self.k % self.nslots
        self.k += 1
        dst32 = V(self.f32, 0, 128, s * self.wide, s * self.wide + width)
        dstbf = V(self.b16, 0, 128, s * self.wide, s * self.wide + width)
        self.pr.dma(q, dst32, dview(dram_buf, src_ap))
        self.pr.copy(dstbf, dst32, engine=self.conv_engine)
        return dstbf


def ph_l0(pr, C, x0T_d, wgT_d, wuT_d, wdT_d, xn0T, g0T):
    ld = [pr.sbuf("l0x%d" % k, 128, 512) for k in range(4)]
    cache = {}

    def getx(i, f0, f1, _c=cache):
        key = (i, f0)
        if key in _c:
            return _c[key]
        slot = ld[(i + (f0 // 512) * 3) % 4]
        dst = vw(slot, 0, 128, 0, f1 - f0)
        pr.dma("sync", dst, dview(x0T_d, x0T_d.h[i * 128:(i + 1) * 128, f0:f1]))
        _c.clear()
        _c[key] = dst
        return dst

    def sink(i, f0, f1):
        return xn0T.v(i, f0, f1)

    ph_ln_stream(pr, "ln0", getx, 8, S, sink, C)

    wl = WLoader(pr, "f0", 1024, 2)
    psg = [pr.pb[4], pr.pb[5]]
    psu = [pr.pb[6], pr.pb[7]]
    psd = [pr.pb[0], pr.pb[1]]
    sg = pr.sbuf("f0sg", 128, 1024)
    hbuf = FM(pr, "f0h", 24, 1024, BF16)
    wdl = WLoader(pr, "fd", 128, 4)

    for tv in range(2):
        t0 = tv * 1024
        for ht in range(24):
            wg = wl.fetch(wgT_d, rearr_block(wgT_d.h, ht * 128, (ht + 1) * 128), 1024, q="sync")
            wu = wl.fetch(wuT_d, rearr_block(wuT_d.h, ht * 128, (ht + 1) * 128), 1024, q="gpsimd")
            for ec in range(8):
                lg = V(wg.buf, 0, 128, wg.rect[2] + ec * 128, wg.rect[2] + (ec + 1) * 128)
                lu = V(wu.buf, 0, 128, wu.rect[2] + ec * 128, wu.rect[2] + (ec + 1) * 128)
                for tb in range(2):
                    rhs = xn0T.v(ec, t0 + tb * 512, t0 + (tb + 1) * 512)
                    pr.mm(vw(psg[tb]), lg, rhs, start=(ec == 0), stop=(ec == 7))
                    pr.mm(vw(psu[tb]), lu, rhs, start=(ec == 0), stop=(ec == 7))
            for tb in range(2):
                pr.act(V(sg, 0, 128, tb * 512, (tb + 1) * 512), vw(psg[tb]),
                       AF.Silu, scale=1.0 / 127.0)
                pr.vtt(hbuf.v(ht, tb * 512, (tb + 1) * 512),
                       V(sg, 0, 128, tb * 512, (tb + 1) * 512), vw(psu[tb]), ALU.mult)
        for et in range(8):
            for ht in range(24):
                wd = wdl.fetch(wdT_d, wdT_d.h[ht * 128:(ht + 1) * 128, et * 128:(et + 1) * 128],
                               128, q="scalar")
                for tb in range(2):
                    pr.mm(vw(psd[tb]), wd, hbuf.v(ht, tb * 512, (tb + 1) * 512),
                          start=(ht == 0), stop=(ht == 23))
            for tb in range(2):
                pr.act(g0T.v(et, t0 + tb * 512, t0 + (tb + 1) * 512), vw(psd[tb]),
                       AF.Silu, scale=1.0 / (127.0 * 127.0))



def _prep_common(I):
    f32 = np.float32
    P = {}
    x0 = np.asarray(I["L0_attn_x"], f32)
    x0T = np.ascontiguousarray(x0.transpose(0, 2, 1))
    P["x0T_rot"] = [np.ascontiguousarray(
        np.concatenate([x0T[c // 2][:, (c % 2) * TOWN:(c % 2) * TOWN + TOWN],
                        x0T[c // 2][:, ((c % 2) ^ 1) * TOWN:((c % 2) ^ 1) * TOWN + TOWN]],
                       axis=1)) for c in range(8)]
    P["wgT"] = np.ascontiguousarray(np.asarray(I["L0_ffn_w_gate"], np.int32).astype(f32).T)
    P["wuT"] = np.ascontiguousarray(np.asarray(I["L0_ffn_w_up"], np.int32).astype(f32).T)
    P["wdT"] = np.ascontiguousarray(np.asarray(I["L0_ffn_w_down"], np.int32).astype(f32).T)
    consts = np.zeros((128, 648), f32)
    consts[:, 0] = 1.0
    consts[:, 1] = 1e-5
    for p_ in (0, 32, 64, 96):
        consts[p_, 8:520] = 1.0
    consts[:, 520:648] = np.eye(128, dtype=f32)
    P["consts"] = consts
    return P


def build_launch_A():
    pr = Prog()
    C = ph_consts(pr)
    x0T_d = pr.dram_in("x0T", [E, S])
    wgT_d = pr.dram_in("wgT", [E, HID0])
    wuT_d = pr.dram_in("wuT", [E, HID0])
    wdT_d = pr.dram_in("wdT", [HID0, E])
    xn0T_o = pr.dram_out("xn0T_o", [E, S], BF16)
    g0T_o = pr.dram_out("g0T_o", [E, TOWN], BF16)
    xn0T = FM(pr, "xn0T", 8, S, BF16)
    g0T = FM(pr, "g0T", 8, TOWN, BF16)
    ph_l0(pr, C, x0T_d, wgT_d, wuT_d, wdT_d, xn0T, g0T)
    for i in range(8):
        pr.dma("gpsimd", dview(xn0T_o, xn0T_o.h[i * 128:(i + 1) * 128, :]), xn0T.v(i))
        pr.dma("scalar", dview(g0T_o, g0T_o.h[i * 128:(i + 1) * 128, :]), g0T.v(i))
    return pr


def ph_attn(pr, C, xn0T, W2sh_d, cvec_d, wqT_d, wkT_d, wvT_d, s0preT):
    wl = WLoader(pr, "at", 512, 3)
    qT = pr.sbuf("at_q", 64, TOWN, BF16)
    kT = pr.sbuf("at_k", 64, S, BF16)
    vhat = pr.sbuf("at_v", 128, 32 * 64, BF16)
    vtmp = pr.sbuf("at_vt", 64, 512)
    band = pr.sbuf("at_bd", 128, 16 * 512, BF16)
    bb32 = pr.sbuf("at_b32", 128, 2 * 512)
    ccol = pr.sbuf("at_cc", 128, 1)
    expS = pr.sbuf("at_es", 128, 2 * 512, BF16)
    vsum = pr.sbuf("at_vs", 1, 64)
    rzrep = pr.sbuf("at_rr", 128, 512)
    tmp = pr.sbuf("at_tmp", 128, 512)

    psP = pr.pb[0]
    psT = pr.pb[1]
    psS = [pr.pb[6], pr.pb[7]]
    psA = pr.pb[4]
    psB = pr.pb[5]
    psZb = pr.pb[3]
    psR = pr.pb[2]

    for h in range(H):
        p0 = (h % 2) * 64
        et = h // 2
        pr.dma("gpsimd", vw(ccol), dview(cvec_d, bass.AP(cvec_d.h, h, [[0, 128], [1, 1]])))
        for d in range(16):
            dst = V(bb32, 0, 128, (d % 2) * 512, (d % 2) * 512 + 512)
            pr.dma("sync", dst, dview(W2sh_d,
                                      W2sh_d.h[h * 128:(h + 1) * 128,
                                               1920 - 128 * d:2432 - 128 * d]))
            pr.vts(V(band, 0, 128, d * 512, (d + 1) * 512), dst, vw(ccol), None,
                   ALU.subtract)
        wq = wl.fetch(wqT_d, rearr_block(wqT_d.h, h * 64, (h + 1) * 64), 512, q="sync")
        wk = wl.fetch(wkT_d, rearr_block(wkT_d.h, h * 64, (h + 1) * 64), 512, q="gpsimd")
        wv = wl.fetch(wvT_d, rearr_block(wvT_d.h, h * 64, (h + 1) * 64), 512, q="scalar")

        def proj(wbf, nblk, dst_row):
            for qb in range(nblk):
                pp = V(psP, 0, 64, 0, 512)
                for ec in range(8):
                    lw = V(wbf.buf, 0, 128, wbf.rect[2] + ec * 64, wbf.rect[2] + (ec + 1) * 64)
                    pr.mm(pp, lw, xn0T.v(ec, qb * 512, (qb + 1) * 512),
                          start=(ec == 0), stop=(ec == 7))
                pr.act(V(dst_row, 0, 64, qb * 512, (qb + 1) * 512), pp, AF.Copy)

        proj(wq, 4, qT)
        proj(wk, 8, kT)
        for qb in range(8):
            pp = V(psP, 0, 64, 0, 512)
            for ec in range(8):
                lw = V(wv.buf, 0, 128, wv.rect[2] + ec * 64, wv.rect[2] + (ec + 1) * 64)
                pr.mm(pp, lw, xn0T.v(ec, qb * 512, (qb + 1) * 512),
                      start=(ec == 0), stop=(ec == 7))
            pr.copy(vw(vtmp), pp)
            for j in range(4):
                tc = qb * 4 + j
                pt = V(psT, 0, 128, 0, 64)
                pr.transp(pt, V(vtmp, 0, 64, j * 128, (j + 1) * 128), C["identN"](64))
                pr.copy(V(vhat, 0, 128, tc * 64, (tc + 1) * 64), pt)
        pc = V(psZb, 0, 1, 0, 64)
        for tc in range(32):
            pr.mm(pc, C["ones_col_bf"], V(vhat, 0, 128, tc * 64, (tc + 1) * 64),
                  start=(tc == 0), stop=(tc == 31))
        pr.vts(vw(vsum), pc, V(ccol, 0, 1, 0, 1), None, ALU.mult)
        for qb in range(4):
            pA = V(psA, p0, p0 + 64, 0, 512)
            pZ = V(psZb, p0, p0 + 1, 0, 512)
            for kc in range(32):
                pS = vw(psS[kc % 2])
                pr.mm(pS, V(kT, 0, 64, kc * 128, (kc + 1) * 128),
                      V(qT, 0, 64, qb * 512, (qb + 1) * 512))
                es = V(expS, 0, 128, (kc % 2) * 512, (kc % 2) * 512 + 512)
                pr.act(es, pS, AF.Exp, scale=1.0 / 32.0)
                pr.mm(pA, V(vhat, 0, 128, kc * 64, (kc + 1) * 64), es,
                      start=(kc == 0), stop=(kc == 31))
                pr.mm(pZ, C["ones_col_bf"], es, start=(kc == 0), stop=(kc == 31))
            pB = V(psB, p0, p0 + 64, 0, 512)
            bl = list(range(16 + 4 * qb, 32))
            for j, kc in enumerate(bl):
                d = kc - 16 - 4 * qb
                pr.mm(pB, V(vhat, 0, 128, kc * 64, (kc + 1) * 64),
                      V(band, 0, 128, d * 512, (d + 1) * 512),
                      start=(j == 0), stop=False)
            pr.mm(pB, vw(vsum), C["ones512_at"](0), start=False, stop=True)
            rzs = V(tmp, p0, p0 + 1, 0, 512)
            pr.recip(rzs, pZ)
            pr.mm(vw(psR), C["ones_row_at"](p0), rzs)
            pr.copy(vw(rzrep), vw(psR))
            o1 = V(s0preT.buf, p0, p0 + 64,
                   et * TOWN + qb * 512, et * TOWN + (qb + 1) * 512)
            pr.vtt(o1, pA, V(rzrep, p0, p0 + 64, 0, 512), ALU.mult)
            pr.vtt(o1, o1, pB, ALU.add)


def build_launch_B():
    pr = Prog()
    C = ph_consts(pr, bf=True)
    xn0T_d = pr.dram_in("xn0T", [E, S], BF16)
    wqT_d = pr.dram_in("wqT", [E, E])
    wkT_d = pr.dram_in("wkT", [E, E])
    wvT_d = pr.dram_in("wvT", [E, E])
    W2sh_d = pr.dram_in("W2sh", [H * 128, 2432])
    cvec_d = pr.dram_in("cvec", [1, 16])
    gb_d = pr.dram_in("gb", [128, 16])
    s0T_o = pr.dram_out("s0T_o", [E, TOWN])
    xn0T = FM(pr, "xn0Ts", 8, S, BF16)
    for i in range(8):
        pr.dma("sync", xn0T.v(i), dview(xn0T_d, xn0T_d.h[i * 128:(i + 1) * 128, :]))
    gbsb = pr.sbuf("gbsb", 128, 16)
    pr.dma("scalar", vw(gbsb), vw(gb_d))
    s0preT = FM(pr, "s0pre", 8, TOWN, F32)
    ph_attn(pr, C, xn0T, W2sh_d, cvec_d, wqT_d, wkT_d, wvT_d, s0preT)
    ph_ln_stream(pr, "s0ln", lambda i, f0, f1: s0preT.v(i, f0, f1), 8, TOWN,
                 lambda i, f0, f1: s0preT.v(i, f0, f1), C,
                 gamma=lambda i: V(gbsb, 0, 128, i, i + 1),
                 beta=lambda i: V(gbsb, 0, 128, 8 + i, 9 + i))
    for i in range(8):
        pr.dma("gpsimd", dview(s0T_o, s0T_o.h[i * 128:(i + 1) * 128, :]), s0preT.v(i))
    return pr


def ph_combine(pr, C, x0T_d, s0T_d, g0T_d, x1T):
    l0 = pr.sbuf("cb0", 128, 512)
    l1 = pr.sbuf("cb1", 128, 512)
    l2 = pr.sbuf("cb2", 128, 512, BF16)
    for i in range(8):
        for bk in range(4):
            f0, f1 = bk * 512, (bk + 1) * 512
            pr.dma("sync", vw(l0), dview(x0T_d, x0T_d.h[i * 128:(i + 1) * 128, f0:f1]))
            pr.dma("gpsimd", vw(l1), dview(s0T_d, s0T_d.h[i * 128:(i + 1) * 128, f0:f1]))
            pr.dma("scalar", vw(l2), dview(g0T_d, g0T_d.h[i * 128:(i + 1) * 128, f0:f1]))
            ov = x1T.v(i, f0, f1)
            pr.vtt(ov, vw(l1), vw(l2), ALU.mult)
            pr.vtt(ov, ov, vw(l0), ALU.add)


def build_launch_C():
    pr = Prog()
    C = ph_consts(pr, bf=True)
    x0T_d = pr.dram_in("x0T", [E, S])
    s0T_d = pr.dram_in("s0T", [E, TOWN])
    g0T_d = pr.dram_in("g0T", [E, TOWN], BF16)
    lgb_d = pr.dram_in("lgb", [128, 16])
    ipT_d = pr.dram_in("ipT", [E, 2 * DIN])
    x1T_o = pr.dram_out("x1T_o", [E, TOWN])
    xn1T_o = pr.dram_out("xn1T_o", [E, TOWN], BF16)
    xpT_o = pr.dram_out("xpT_o", [DIN, TOWN], BF16)
    zsT_o = pr.dram_out("zsT_o", [DIN, TOWN], BF16)

    x1T = FM(pr, "x1T", 8, TOWN, F32)
    xn1T = FM(pr, "xn1T", 8, TOWN, BF16)
    ln2T = FM(pr, "ln2T", 8, TOWN, BF16)
    lgb = pr.sbuf("lgbs", 128, 16)
    pr.dma("scalar", vw(lgb), vw(lgb_d))

    ph_combine(pr, C, x0T_d, s0T_d, g0T_d, x1T)
    for i in range(8):
        pr.dma("sync", dview(x1T_o, x1T_o.h[i * 128:(i + 1) * 128, :]), x1T.v(i))
    ph_ln_stream(pr, "ln1", lambda i, f0, f1: x1T.v(i, f0, f1), 8, TOWN,
                 lambda i, f0, f1: xn1T.v(i, f0, f1), C)
    for i in range(8):
        pr.dma("gpsimd", dview(xn1T_o, xn1T_o.h[i * 128:(i + 1) * 128, :]), xn1T.v(i))
    ph_ln_stream(pr, "ln2", lambda i, f0, f1: xn1T.v(i, f0, f1), 8, TOWN,
                 lambda i, f0, f1: ln2T.v(i, f0, f1), C,
                 gamma=lambda i: V(lgb, 0, 128, i, i + 1),
                 beta=lambda i: V(lgb, 0, 128, 8 + i, 9 + i), in_bf=True)
    wl = WLoader(pr, "ip", 1024, 2)
    ps4 = [pr.pb[4], pr.pb[5], pr.pb[6], pr.pb[7]]
    stage = pr.sbuf("ipst", 128, 2048, BF16)
    for rt in range(32):
        wb = wl.fetch(ipT_d, rearr_block(ipT_d.h, rt * 128, (rt + 1) * 128), 1024,
                      q=("sync", "gpsimd")[rt % 2])
        for ec in range(8):
            lw = V(wb.buf, 0, 128, wb.rect[2] + ec * 128, wb.rect[2] + (ec + 1) * 128)
            for tb in range(4):
                pr.mm(vw(ps4[tb]), lw, ln2T.v(ec, tb * 512, (tb + 1) * 512),
                      start=(ec == 0), stop=(ec == 7))
        for tb in range(4):
            sv = V(stage, 0, 128, tb * 512, (tb + 1) * 512)
            if rt < 16:
                pr.act(sv, vw(ps4[tb]), AF.Copy)
            else:
                pr.act(sv, vw(ps4[tb]), AF.Silu)
        dst = xpT_o if rt < 16 else zsT_o
        r0 = (rt % 16) * 128
        pr.dma("scalar", dview(dst, dst.h[r0:r0 + 128, :]), vw(stage))
    return pr


def build_launch_D():
    pr = Prog()
    C = ph_consts(pr, bf=True)
    xpT_d = pr.dram_in("xpT", [DIN, TOWN], BF16)
    halo_d = pr.dram_in("halo", [DIN, 3], BF16)
    convw_d = pr.dram_in("convw", [DIN, 4])
    cols_d = pr.dram_in("cols", [128, 32])
    xprojT_d = pr.dram_in("xprojT", [DIN, 96])
    dtprojT_d = pr.dram_in("dtprojT", [DTR, DIN])
    uT_o = pr.dram_out("uT_o", [DIN, TOWN], BF16)
    dtT_o = pr.dram_out("dtT_o", [DIN, TOWN])
    bcT_o = pr.dram_out("bcT_o", [32, TOWN])

    cw = pr.sbuf("cw", 128, 64)
    pr.dma("sync", vw(cw), dview(convw_d, convw_d.h[:, :].rearrange("(a p) k -> p a k", p=128)))
    cols = pr.sbuf("colss", 128, 32)
    pr.dma("sync", vw(cols), vw(cols_d))

    uT = FM(pr, "uT", 16, TOWN, BF16)
    cbuf = pr.sbuf("cbuf", 128, 3 + TOWN, BF16)
    acc = pr.sbuf("acc", 128, TOWN)
    for dt_ in range(16):
        pr.dma("sync", V(cbuf, 0, 128, 0, 3),
               dview(halo_d, halo_d.h[dt_ * 128:(dt_ + 1) * 128, :]))
        pr.dma("sync", V(cbuf, 0, 128, 3, 3 + TOWN),
               dview(xpT_d, xpT_d.h[dt_ * 128:(dt_ + 1) * 128, :]))

        def win(k):
            return V(cbuf, 0, 128, k, k + TOWN)

        pr.vts(vw(acc), win(0), V(cw, 0, 128, dt_ * 4, dt_ * 4 + 1), None, ALU.mult)
        for k in range(1, 4):
            pr.stt(vw(acc), win(k), V(cw, 0, 128, dt_ * 4 + k, dt_ * 4 + k + 1),
                   vw(acc), ALU.mult, ALU.add)
        pr.act(uT.v(dt_), vw(acc), AF.Silu, bias=V(cols, 0, 128, dt_, dt_ + 1))
        pr.dma("gpsimd", dview(uT_o, uT_o.h[dt_ * 128:(dt_ + 1) * 128, :]), uT.v(dt_))
    wx32 = pr.sbuf("wx32", 128, 16 * 96)
    wx = pr.sbuf("wx", 128, 16 * 96, BF16)
    pr.dma("sync", wx32.v(), dview(xprojT_d, xprojT_d.h[:, :].rearrange("(a p) c -> p a c", p=128)))
    pr.copy(vw(wx), vw(wx32))
    xdbl = pr.sbuf("xdbl", 96, TOWN)
    psX = pr.pb[0]
    for tb in range(4):
        px = V(psX, 0, 96, 0, 512)
        for dc in range(16):
            pr.mm(px, V(wx, 0, 128, dc * 96, (dc + 1) * 96),
                  uT.v(dc, tb * 512, (tb + 1) * 512), start=(dc == 0), stop=(dc == 15))
        pr.copy(V(xdbl, 0, 96, tb * 512, (tb + 1) * 512), px)
    pr.dma("gpsimd", dview(bcT_o, bcT_o.h[:, :]), V(xdbl, 64, 96, 0, TOWN))
    dtp = pr.sbuf("dtp", 64, TOWN, BF16)
    pr.copy(vw(dtp), V(xdbl, 0, 64, 0, TOWN))
    wd32 = pr.sbuf("wd32", 64, DIN)
    wdt = pr.sbuf("wdt", 64, DIN, BF16)
    pr.dma("sync", vw(wd32), vw(dtprojT_d))
    pr.copy(vw(wdt), vw(wd32))
    psD = pr.pb[1]
    s1 = pr.sbuf("sp1", 128, 512)
    s2 = pr.sbuf("sp2", 128, 512)
    s3 = pr.sbuf("sp3", 128, 512)
    stg = pr.sbuf("dtstage", 128, TOWN)
    for dt_ in range(16):
        for tb in range(4):
            pd = vw(psD)
            pr.mm(pd, V(wdt, 0, 64, dt_ * 128, (dt_ + 1) * 128),
                  V(dtp, 0, 64, tb * 512, (tb + 1) * 512))
            pr.act(vw(s1), pd, AF.Identity, bias=V(cols, 0, 128, 16 + dt_, 17 + dt_))
            pr.act(vw(s2), vw(s1), AF.Abs)
            pr.act(vw(s2), vw(s2), AF.Exp, scale=-1.0)
            pr.act(vw(s2), vw(s2), AF.Ln, bias=1.0)
            pr.act(vw(s3), vw(s1), AF.Relu)
            pr.vtt(V(stg, 0, 128, tb * 512, (tb + 1) * 512), vw(s3), vw(s2), ALU.add)
        pr.dma("scalar", dview(dtT_o, dtT_o.h[dt_ * 128:(dt_ + 1) * 128, :]), vw(stg))
    return pr


def build_launch_E():
    pr = Prog()
    C = ph_consts(pr)
    dtS_d = pr.dram_in("dtS", [E, S])
    uS_d = pr.dram_in("uS", [E, S], BF16)
    bS_d = pr.dram_in("bS", [NST, S])
    cS_d = pr.dram_in("cS", [NST, S])
    alog_d = pr.dram_in("alog", [128, 8 * NST])
    dD_d = pr.dram_in("dD", [128, 8])
    yS_o = pr.dram_out("yS_o", [E, S], BF16)

    asb = pr.sbuf("asb", 128, 8 * NST)
    pr.dma("sync", vw(asb), vw(alog_d))
    pr.act(vw(asb), vw(asb), AF.Exp)
    pr.vts(vw(asb), vw(asb), -1.0, None, ALU.mult)
    dsb = pr.sbuf("dsb", 128, 8)
    pr.dma("sync", vw(dsb), vw(dD_d))

    dts = pr.sbuf("dts", 128, S)
    us = pr.sbuf("us", 128, S, BF16)
    dub = pr.sbuf("dub", 128, S, BF16)
    yac = pr.sbuf("yac", 128, S)
    hbf = pr.sbuf("hbf", 128, S)
    da = pr.sbuf("da", 128, 2 * 512)
    db = pr.sbuf("db", 128, 2 * 512)
    brow = pr.sbuf("brow", 1, S)
    crow = pr.sbuf("crow", 1, S)
    ys = pr.sbuf("ys", 128, S, BF16)
    psB_ = pr.pb[0]
    psC_ = pr.pb[1]

    for dt_ in range(8):
        pr.dma("sync", vw(dts), dview(dtS_d, dtS_d.h[dt_ * 128:(dt_ + 1) * 128, :]))
        pr.dma("gpsimd", vw(us), dview(uS_d, uS_d.h[dt_ * 128:(dt_ + 1) * 128, :]))
        pr.vtt(vw(dub), vw(dts), vw(us), ALU.mult)
        for n in range(NST):
            pr.dma("sync", vw(brow), dview(bS_d, bS_d.h[n:n + 1, :]))
            pr.dma("gpsimd", vw(crow), dview(cS_d, cS_d.h[n:n + 1, :]))
            acol = V(asb, 0, 128, dt_ * NST + n, dt_ * NST + n + 1)
            for tc in range(8):
                f0, f1 = tc * 512, (tc + 1) * 512
                pB = vw(psB_)
                pC = vw(psC_)
                pr.mm(pB, C["ones_row"], V(brow, 0, 1, f0, f1))
                pr.mm(pC, C["ones_row"], V(crow, 0, 1, f0, f1))
                dav = V(da, 0, 128, (tc % 2) * 512, (tc % 2) * 512 + 512)
                pr.act(dav, V(dts, 0, 128, f0, f1), AF.Exp, scale=acol)
                dbv = V(db, 0, 128, (tc % 2) * 512, (tc % 2) * 512 + 512)
                pr.vtt(dbv, V(dub, 0, 128, f0, f1), pB, ALU.mult)
                ini = 0.0 if tc == 0 else V(hbf, 0, 128, f0 - 1, f0)
                pr.scan_op(V(hbf, 0, 128, f0, f1), dav, dbv, ini)
                if n == 0:
                    pr.vtt(V(yac, 0, 128, f0, f1), V(hbf, 0, 128, f0, f1), pC, ALU.mult)
                else:
                    pr.vtt(dav, V(hbf, 0, 128, f0, f1), pC, ALU.mult)
                    pr.vtt(V(yac, 0, 128, f0, f1), V(yac, 0, 128, f0, f1), dav, ALU.add)
        dcol = V(dsb, 0, 128, dt_, dt_ + 1)
        pr.stt(vw(ys), vw(us), dcol, vw(yac), ALU.mult, ALU.add)
        pr.dma("scalar", dview(yS_o, yS_o.h[dt_ * 128:(dt_ + 1) * 128, :]), vw(ys))
    return pr


def build_launch_F():
    pr = Prog()
    C = ph_consts(pr)
    y2_d = pr.dram_in("y2", [DIN, TOWN], BF16)
    zsT_d = pr.dram_in("zsT", [DIN, TOWN], BF16)
    opT_d = pr.dram_in("opT", [DIN, E])
    gwT_d = pr.dram_in("gwT", [E, DIN])
    gbias_d = pr.dram_in("gbias", [128, 16])
    xn1T_d = pr.dram_in("xn1T", [E, TOWN], BF16)
    s1T_o = pr.dram_out("s1T_o", [E, TOWN])

    gbias = pr.sbuf("gbiass", 128, 16)
    pr.dma("scalar", vw(gbias), vw(gbias_d))
    xn1T = FM(pr, "xn1Ts", 8, TOWN, BF16)
    for i in range(8):
        pr.dma("sync", xn1T.v(i), dview(xn1T_d, xn1T_d.h[i * 128:(i + 1) * 128, :]))

    ygT = FM(pr, "ygT", 16, TOWN, BF16)
    l0 = pr.sbuf("fl0", 128, TOWN, BF16)
    l1 = pr.sbuf("fl1", 128, TOWN, BF16)
    for dc in range(16):
        pr.dma("sync", vw(l0), dview(y2_d, y2_d.h[dc * 128:(dc + 1) * 128, :]))
        pr.dma("gpsimd", vw(l1), dview(zsT_d, zsT_d.h[dc * 128:(dc + 1) * 128, :]))
        pr.vtt(ygT.v(dc), vw(l0), vw(l1), ALU.mult)
    mgT = FM(pr, "mgT", 8, TOWN, BF16)
    wl = WLoader(pr, "op", 1024, 2)
    ps4 = [pr.pb[4], pr.pb[5], pr.pb[6], pr.pb[7]]
    for et in range(8):
        for hf in range(2):
            wb = wl.fetch(opT_d,
                          opT_d.h[hf * 1024:(hf + 1) * 1024, et * 128:(et + 1) * 128]
                          .rearrange("(a p) c -> p a c", p=128), 1024,
                          q=("sync", "gpsimd")[hf])
            for dcl in range(8):
                dc = hf * 8 + dcl
                lw = V(wb.buf, 0, 128, wb.rect[2] + dcl * 128, wb.rect[2] + (dcl + 1) * 128)
                for tb in range(4):
                    pr.mm(vw(ps4[tb]), lw, ygT.v(dc, tb * 512, (tb + 1) * 512),
                          start=(dc == 0), stop=(dc == 15))
        for tb in range(4):
            pr.act(mgT.v(et, tb * 512, (tb + 1) * 512), vw(ps4[tb]), AF.Gelu)
    sgT = FM(pr, "sgT", 8, TOWN, BF16)
    wl2 = WLoader(pr, "gl", 1024, 2)
    tmpf = pr.sbuf("ftmp", 128, 512)
    stage = pr.sbuf("fstage", 128, 512)
    for half in range(2):
        for rt in range(8):
            wb = wl2.fetch(gwT_d, rearr_block(gwT_d.h, (1 - half) * 1024 + rt * 128,
                                              (1 - half) * 1024 + (rt + 1) * 128), 1024,
                           q=("sync", "gpsimd")[rt % 2])
            for ec in range(8):
                lw = V(wb.buf, 0, 128, wb.rect[2] + ec * 128, wb.rect[2] + (ec + 1) * 128)
                for tb in range(4):
                    pr.mm(vw(ps4[tb]), lw, mgT.v(ec, tb * 512, (tb + 1) * 512),
                          start=(ec == 0), stop=(ec == 7))
            if half == 0:
                for tb in range(4):
                    pr.act(sgT.v(rt, tb * 512, (tb + 1) * 512), vw(ps4[tb]), AF.Sigmoid,
                           bias=V(gbias, 0, 128, 8 + rt, 9 + rt))
            else:
                for tb in range(4):
                    pr.act(vw(tmpf), vw(ps4[tb]), AF.Identity,
                           bias=V(gbias, 0, 128, rt, rt + 1))
                    sv = vw(stage)
                    pr.vtt(sv, vw(tmpf), sgT.v(rt, tb * 512, (tb + 1) * 512), ALU.mult)
                    pr.vtt(sv, sv, xn1T.v(rt, tb * 512, (tb + 1) * 512), ALU.add)
                    pr.dma("scalar",
                           dview(s1T_o, s1T_o.h[rt * 128:(rt + 1) * 128,
                                                tb * 512:(tb + 1) * 512]), sv)
    return pr


def build_launch_G():
    pr = Prog()
    C = ph_consts(pr, bf=True)
    xn1T_d = pr.dram_in("xn1T", [E, TOWN], BF16)
    x1T_d = pr.dram_in("x1T", [E, TOWN])
    s1T_d = pr.dram_in("s1T", [E, TOWN])
    rw_d = pr.dram_in("rw", [E, NEXP])
    rb_d = pr.dram_in("rb", [NEXP, 1])
    w1_d = pr.dram_in("w1", [NEXP * E, HIDM])
    b1r_d = pr.dram_in("b1r", [NEXP * 128, 32])
    w2_d = pr.dram_in("w2", [NEXP * HIDM, E])
    b2_d = pr.dram_in("b2", [NEXP, E])
    outT_o = pr.dram_out("outT_o", [E, TOWN])

    xn1T = FM(pr, "xn1Ts", 8, TOWN, BF16)
    for i in range(8):
        pr.dma("sync", xn1T.v(i), dview(xn1T_d, xn1T_d.h[i * 128:(i + 1) * 128, :]))

    rw32 = pr.sbuf("rw32", 128, 64)
    rwbf = pr.sbuf("rwbf", 128, 64, BF16)
    pr.dma("sync", vw(rw32), dview(rw_d, rw_d.h[:, :].rearrange("(a p) c -> p a c", p=128)))
    pr.copy(vw(rwbf), vw(rw32))
    rbsb = pr.sbuf("rbsb", 8, 1)
    pr.dma("sync", vw(rbsb), vw(rb_d))
    lgT = pr.sbuf("lgT", 8, TOWN)
    ps0 = pr.pb[0]
    for tb in range(4):
        p8 = V(ps0, 0, 8, 0, 512)
        for ec in range(8):
            pr.mm(p8, V(rwbf, 0, 128, ec * 8, (ec + 1) * 8),
                  xn1T.v(ec, tb * 512, (tb + 1) * 512), start=(ec == 0), stop=(ec == 7))
        pr.act(V(lgT, 0, 8, tb * 512, (tb + 1) * 512), p8, AF.Identity, bias=vw(rbsb))
    gbuf = pr.sbuf("gbuf", 128, 16 * 8)
    gTb = pr.sbuf("gTb", 8, TOWN, BF16)
    lt = pr.sbuf("lt", 128, 8)
    m1 = pr.sbuf("gm", 128, 4)
    e1 = pr.sbuf("ge", 128, 8)
    ps1 = pr.pb[1]
    ps2 = pr.pb[2]
    for tt in range(16):
        pt = V(ps1, 0, 128, 0, 8)
        pr.transp(pt, V(lgT, 0, 8, tt * 128, (tt + 1) * 128), C["identN"](8))
        pr.copy(vw(lt), pt)
        pr.reduce(V(m1, 0, 128, 0, 1), vw(lt), ALU.max)
        pr.vts(vw(e1), vw(lt), V(m1, 0, 128, 0, 1), None, ALU.is_equal)
        pr.stt(vw(e1), vw(e1), -1e30, vw(lt), ALU.mult, ALU.add)
        pr.reduce(V(m1, 0, 128, 1, 2), vw(e1), ALU.max)
        pr.vts(V(m1, 0, 128, 2, 3), V(m1, 0, 128, 0, 1), -1.0, None, ALU.mult)
        pr.act(vw(e1), vw(lt), AF.Exp, bias=V(m1, 0, 128, 2, 3))
        gv = V(gbuf, 0, 128, tt * 8, (tt + 1) * 8)
        pr.vts(gv, vw(lt), V(m1, 0, 128, 1, 2), None, ALU.is_ge)
        pr.vtt(gv, gv, vw(e1), ALU.mult)
        pr.reduce(V(m1, 0, 128, 3, 4), gv, ALU.add)
        pr.recip(V(m1, 0, 128, 3, 4), V(m1, 0, 128, 3, 4))
        pr.vts(gv, gv, V(m1, 0, 128, 3, 4), None, ALU.mult)
        pg = V(ps2, 0, 8, 0, 128)
        pr.transp(pg, gv, C["ident"])
        pr.copy(V(gTb, 0, 8, tt * 128, (tt + 1) * 128), pg)
    macc = pr.sbuf("macc", 128, 16 * 1024)
    b2bf = pr.sbuf("b2bf", 8, E, BF16)
    b232 = pr.sbuf("b232", 8, E)
    pr.dma("sync", vw(b232), vw(b2_d))
    pr.copy(vw(b2bf), vw(b232))
    ps3 = pr.pb[3]
    for tt in range(16):
        for cb in range(2):
            pb_ = vw(ps3)
            pr.mm(pb_, V(gTb, 0, 8, tt * 128, (tt + 1) * 128),
                  V(b2bf, 0, 8, cb * 512, (cb + 1) * 512))
            pr.copy(V(macc, 0, 128, tt * 1024 + cb * 512, tt * 1024 + (cb + 1) * 512), pb_)
    hbuf = FM(pr, "hq", 8, TOWN, BF16)
    b1sb = pr.sbuf("b1sb", 128, 32)
    wl = WLoader(pr, "w1", 1024, 2)
    w2r = pr.sbuf("w2r", 128, 1024)
    w2b = pr.sbuf("w2b", 128, 8 * 1024, BF16)
    psH = [pr.pb[4], pr.pb[5], pr.pb[6], pr.pb[7]]
    psY = [pr.pb[1], pr.pb[2]]
    for e in range(NEXP):
        pr.dma("scalar", vw(b1sb), dview(b1r_d, b1r_d.h[e * 128:(e + 1) * 128, :]))
        for hc in range(4):
            for htl in range(8):
                ht = hc * 8 + htl
                src = w1_d.h[e * E:(e + 1) * E, ht * 128:(ht + 1) * 128].rearrange("(a p) c -> p a c", p=128)
                wb = wl.fetch(w1_d, src, 1024, q=("sync", "gpsimd")[htl % 2])
                for ec in range(8):
                    lw = V(wb.buf, 0, 128, wb.rect[2] + ec * 128, wb.rect[2] + (ec + 1) * 128)
                    for tb in range(4):
                        pr.mm(vw(psH[tb]), lw, xn1T.v(ec, tb * 512, (tb + 1) * 512),
                              start=(ec == 0), stop=(ec == 7))
                for tb in range(4):
                    pr.act(hbuf.v(htl, tb * 512, (tb + 1) * 512), vw(psH[tb]), AF.Relu,
                           bias=V(b1sb, 0, 128, ht, ht + 1))
                pr.dma("scalar", vw(w2r),
                       dview(w2_d, w2_d.h[e * HIDM + ht * 128:e * HIDM + (ht + 1) * 128, :]))
                pr.copy(V(w2b, 0, 128, htl * 1024, (htl + 1) * 1024), vw(w2r))
            for tt in range(16):
                for cb in range(2):
                    pY = vw(psY[cb])
                    for htl in range(8):
                        pr.mm(pY, hbuf.v(htl, tt * 128, (tt + 1) * 128),
                              V(w2b, 0, 128, htl * 1024 + cb * 512, htl * 1024 + (cb + 1) * 512),
                              start=(htl == 0), stop=(htl == 7))
                    mv = V(macc, 0, 128, tt * 1024 + cb * 512, tt * 1024 + (cb + 1) * 512)
                    pr.stt(mv, pY, V(gbuf, 0, 128, tt * 8 + e, tt * 8 + e + 1), mv,
                           ALU.mult, ALU.add)
    g1t = pr.sbuf("g1t", 128, 1024)
    xl = pr.sbuf("xl", 128, 128)
    sl = pr.sbuf("sl", 128, 128)
    ost = pr.sbuf("ost", 128, 128)
    psG = pr.pb[0]
    for tt in range(16):
        pr.act(vw(g1t), V(macc, 0, 128, tt * 1024, (tt + 1) * 1024), AF.Silu)
        for ec in range(8):
            pg = V(psG, 0, 128, 0, 128)
            pr.transp(pg, V(g1t, 0, 128, ec * 128, (ec + 1) * 128), C["ident"])
            pr.dma("sync", vw(xl),
                   dview(x1T_d, x1T_d.h[ec * 128:(ec + 1) * 128, tt * 128:(tt + 1) * 128]))
            pr.dma("gpsimd", vw(sl),
                   dview(s1T_d, s1T_d.h[ec * 128:(ec + 1) * 128, tt * 128:(tt + 1) * 128]))
            pr.vtt(vw(ost), vw(sl), pg, ALU.mult)
            pr.vtt(vw(ost), vw(ost), vw(xl), ALU.add)
            pr.dma("scalar",
                   dview(outT_o, outT_o.h[ec * 128:(ec + 1) * 128, tt * 128:(tt + 1) * 128]),
                   vw(ost))
    return pr



def _prep_all(I):
    f32 = np.float32
    P = _prep_common(I)
    tab = np.asarray(I["L0_attn_bias_table"], f32)
    P["cvec"] = np.ascontiguousarray(tab[2046:2047, :])
    j = np.arange(2432)[None, :]
    p = np.arange(128)[:, None]
    P["W2sh"] = []
    for c in range(8):
        th = c % 2
        own0 = th * TOWN
        other0 = (1 - th) * TOWN
        idx = np.clip(own0 - other0 + 2175 + j - p, 0, 2046)
        P["W2sh"].append(np.ascontiguousarray(
            tab[idx, :].transpose(2, 0, 1).reshape(H * 128, 2432)))
    P["wqT"] = np.ascontiguousarray(np.asarray(I["L0_attn_Wq"], f32).T)
    P["wkT"] = np.ascontiguousarray(np.asarray(I["L0_attn_Wk"], f32).T)
    P["wvT"] = np.ascontiguousarray(np.asarray(I["L0_attn_Wv"], f32).T)
    gb = np.zeros((128, 16), f32)
    gb[:, 0:8] = np.asarray(I["L0_attn_ln_gamma"], f32).reshape(8, 128).T
    gb[:, 8:16] = np.asarray(I["L0_attn_ln_beta"], f32).reshape(8, 128).T
    P["gb"] = gb
    lgb = np.zeros((128, 16), f32)
    lgb[:, 0:8] = np.asarray(I["L1_attn_ln_g"], f32).reshape(8, 128).T
    lgb[:, 8:16] = np.asarray(I["L1_attn_ln_b"], f32).reshape(8, 128).T
    P["lgb"] = lgb
    P["ipT"] = np.ascontiguousarray(np.asarray(I["L1_attn_in_proj_w"], f32).T)
    P["convw"] = np.ascontiguousarray(np.asarray(I["L1_attn_conv_w"], f32)[:, 0, :])
    cols = np.zeros((128, 32), f32)
    cols[:, 0:16] = np.asarray(I["L1_attn_conv_b"], f32).reshape(16, 128).T
    cols[:, 16:32] = np.asarray(I["L1_attn_dt_proj_b"], f32).reshape(16, 128).T
    P["cols"] = cols
    P["xprojT"] = np.ascontiguousarray(np.asarray(I["L1_attn_x_proj_w"], f32).T)
    P["dtprojT"] = np.ascontiguousarray(np.asarray(I["L1_attn_dt_proj_w"], f32).T)
    alog = np.asarray(I["L1_attn_A_log"], f32)
    P["alog"] = [np.ascontiguousarray(
        alog[(c % 2) * 1024:(c % 2) * 1024 + 1024, :].reshape(8, 128, 16)
        .transpose(1, 0, 2).reshape(128, 8 * 16)) for c in range(8)]
    dd = np.asarray(I["L1_attn_D"], f32)
    P["dD"] = [np.ascontiguousarray(
        dd[(c % 2) * 1024:(c % 2) * 1024 + 1024].reshape(8, 128).T) for c in range(8)]
    P["opT"] = np.ascontiguousarray(np.asarray(I["L1_attn_out_proj_w"], f32).T)
    P["gwT"] = np.ascontiguousarray(np.asarray(I["L1_attn_glu_w"], f32).T)
    glb = np.asarray(I["L1_attn_glu_b"], f32)
    gbias = np.zeros((128, 16), f32)
    gbias[:, 0:8] = glb[0:1024].reshape(8, 128).T
    gbias[:, 8:16] = glb[1024:2048].reshape(8, 128).T
    P["gbias"] = gbias
    P["rw"] = np.ascontiguousarray(np.asarray(I["L1_ffn_router_w"], f32))
    P["rb"] = np.ascontiguousarray(np.asarray(I["L1_ffn_router_b"], f32).reshape(8, 1))
    P["w1"] = np.ascontiguousarray(np.asarray(I["L1_ffn_w1"], f32).reshape(NEXP * E, HIDM))
    P["b1r"] = np.ascontiguousarray(
        np.asarray(I["L1_ffn_b1"], f32).reshape(NEXP, 32, 128).transpose(0, 2, 1)
        .reshape(NEXP * 128, 32))
    P["w2"] = np.ascontiguousarray(np.asarray(I["L1_ffn_w2"], f32).reshape(NEXP * HIDM, E))
    P["b2"] = np.ascontiguousarray(np.asarray(I["L1_ffn_b2"], f32))
    return P


def kernel(**inputs):
    return kernel_fused(inputs)


def kernel_unfused(**inputs):
    P = _prep_all(inputs)
    cores = list(range(8))
    bf = lambda a: np.ascontiguousarray(a)

    prA = build_launch_A()
    resA = run_bass_kernel_spmd(prA.emit(), [dict(
        consts=P["consts"], x0T=P["x0T_rot"][c], wgT=P["wgT"], wuT=P["wuT"],
        wdT=P["wdT"]) for c in cores], core_ids=cores)
    xn0T = [resA.results[c]["xn0T_o"] for c in cores]
    g0T = [resA.results[c]["g0T_o"] for c in cores]

    prB = build_launch_B()
    resB = run_bass_kernel_spmd(prB.emit(), [dict(
        consts=P["consts"], xn0T=xn0T[c], wqT=P["wqT"], wkT=P["wkT"], wvT=P["wvT"],
        W2sh=P["W2sh"][c], cvec=P["cvec"], gb=P["gb"]) for c in cores], core_ids=cores)
    s0T = [resB.results[c]["s0T_o"] for c in cores]

    prC = build_launch_C()
    resC = run_bass_kernel_spmd(prC.emit(), [dict(
        consts=P["consts"], x0T=P["x0T_rot"][c], s0T=s0T[c], g0T=g0T[c],
        lgb=P["lgb"], ipT=P["ipT"]) for c in cores], core_ids=cores)
    x1T = [resC.results[c]["x1T_o"] for c in cores]
    xn1T = [resC.results[c]["xn1T_o"] for c in cores]
    xpT = [resC.results[c]["xpT_o"] for c in cores]
    zsT = [resC.results[c]["zsT_o"] for c in cores]

    zeros_halo = np.zeros((DIN, 3), BF16NP)
    prD = build_launch_D()
    resD = run_bass_kernel_spmd(prD.emit(), [dict(
        consts=P["consts"], xpT=xpT[c],
        halo=(zeros_halo if c % 2 == 0 else np.ascontiguousarray(xpT[c - 1][:, TOWN - 3:])),
        convw=P["convw"], cols=P["cols"], xprojT=P["xprojT"], dtprojT=P["dtprojT"])
        for c in cores], core_ids=cores)
    uT = [resD.results[c]["uT_o"] for c in cores]
    dtT = [resD.results[c]["dtT_o"] for c in cores]
    bcT = [resD.results[c]["bcT_o"] for c in cores]

    prE = build_launch_E()
    in_E = []
    for c in cores:
        b, dh = c // 2, c % 2
        r = slice(dh * 1024, dh * 1024 + 1024)
        in_E.append(dict(
            consts=P["consts"],
            dtS=np.ascontiguousarray(np.concatenate([dtT[2 * b][r], dtT[2 * b + 1][r]], 1)),
            uS=np.ascontiguousarray(np.concatenate([uT[2 * b][r], uT[2 * b + 1][r]], 1)),
            bS=np.ascontiguousarray(np.concatenate([bcT[2 * b][0:16], bcT[2 * b + 1][0:16]], 1)),
            cS=np.ascontiguousarray(np.concatenate([bcT[2 * b][16:32], bcT[2 * b + 1][16:32]], 1)),
            alog=P["alog"][c], dD=P["dD"][c]))
    resE = run_bass_kernel_spmd(prE.emit(), in_E, core_ids=cores)
    yS = [resE.results[c]["yS_o"] for c in cores]

    prF = build_launch_F()
    in_F = []
    for c in cores:
        b, th = c // 2, c % 2
        cs = slice(th * TOWN, th * TOWN + TOWN)
        y2 = np.ascontiguousarray(np.concatenate([yS[2 * b][:, cs], yS[2 * b + 1][:, cs]], 0))
        in_F.append(dict(consts=P["consts"], y2=y2, zsT=zsT[c], opT=P["opT"],
                         gwT=P["gwT"], gbias=P["gbias"], xn1T=xn1T[c]))
    resF = run_bass_kernel_spmd(prF.emit(), in_F, core_ids=cores)
    s1T = [resF.results[c]["s1T_o"] for c in cores]

    prG = build_launch_G()
    resG = run_bass_kernel_spmd(prG.emit(), [dict(
        consts=P["consts"], xn1T=xn1T[c], x1T=x1T[c], s1T=s1T[c], rw=P["rw"],
        rb=P["rb"], w1=P["w1"], b1r=P["b1r"], w2=P["w2"], b2=P["b2"])
        for c in cores], core_ids=cores)

    out = np.empty((B, S, E), np.float32)
    for c in cores:
        b, th = c // 2, c % 2
        out[b, th * TOWN:(th + 1) * TOWN, :] = resG.results[c]["outT_o"].T
    return out



PAIRS = [[0, 1], [2, 3], [4, 5], [6, 7]]


def build_fused(dbg=False):
    pr = Prog()
    C = ph_consts(pr, bf=True)
    pr.ln_scratch = (pr.sbuf("lnsq", 128, 512), pr.sbuf("lnst", 128, 2048),
                     pr.sbuf("lntm", 128, 512))

    SBF = pr.sbuf("SBF", 128, 51200, BF16)
    SF = pr.sbuf("SF", 128, 22528, F32)

    x0T_d = pr.dram_in("x0T", [E, S])
    wgT_d = pr.dram_in("wgT", [E, HID0])
    wuT_d = pr.dram_in("wuT", [E, HID0])
    wdT_d = pr.dram_in("wdT", [HID0, E])
    wqT_d = pr.dram_in("wqT", [E, E])
    wkT_d = pr.dram_in("wkT", [E, E])
    wvT_d = pr.dram_in("wvT", [E, E])
    W2sh_d = pr.dram_in("W2sh", [H * 128, 2432])
    cvec_d = pr.dram_in("cvec", [1, 16])
    gb_d = pr.dram_in("gb", [128, 16])
    lgb_d = pr.dram_in("lgb", [128, 16])
    ipT_d = pr.dram_in("ipT", [E, 2 * DIN])
    convw_d = pr.dram_in("convw", [DIN, 4])
    cols_d = pr.dram_in("cols", [128, 32])
    xprojT_d = pr.dram_in("xprojT", [DIN, 96])
    dtprojT_d = pr.dram_in("dtprojT", [DTR, DIN])
    alog_d = pr.dram_in("alog", [128, 8 * NST])
    dD_d = pr.dram_in("dD", [128, 8])
    maskbf_d = pr.dram_in("maskbf", [128, 1])
    mask0_d = pr.dram_in("mask0", [128, 1])
    opT_d = pr.dram_in("opT", [DIN, E])
    gwT_d = pr.dram_in("gwT", [E, DIN])
    gbias_d = pr.dram_in("gbias", [128, 16])
    rw_d = pr.dram_in("rw", [E, NEXP])
    rb_d = pr.dram_in("rb", [NEXP, 1])
    w1_d = pr.dram_in("w1", [NEXP * E, HIDM])
    b1r_d = pr.dram_in("b1r", [NEXP * 128, 32])
    w2_d = pr.dram_in("w2", [NEXP * HIDM, E])
    b2_d = pr.dram_in("b2", [NEXP, E])
    outT_o = pr.dram_out("outT_o", [E, TOWN])

    mk = pr.dram_out if dbg else pr.dram_tmp
    g0_d = mk("g0_d", [E, TOWN], BF16)
    s0_d = mk("s0_d", [E, TOWN])
    x1_d = mk("x1_d", [E, TOWN])
    xn1_d = mk("xn1_d", [E, TOWN], BF16)
    zs_d = mk("zs_d", [DIN, TOWN], BF16)
    xp_d = mk("xp_d", [DIN, TOWN], BF16)
    s1_d = mk("s1_d", [E, TOWN])
    cc1i = pr.dram_tmp("cc1i", [DIN, 3], BF16)
    cc1o = pr.dram_tmp("cc1o", [2 * DIN, 3], BF16)
    cc2i = pr.dram_tmp("cc2i", [DIN, TOWN])
    cc2o = pr.dram_tmp("cc2o", [2 * DIN, TOWN])
    cc3i = pr.dram_tmp("cc3i", [DIN, TOWN], BF16)
    cc3o = pr.dram_tmp("cc3o", [2 * DIN, TOWN], BF16)
    cc4i = pr.dram_tmp("cc4i", [32, TOWN])
    cc4o = pr.dram_tmp("cc4o", [64, TOWN])
    cc5i = pr.dram_tmp("cc5i", [DIN, TOWN], BF16)
    cc5o = pr.dram_tmp("cc5o", [2 * DIN, TOWN], BF16)

    gbsb = pr.sbuf("gbsb", 128, 16)
    pr.dma("scalar", vw(gbsb), vw(gb_d))
    lgbsb = pr.sbuf("lgbsb", 128, 16)
    pr.dma("scalar", vw(lgbsb), vw(lgb_d))
    gbias = pr.sbuf("gbiasb", 128, 16)
    pr.dma("scalar", vw(gbias), vw(gbias_d))
    colsb = pr.sbuf("colsb", 128, 32)
    pr.dma("scalar", vw(colsb), vw(cols_d))
    maskb = pr.sbuf("maskb", 128, 1)
    pr.dma("scalar", vw(maskb), vw(maskbf_d))
    mask0 = pr.sbuf("mask0b", 128, 1)
    pr.dma("scalar", vw(mask0), vw(mask0_d))

    xn0T = FMv(SBF, 0, 8, S)
    hbuf = FMv(SBF, 32768, 24, 512)
    wlb = Sub(SBF, 45056, 128, 2048)
    wdlb = Sub(SBF, 47104, 128, 512)
    g0st = Sub(SBF, 47616, 128, 512)
    xls = [Sub(SF, k * 512, 128, 512) for k in range(4)]
    wlf = Sub(SF, 2048, 128, 2048)
    wdlf = Sub(SF, 4096, 128, 512)
    sgq = Sub(SF, 4608, 128, 512)

    cache = {}

    def getx0(i, f0, f1, _c=cache):
        key = (i, f0)
        if key in _c:
            return _c[key]
        dst = xls[(i + (f0 // 512) * 3) % 4].v()
        pr.dma("sync", dst, dview(x0T_d, x0T_d.h[i * 128:(i + 1) * 128, f0:f1]))
        _c.clear()
        _c[key] = dst
        return dst

    ph_ln_stream(pr, "ln0", getx0, 8, S, lambda i, f0, f1: xn0T.v(i, f0, f1), C)

    kslot = [0]

    def fetch(dbuf, src_ap, fsub, bsub, width, nslots, q, conv="vector"):
        s = kslot[0] % nslots
        kslot[0] += 1
        d32 = fsub.v(0, 128, s * width, (s + 1) * width)
        dbf = bsub.v(0, 128, s * width, (s + 1) * width)
        pr.dma(q, d32, dview(dbuf, src_ap))
        pr.copy(dbf, d32, engine=conv)
        return dbf

    psg, psu, psd = pr.pb[4], pr.pb[5], pr.pb[6]
    for tq in range(4):
        t0 = tq * 512
        for ht in range(24):
            wg = fetch(wgT_d, rearr_block(wgT_d.h, ht * 128, (ht + 1) * 128), wlf, wlb,
                       1024, 2, ("sync", "gpsimd")[ht % 2])
            wu = fetch(wuT_d, rearr_block(wuT_d.h, ht * 128, (ht + 1) * 128), wlf, wlb,
                       1024, 2, ("gpsimd", "sync")[ht % 2])
            for ec in range(8):
                lg = V(wg.buf, 0, 128, wg.rect[2] + ec * 128, wg.rect[2] + (ec + 1) * 128)
                lu = V(wu.buf, 0, 128, wu.rect[2] + ec * 128, wu.rect[2] + (ec + 1) * 128)
                rhs = xn0T.v(ec, t0, t0 + 512)
                pr.mm(vw(psg), lg, rhs, start=(ec == 0), stop=(ec == 7))
                pr.mm(vw(psu), lu, rhs, start=(ec == 0), stop=(ec == 7))
            pr.act(sgq.v(), vw(psg), AF.Silu, scale=1.0 / 127.0)
            pr.vtt(hbuf.v(ht), sgq.v(), vw(psu), ALU.mult)
        for et in range(8):
            for ht in range(24):
                wd = fetch(wdT_d, wdT_d.h[ht * 128:(ht + 1) * 128, et * 128:(et + 1) * 128],
                           wdlf, wdlb, 128, 4, "scalar")
                pr.mm(vw(psd), wd, hbuf.v(ht), start=(ht == 0), stop=(ht == 23))
            pr.act(g0st.v(), vw(psd), AF.Silu, scale=1.0 / (127.0 * 127.0))
            pr.dma("scalar", dview(g0_d, g0_d.h[et * 128:(et + 1) * 128, t0:t0 + 512]),
                   g0st.v())

    s0preT = FMv(SF, 0, 8, TOWN)
    qT = Sub(SBF, 32768, 64, 2048)
    kT = Sub(SBF, 34816, 64, 4096)
    vhat = Sub(SBF, 38912, 128, 2048)
    band = Sub(SBF, 40960, 128, 8192)
    awlb = Sub(SBF, 50176, 128, 1024)
    expS = Sub(SBF, 49152, 128, 1024)
    bb32 = Sub(SF, 16384, 128, 1024)
    vtmp = Sub(SF, 17408, 64, 512)
    rzrep = Sub(SF, 17920, 128, 512)
    atmp = Sub(SF, 18432, 128, 512)
    awlf = Sub(SF, 18944, 128, 1536)
    ccol = Sub(SF, 20480, 128, 1)
    vsum = Sub(SF, 20482, 1, 64)

    psP, psT_, psR, psZb, psA, psB = pr.pb[0], pr.pb[1], pr.pb[2], pr.pb[3], pr.pb[4], pr.pb[5]
    psS = [pr.pb[6], pr.pb[7]]
    aslot = [0]

    def afetch(dbuf, src_ap, q):
        s = aslot[0] % 2
        aslot[0] += 1
        d32 = awlf.v(0, 128, s * 512, (s + 1) * 512)
        dbf = awlb.v(0, 128, s * 512, (s + 1) * 512)
        pr.dma(q, d32, dview(dbuf, src_ap))
        pr.copy(dbf, d32)
        return dbf

    for h in range(H):
        p0 = (h % 2) * 64
        et = h // 2
        pr.dma("gpsimd", ccol.v(), dview(cvec_d, bass.AP(cvec_d.h, h, [[0, 128], [1, 1]])))
        for d in range(16):
            dst = bb32.v(0, 128, (d % 2) * 512, (d % 2) * 512 + 512)
            pr.dma("sync", dst, dview(W2sh_d, W2sh_d.h[h * 128:(h + 1) * 128,
                                                       1920 - 128 * d:2432 - 128 * d]))
            pr.vts(band.v(0, 128, d * 512, (d + 1) * 512), dst, ccol.v(), None, ALU.subtract)
        wq = afetch(wqT_d, rearr_block(wqT_d.h, h * 64, (h + 1) * 64), "sync")
        wk = afetch(wkT_d, rearr_block(wkT_d.h, h * 64, (h + 1) * 64), "gpsimd")
        wv = afetch(wvT_d, rearr_block(wvT_d.h, h * 64, (h + 1) * 64), "scalar")

        def proj(wbf, nblk, dst):
            for qb in range(nblk):
                pp = V(psP, 0, 64, 0, 512)
                for ec in range(8):
                    lw = V(wbf.buf, 0, 128, wbf.rect[2] + ec * 64, wbf.rect[2] + (ec + 1) * 64)
                    pr.mm(pp, lw, xn0T.v(ec, qb * 512, (qb + 1) * 512),
                          start=(ec == 0), stop=(ec == 7))
                pr.act(dst.v(0, 64, qb * 512, (qb + 1) * 512), pp, AF.Copy)

        proj(wq, 4, qT)
        proj(wk, 8, kT)
        for qb in range(8):
            pp = V(psP, 0, 64, 0, 512)
            for ec in range(8):
                lw = V(wv.buf, 0, 128, wv.rect[2] + ec * 64, wv.rect[2] + (ec + 1) * 64)
                pr.mm(pp, lw, xn0T.v(ec, qb * 512, (qb + 1) * 512),
                      start=(ec == 0), stop=(ec == 7))
            pr.copy(vtmp.v(), pp)
            for j in range(4):
                tc = qb * 4 + j
                pt = V(psT_, 0, 128, 0, 64)
                pr.transp(pt, vtmp.v(0, 64, j * 128, (j + 1) * 128), C["identN"](64))
                pr.copy(vhat.v(0, 128, tc * 64, (tc + 1) * 64), pt)
        pc = V(psZb, 0, 1, 0, 64)
        for tc in range(32):
            pr.mm(pc, C["ones_col_bf"], vhat.v(0, 128, tc * 64, (tc + 1) * 64),
                  start=(tc == 0), stop=(tc == 31))
        pr.vts(vsum.v(), pc, V(ccol.buf, 0, 1, ccol.off, ccol.off + 1), None, ALU.mult)
        for qb in range(4):
            pA = V(psA, p0, p0 + 64, 0, 512)
            pZ = V(psZb, p0, p0 + 1, 0, 512)
            for kc in range(32):
                pS = vw(psS[kc % 2])
                pr.mm(pS, kT.v(0, 64, kc * 128, (kc + 1) * 128),
                      qT.v(0, 64, qb * 512, (qb + 1) * 512))
                es = expS.v(0, 128, (kc % 2) * 512, (kc % 2) * 512 + 512)
                pr.act(es, pS, AF.Exp, scale=1.0 / 32.0)
                pr.mm(pA, vhat.v(0, 128, kc * 64, (kc + 1) * 64), es,
                      start=(kc == 0), stop=(kc == 31))
                pr.mm(pZ, C["ones_col_bf"], es, start=(kc == 0), stop=(kc == 31))
            pB = V(psB, p0, p0 + 64, 0, 512)
            bl = list(range(16 + 4 * qb, 32))
            for j, kc in enumerate(bl):
                d = kc - 16 - 4 * qb
                pr.mm(pB, vhat.v(0, 128, kc * 64, (kc + 1) * 64),
                      band.v(0, 128, d * 512, (d + 1) * 512), start=(j == 0), stop=False)
            pr.mm(pB, vsum.v(), C["ones512_at"](0), start=False, stop=True)
            rzs = atmp.v(p0, p0 + 1, 0, 512)
            pr.recip(rzs, pZ)
            pr.mm(vw(psR), C["ones_row_at"](p0), rzs)
            pr.copy(rzrep.v(), vw(psR))
            o1 = V(s0preT.buf, p0, p0 + 64, et * TOWN + qb * 512, et * TOWN + (qb + 1) * 512)
            pr.vtt(o1, pA, rzrep.v(p0, p0 + 64, 0, 512), ALU.mult)
            pr.vtt(o1, o1, pB, ALU.add)
    if dbg:
        s0pre_o = pr.dram_out("s0pre_o", [E, TOWN])
        for i in range(8):
            pr.dma("gpsimd", dview(s0pre_o, s0pre_o.h[i * 128:(i + 1) * 128, :]), s0preT.v(i))
        qT_o = pr.dram_out("qT_o", [64, TOWN])
        pr.dma("gpsimd", dview(qT_o, qT_o.h[:, :]), qT.v(0, 64, 0, 2048))
    ph_ln_stream(pr, "s0ln", lambda i, f0, f1: s0preT.v(i, f0, f1), 8, TOWN,
                 lambda i, f0, f1: s0preT.v(i, f0, f1), C,
                 gamma=lambda i: V(gbsb, 0, 128, i, i + 1),
                 beta=lambda i: V(gbsb, 0, 128, 8 + i, 9 + i))
    for i in range(8):
        pr.dma("gpsimd", dview(s0_d, s0_d.h[i * 128:(i + 1) * 128, :]), s0preT.v(i))

    xn1T = FMv(SBF, 0, 8, TOWN)
    ln2T = FMv(SBF, 16384, 8, TOWN)
    g0l = Sub(SBF, 32768, 128, 1024)
    ipb = Sub(SBF, 33792, 128, 2048)
    ipst = Sub(SBF, 35840, 128, 2048)
    x1s = Sub(SF, 0, 128, 1024)
    x0l = Sub(SF, 1024, 128, 1024)
    s0l = Sub(SF, 2048, 128, 1024)
    x1l = [Sub(SF, 3072 + k * 512, 128, 512) for k in range(4)]
    ipf = Sub(SF, 5120, 128, 2048)

    for i in range(8):
        for bk in range(4):
            f0, f1 = bk * 512, (bk + 1) * 512
            par = bk % 2
            xv = x0l.v(0, 128, par * 512, par * 512 + 512)
            sv = s0l.v(0, 128, par * 512, par * 512 + 512)
            gv = g0l.v(0, 128, par * 512, par * 512 + 512)
            pr.dma("sync", xv, dview(x0T_d, x0T_d.h[i * 128:(i + 1) * 128, f0:f1]))
            pr.dma("gpsimd", sv, dview(s0_d, s0_d.h[i * 128:(i + 1) * 128, f0:f1]))
            pr.dma("scalar", gv, dview(g0_d, g0_d.h[i * 128:(i + 1) * 128, f0:f1]))
            ov = x1s.v(0, 128, par * 512, par * 512 + 512)
            pr.vtt(ov, sv, gv, ALU.mult)
            pr.vtt(ov, ov, xv, ALU.add)
            pr.dma("sync", dview(x1_d, x1_d.h[i * 128:(i + 1) * 128, f0:f1]), ov)

    c2 = {}

    def getx1(i, f0, f1, _c=c2):
        key = (i, f0)
        if key in _c:
            return _c[key]
        dst = x1l[(i + (f0 // 512) * 3) % 4].v()
        pr.dma("sync", dst, dview(x1_d, x1_d.h[i * 128:(i + 1) * 128, f0:f1]))
        _c.clear()
        _c[key] = dst
        return dst

    ph_ln_stream(pr, "ln1", getx1, 8, TOWN, lambda i, f0, f1: xn1T.v(i, f0, f1), C)
    for i in range(8):
        pr.dma("gpsimd", dview(xn1_d, xn1_d.h[i * 128:(i + 1) * 128, :]), xn1T.v(i))
    ph_ln_stream(pr, "ln2", lambda i, f0, f1: xn1T.v(i, f0, f1), 8, TOWN,
                 lambda i, f0, f1: ln2T.v(i, f0, f1), C,
                 gamma=lambda i: V(lgbsb, 0, 128, i, i + 1),
                 beta=lambda i: V(lgbsb, 0, 128, 8 + i, 9 + i), in_bf=True)
    ps4 = [pr.pb[4], pr.pb[5], pr.pb[6], pr.pb[7]]
    islot = [0]
    for rt in range(32):
        s = islot[0] % 2
        islot[0] += 1
        d32 = ipf.v(0, 128, s * 1024, (s + 1) * 1024)
        dbf = ipb.v(0, 128, s * 1024, (s + 1) * 1024)
        pr.dma(("sync", "gpsimd")[rt % 2], d32,
               dview(ipT_d, rearr_block(ipT_d.h, rt * 128, (rt + 1) * 128)))
        pr.copy(dbf, d32)
        for ec in range(8):
            lw = V(dbf.buf, 0, 128, dbf.rect[2] + ec * 128, dbf.rect[2] + (ec + 1) * 128)
            for tb in range(4):
                pr.mm(vw(ps4[tb]), lw, ln2T.v(ec, tb * 512, (tb + 1) * 512),
                      start=(ec == 0), stop=(ec == 7))
        for tb in range(4):
            sv = ipst.v(0, 128, tb * 512, (tb + 1) * 512)
            if rt < 16:
                pr.act(sv, vw(ps4[tb]), AF.Copy)
            else:
                pr.act(sv, vw(ps4[tb]), AF.Silu)
        dst = xp_d if rt < 16 else zs_d
        r0 = (rt % 16) * 128
        pr.dma("scalar", dview(dst, dst.h[r0:r0 + 128, :]), ipst.v())
    pr.dma("gpsimd", dview(cc1i, cc1i.h[:, :]), dview(xp_d, xp_d.h[:, TOWN - 3:TOWN]))
    pr.cc_op("AllGather", cc1i, cc1o, PAIRS)

    cw = pr.sbuf("cwf", 128, 64)
    pr.dma("sync", vw(cw), dview(convw_d, convw_d.h[:, :].rearrange("(a p) k -> p a k", p=128)))
    cbuf = Sub(SBF, 32768, 128, 2052)
    ul = Sub(SBF, 34820, 128, 1024)
    dtp = Sub(SBF, 35844, 64, 2048)
    wdt = Sub(SBF, 37892, 64, 2048)
    wx = Sub(SBF, 39940, 128, 1536)
    facc = Sub(SF, 0, 128, 2048)
    xdbl = Sub(SF, 2048, 96, 2048)
    sp1 = Sub(SF, 4096, 128, 512)
    sp2 = Sub(SF, 4608, 128, 512)
    sp3 = Sub(SF, 5120, 128, 512)
    wd32 = Sub(SF, 5632, 64, 2048)
    wx32 = Sub(SF, 7680, 128, 1536)
    dtst = Sub(SF, 9216, 128, 2048)
    ust = Sub(SBF, 41476, 128, 2048)

    for dt_ in range(16):
        pr.dma("sync", cbuf.v(0, 128, 0, 3),
               dview(cc1o, cc1o.h[dt_ * 128:(dt_ + 1) * 128, :]))
        pr.vts(cbuf.v(0, 128, 0, 3), cbuf.v(0, 128, 0, 3), vw(maskb), None, ALU.mult)
        pr.dma("sync", cbuf.v(0, 128, 3, 3 + TOWN),
               dview(xp_d, xp_d.h[dt_ * 128:(dt_ + 1) * 128, :]))

        def win(k):
            return cbuf.v(0, 128, k, k + TOWN)

        pr.vts(facc.v(), win(0), V(cw, 0, 128, dt_ * 4, dt_ * 4 + 1), None, ALU.mult)
        for k in range(1, 4):
            pr.stt(facc.v(), win(k), V(cw, 0, 128, dt_ * 4 + k, dt_ * 4 + k + 1),
                   facc.v(), ALU.mult, ALU.add)
        pr.act(ust.v(), facc.v(), AF.Silu, bias=V(colsb, 0, 128, dt_, dt_ + 1))
        pr.dma("gpsimd", dview(cc3i, cc3i.h[dt_ * 128:(dt_ + 1) * 128, :]), ust.v())
    pr.dma("sync", wx32.v(), dview(xprojT_d, xprojT_d.h[:, :].rearrange("(a p) c -> p a c", p=128)))
    pr.copy(wx.v(0, 128, 0, 1536), wx32.v())
    psX = pr.pb[0]
    for tb in range(4):
        px = V(psX, 0, 96, 0, 512)
        for dc in range(16):
            uv = ul.v(0, 128, (dc % 2) * 512, (dc % 2) * 512 + 512)
            pr.dma("gpsimd", uv, dview(cc3i, cc3i.h[dc * 128:(dc + 1) * 128,
                                                    tb * 512:(tb + 1) * 512]))
            pr.mm(px, wx.v(0, 128, dc * 96, (dc + 1) * 96), uv,
                  start=(dc == 0), stop=(dc == 15))
        pr.copy(xdbl.v(0, 96, tb * 512, (tb + 1) * 512), px)
    pr.dma("gpsimd", dview(cc4i, cc4i.h[:, :]), xdbl.v(64, 96, 0, TOWN))
    pr.copy(dtp.v(0, 64, 0, 2048), xdbl.v(0, 64, 0, TOWN))
    pr.dma("sync", wd32.v(0, 64, 0, 2048), vw(dtprojT_d))
    pr.copy(wdt.v(0, 64, 0, 2048), wd32.v(0, 64, 0, 2048))
    psD = pr.pb[1]
    for dt_ in range(16):
        for tb in range(4):
            pd = vw(psD)
            pr.mm(pd, wdt.v(0, 64, dt_ * 128, (dt_ + 1) * 128),
                  dtp.v(0, 64, tb * 512, (tb + 1) * 512))
            pr.act(sp1.v(), pd, AF.Identity, bias=V(colsb, 0, 128, 16 + dt_, 17 + dt_))
            pr.act(sp2.v(), sp1.v(), AF.Abs)
            pr.act(sp2.v(), sp2.v(), AF.Exp, scale=-1.0)
            pr.act(sp2.v(), sp2.v(), AF.Ln, bias=1.0)
            pr.act(sp3.v(), sp1.v(), AF.Relu)
            pr.vtt(dtst.v(0, 128, tb * 512, (tb + 1) * 512), sp3.v(), sp2.v(), ALU.add)
        pr.dma("scalar", dview(cc2i, cc2i.h[dt_ * 128:(dt_ + 1) * 128, :]), dtst.v())
    pr.cc_op("AllGather", cc2i, cc2o, PAIRS)
    pr.cc_op("AllGather", cc3i, cc3o, PAIRS)
    pr.cc_op("AllGather", cc4i, cc4o, PAIRS)

    rw32 = Sub(SF, 0, 128, 64)
    rwbf = Sub(SBF, 43008, 128, 2048)
    pr.dma("sync", rw32.v(), dview(rw_d, rw_d.h[:, :].rearrange("(a p) c -> p a c", p=128)))
    pr.copy(rwbf.v(0, 128, 0, 64), rw32.v())
    rbsb = Sub(SF, 64, 8, 1)
    pr.dma("sync", rbsb.v(0, 8, 0, 1), vw(rb_d))
    lgT = Sub(SF, 2048, 8, 2048)
    ps0 = pr.pb[0]
    for tb in range(4):
        p8 = V(ps0, 0, 8, 0, 512)
        for ec in range(8):
            pr.mm(p8, rwbf.v(0, 128, ec * 8, (ec + 1) * 8),
                  xn1T.v(ec, tb * 512, (tb + 1) * 512), start=(ec == 0), stop=(ec == 7))
        pr.act(lgT.v(0, 8, tb * 512, (tb + 1) * 512), p8, AF.Identity,
               bias=rbsb.v(0, 8, 0, 1))
    gbuf = pr.sbuf("gbuff", 128, 128)
    gTb = Sub(SBF, 43008, 8, 2048)
    lt = Sub(SF, 128, 128, 8)
    m1 = Sub(SF, 640, 128, 4)
    e1 = Sub(SF, 1024, 128, 8)
    ps1_, ps2_ = pr.pb[1], pr.pb[2]
    for tt in range(16):
        pt = V(ps1_, 0, 128, 0, 8)
        pr.transp(pt, lgT.v(0, 8, tt * 128, (tt + 1) * 128), C["identN"](8))
        pr.copy(lt.v(), pt)
        pr.reduce(m1.v(0, 128, 0, 1), lt.v(), ALU.max)
        pr.vts(e1.v(), lt.v(), m1.v(0, 128, 0, 1), None, ALU.is_equal)
        pr.stt(e1.v(), e1.v(), -1e30, lt.v(), ALU.mult, ALU.add)
        pr.reduce(m1.v(0, 128, 1, 2), e1.v(), ALU.max)
        pr.vts(m1.v(0, 128, 2, 3), m1.v(0, 128, 0, 1), -1.0, None, ALU.mult)
        pr.act(e1.v(), lt.v(), AF.Exp, bias=m1.v(0, 128, 2, 3))
        gv = V(gbuf, 0, 128, tt * 8, (tt + 1) * 8)
        pr.vts(gv, lt.v(), m1.v(0, 128, 1, 2), None, ALU.is_ge)
        pr.vtt(gv, gv, e1.v(), ALU.mult)
        pr.reduce(m1.v(0, 128, 3, 4), gv, ALU.add)
        pr.recip(m1.v(0, 128, 3, 4), m1.v(0, 128, 3, 4))
        pr.vts(gv, gv, m1.v(0, 128, 3, 4), None, ALU.mult)
        pg = V(ps2_, 0, 8, 0, 128)
        pr.transp(pg, gv, C["ident"])
        pr.copy(V(gTb.buf, 0, 8, gTb.off + tt * 128, gTb.off + (tt + 1) * 128), pg)
    macc = Sub(SF, 6144, 128, 16384)
    b232 = Sub(SF, 0, 8, 1024)
    b2bf = Sub(SBF, 46080, 8, 1024)
    pr.dma("sync", b232.v(0, 8, 0, 1024), vw(b2_d))
    pr.copy(b2bf.v(0, 8, 0, 1024), b232.v(0, 8, 0, 1024))
    ps3_ = pr.pb[3]
    for tt in range(16):
        for cb in range(2):
            pb_ = vw(ps3_)
            pr.mm(pb_, V(gTb.buf, 0, 8, gTb.off + tt * 128, gTb.off + (tt + 1) * 128),
                  b2bf.v(0, 8, cb * 512, (cb + 1) * 512))
            pr.copy(macc.v(0, 128, tt * 1024 + cb * 512, tt * 1024 + (cb + 1) * 512), pb_)
    asb = pr.sbuf("asbf", 128, 128)
    pr.dma("sync", vw(asb), vw(alog_d))
    pr.act(vw(asb), vw(asb), AF.Exp)
    pr.vts(vw(asb), vw(asb), -1.0, None, ALU.mult)
    dsb = pr.sbuf("dsbf", 128, 8)
    pr.dma("sync", vw(dsb), vw(dD_d))
    dts = [Sub(SF, 0, 128, 512), Sub(SF, 512, 128, 512)]
    yac = Sub(SF, 1024, 128, 512)
    hbf = Sub(SF, 1536, 128, 512)
    brow = Sub(SF, 2048, 1, 512)
    crow = Sub(SF, 2560, 1, 512)
    da = Sub(SF, 3072, 128, 512)
    db = Sub(SF, 3584, 128, 512)
    carry = Sub(SF, 4096, 128, 16)
    w1f = Sub(SF, 4352, 128, 1024)
    w2rf = Sub(SF, 5376, 128, 512)
    usl = [Sub(SBF, 16384, 128, 512), Sub(SBF, 16896, 128, 512)]
    dub = Sub(SBF, 17408, 128, 512)
    yst = Sub(SBF, 17920, 128, 512)
    hq = FMv(SBF, 18432, 8, 2048)
    w2b = Sub(SBF, 34816, 128, 8192)
    w1b = Sub(SBF, 45056, 128, 1024)
    b1sb = pr.sbuf("b1sbf", 128, 32)
    psBr, psCr = pr.pb[0], pr.pb[1]
    psY = [pr.pb[2], pr.pb[3]]
    psH = [pr.pb[4], pr.pb[5], pr.pb[6], pr.pb[7]]

    def scan_unit(dt_, tc):
        half = tc // 4
        cl0 = (tc % 4) * 512
        sl_ = tc % 2
        r0 = half * 2048 + dt_ * 128
        ca, cb = da.v(), db.v()
        pr.dma("sync", ca, dview(cc2o, cc2o.h[r0:r0 + 128, cl0:cl0 + 512]))
        pr.dma("sync", cb, dview(cc2o, cc2o.h[r0 + 1024:r0 + 1152, cl0:cl0 + 512]))
        pr.act(hbf.v(), ca, AF.Identity, scale=V(mask0, 0, 128, 0, 1))
        pr.stt(dts[sl_].v(), cb, V(maskb, 0, 128, 0, 1), hbf.v(), ALU.mult, ALU.add)
        ua, ub = usl[1 - sl_].v(), yst.v()
        pr.dma("gpsimd", ua, dview(cc3o, cc3o.h[r0:r0 + 128, cl0:cl0 + 512]))
        pr.dma("gpsimd", ub, dview(cc3o, cc3o.h[r0 + 1024:r0 + 1152, cl0:cl0 + 512]))
        pr.act(hbf.v(), ua, AF.Identity, scale=V(mask0, 0, 128, 0, 1))
        pr.stt(usl[sl_].v(), ub, V(maskb, 0, 128, 0, 1), hbf.v(), ALU.mult, ALU.add)
        pr.vtt(dub.v(), dts[sl_].v(), usl[sl_].v(), ALU.mult)
        for n in range(NST):
            pr.dma("sync", brow.v(0, 1, 0, 512),
                   dview(cc4o, cc4o.h[half * 32 + n:half * 32 + n + 1, cl0:cl0 + 512]))
            pr.dma("gpsimd", crow.v(0, 1, 0, 512),
                   dview(cc4o, cc4o.h[half * 32 + 16 + n:half * 32 + 17 + n, cl0:cl0 + 512]))
            acol = V(asb, 0, 128, dt_ * NST + n, dt_ * NST + n + 1)
            pB = vw(psBr)
            pC = vw(psCr)
            pr.mm(pB, C["ones_row"], brow.v(0, 1, 0, 512))
            pr.mm(pC, C["ones_row"], crow.v(0, 1, 0, 512))
            pr.act(da.v(), dts[sl_].v(), AF.Exp, scale=acol)
            pr.vtt(db.v(), dub.v(), pB, ALU.mult)
            ini = 0.0 if tc == 0 else V(carry.buf, 0, 128, carry.off + n, carry.off + n + 1)
            pr.scan_op(hbf.v(), da.v(), db.v(), ini)
            pr.copy(V(carry.buf, 0, 128, carry.off + n, carry.off + n + 1),
                    hbf.v(0, 128, 511, 512), engine="scalar")
            if n == 0:
                pr.vtt(yac.v(), hbf.v(), pC, ALU.mult)
            else:
                pr.vtt(da.v(), hbf.v(), pC, ALU.mult)
                pr.vtt(yac.v(), yac.v(), da.v(), ALU.add, engine="gpsimd")
        dcol = V(dsb, 0, 128, dt_, dt_ + 1)
        pr.stt(yst.v(), usl[sl_].v(), dcol, yac.v(), ALU.mult, ALU.add)
        r2 = dt_ * 128 + (1 - half) * 0 + half * 1024
        pr.dma("scalar", dview(cc5i, cc5i.h[half * 1024 + dt_ * 128:half * 1024 + (dt_ + 1) * 128,
                                            cl0:cl0 + 512]), yst.v())

    def moe_unit(e, hc):
        if hc == 0:
            pr.dma("scalar", vw(b1sb), dview(b1r_d, b1r_d.h[e * 128:(e + 1) * 128, :]))
        for htl in range(8):
            ht = hc * 8 + htl
            src = w1_d.h[e * E:(e + 1) * E, ht * 128:(ht + 1) * 128].rearrange(
                "(a p) c -> p a c", p=128)
            s = htl % 2
            d32 = w1f.v(0, 128, s * 512, s * 512 + 512)
            dbf = w1b.v(0, 128, 0, 1024)
            pr.dma(("sync", "gpsimd")[htl % 2], w1f.v(0, 128, 0, 1024) if False else d32,
                   dview(w1_d, src)) if False else None
            pr.dma(("sync", "gpsimd")[htl % 2], w1f.v(0, 128, 0, 1024), dview(w1_d, src))
            pr.copy(w1b.v(0, 128, 0, 1024), w1f.v(0, 128, 0, 1024))
            for ec in range(8):
                lw = V(w1b.buf, 0, 128, w1b.off + ec * 128, w1b.off + (ec + 1) * 128)
                for tb in range(4):
                    pr.mm(vw(psH[tb]), lw, xn1T.v(ec, tb * 512, (tb + 1) * 512),
                          start=(ec == 0), stop=(ec == 7))
            for tb in range(4):
                pr.act(hq.v(htl, tb * 512, (tb + 1) * 512), vw(psH[tb]), AF.Relu,
                       bias=V(b1sb, 0, 128, ht, ht + 1))
            for wh in range(2):
                pr.dma("scalar", w2rf.v(),
                       dview(w2_d, w2_d.h[e * HIDM + ht * 128:e * HIDM + (ht + 1) * 128,
                                          wh * 512:(wh + 1) * 512]))
                pr.copy(w2b.v(0, 128, htl * 1024 + wh * 512, htl * 1024 + (wh + 1) * 512),
                        w2rf.v())
        for tt in range(16):
            for cb in range(2):
                pY = vw(psY[cb])
                for htl in range(8):
                    pr.mm(pY, hq.v(htl, tt * 128, (tt + 1) * 128),
                          w2b.v(0, 128, htl * 1024 + cb * 512, htl * 1024 + (cb + 1) * 512),
                          start=(htl == 0), stop=(htl == 7))
                mv = macc.v(0, 128, tt * 1024 + cb * 512, tt * 1024 + (cb + 1) * 512)
                pr.stt(mv, pY, V(gbuf, 0, 128, tt * 8 + e, tt * 8 + e + 1), mv,
                       ALU.mult, ALU.add)

    su = [(d_, t_) for d_ in range(8) for t_ in range(8)]
    mu = [(e_, h_) for e_ in range(NEXP) for h_ in range(4)]
    si = 0
    for k, (e_, h_) in enumerate(mu):
        for _ in range(2):
            if si < len(su):
                scan_unit(*su[si])
                si += 1
        moe_unit(e_, h_)
    while si < len(su):
        scan_unit(*su[si])
        si += 1
    pr.cc_op("AllGather", cc5i, cc5o, PAIRS)
    if dbg:
        ydbg = pr.dram_out("ydbg", [2 * DIN, TOWN], BF16)
        pr.dma("gpsimd", dview(ydbg, ydbg.h[:, :]), dview(cc5o, cc5o.h[:, :]))

    ygT = FMv(SBF, 0, 16, TOWN)
    mgT = FMv(SBF, 32768, 8, TOWN)
    sgT = FMv(SBF, 0, 8, TOWN)
    fwb = Sub(SBF, 49152, 128, 2048)
    fwf = Sub(SF, 0, 128, 2048)
    yl = Sub(SBF, 16384 + 0, 128, 0, ) if False else None
    y0l = Sub(SBF, 49152, 128, 0) if False else None
    ylr = Sub(SBF, 47104, 128, 1024)
    zlr = Sub(SBF, 48128, 128, 1024)
    ftmp = Sub(SF, 2048, 128, 512)
    fstage = Sub(SF, 2560, 128, 512)
    xnl = Sub(SBF, 16384, 128, 1024)

    ybl = Sub(SF, 4096, 128, 512)
    for dc in range(16):
        yva = ylr.v(0, 128, 0, 512)
        yvb = ylr.v(0, 128, 512, 1024)
        zv = zlr.v(0, 128, (dc % 2) * 512, (dc % 2) * 512 + 512)
        hd = dc // 8
        rr = hd * 2048 + (dc % 8) * 128
        for tb in range(4):
            pr.dma("sync", yva, dview(cc5o, cc5o.h[rr:rr + 128, tb * 512:(tb + 1) * 512]))
            pr.dma("sync", yvb, dview(cc5o, cc5o.h[rr + 1024:rr + 1152, tb * 512:(tb + 1) * 512]))
            pr.act(ybl.v(), yva, AF.Identity, scale=V(mask0, 0, 128, 0, 1))
            pr.stt(yva, yvb, V(maskb, 0, 128, 0, 1), ybl.v(), ALU.mult, ALU.add)
            pr.dma("gpsimd", zv, dview(zs_d, zs_d.h[dc * 128:(dc + 1) * 128,
                                                    tb * 512:(tb + 1) * 512]))
            pr.vtt(ygT.v(dc, tb * 512, (tb + 1) * 512), yva, zv, ALU.mult)
    fslot = [0]

    def ffetch(dbuf, src_ap, q):
        s = fslot[0] % 2
        fslot[0] += 1
        d32 = fwf.v(0, 128, s * 1024, (s + 1) * 1024)
        dbf = fwb.v(0, 128, s * 1024, (s + 1) * 1024)
        pr.dma(q, d32, dview(dbuf, src_ap))
        pr.copy(dbf, d32)
        return dbf

    ps4b = [pr.pb[4], pr.pb[5], pr.pb[6], pr.pb[7]]
    for et in range(8):
        for hf in range(2):
            wb = ffetch(opT_d, opT_d.h[hf * 1024:(hf + 1) * 1024, et * 128:(et + 1) * 128]
                        .rearrange("(a p) c -> p a c", p=128), ("sync", "gpsimd")[hf])
            for dcl in range(8):
                dc = hf * 8 + dcl
                lw = V(wb.buf, 0, 128, wb.rect[2] + dcl * 128, wb.rect[2] + (dcl + 1) * 128)
                for tb in range(4):
                    pr.mm(vw(ps4b[tb]), lw, ygT.v(dc, tb * 512, (tb + 1) * 512),
                          start=(dc == 0), stop=(dc == 15))
        for tb in range(4):
            pr.act(mgT.v(et, tb * 512, (tb + 1) * 512), vw(ps4b[tb]), AF.Gelu)
    for half in range(2):
        for rt in range(8):
            wb = ffetch(gwT_d, rearr_block(gwT_d.h, (1 - half) * 1024 + rt * 128,
                                           (1 - half) * 1024 + (rt + 1) * 128),
                        ("sync", "gpsimd")[rt % 2])
            for ec in range(8):
                lw = V(wb.buf, 0, 128, wb.rect[2] + ec * 128, wb.rect[2] + (ec + 1) * 128)
                for tb in range(4):
                    pr.mm(vw(ps4b[tb]), lw, mgT.v(ec, tb * 512, (tb + 1) * 512),
                          start=(ec == 0), stop=(ec == 7))
            if half == 0:
                for tb in range(4):
                    pr.act(sgT.v(rt, tb * 512, (tb + 1) * 512), vw(ps4b[tb]), AF.Sigmoid,
                           bias=V(gbias, 0, 128, 8 + rt, 9 + rt))
            else:
                for tb in range(4):
                    pr.act(ftmp.v(), vw(ps4b[tb]), AF.Identity,
                           bias=V(gbias, 0, 128, rt, rt + 1))
                    xv = xnl.v(0, 128, (tb % 2) * 512, (tb % 2) * 512 + 512)
                    pr.dma("sync", xv, dview(xn1_d, xn1_d.h[rt * 128:(rt + 1) * 128,
                                                            tb * 512:(tb + 1) * 512]))
                    sv = fstage.v()
                    pr.vtt(sv, ftmp.v(), sgT.v(rt, tb * 512, (tb + 1) * 512), ALU.mult)
                    pr.vtt(sv, sv, xv, ALU.add)
                    pr.dma("scalar", dview(s1_d, s1_d.h[rt * 128:(rt + 1) * 128,
                                                        tb * 512:(tb + 1) * 512]), sv)

    g1t = Sub(SF, 0, 128, 1024)
    xl_ = Sub(SF, 1024, 128, 128)
    sl_ = Sub(SF, 1152, 128, 128)
    ost = Sub(SF, 1280, 128, 128)
    psG = pr.pb[0]
    for tt in range(16):
        pr.act(g1t.v(), macc.v(0, 128, tt * 1024, (tt + 1) * 1024), AF.Silu)
        for ec in range(8):
            pg = V(psG, 0, 128, 0, 128)
            pr.transp(pg, g1t.v(0, 128, ec * 128, (ec + 1) * 128), C["ident"])
            pr.dma("sync", xl_.v(), dview(x1_d, x1_d.h[ec * 128:(ec + 1) * 128,
                                                       tt * 128:(tt + 1) * 128]))
            pr.dma("gpsimd", sl_.v(), dview(s1_d, s1_d.h[ec * 128:(ec + 1) * 128,
                                                         tt * 128:(tt + 1) * 128]))
            pr.vtt(ost.v(), sl_.v(), pg, ALU.mult)
            pr.vtt(ost.v(), ost.v(), xl_.v(), ALU.add)
            pr.dma("scalar", dview(outT_o, outT_o.h[ec * 128:(ec + 1) * 128,
                                                    tt * 128:(tt + 1) * 128]), ost.v())
    return pr


def kernel_fused(inputs, dbg=False, _res=[]):
    P = _prep_all(inputs)
    cores = list(range(8))
    prF = build_fused(dbg)
    masks = [np.full((128, 1), float(c % 2), np.float32) for c in cores]
    masks0 = [np.full((128, 1), float(1 - c % 2), np.float32) for c in cores]
    res = run_bass_kernel_spmd(prF.emit(), [dict(
        consts=P["consts"], x0T=P["x0T_rot"][c], wgT=P["wgT"], wuT=P["wuT"], wdT=P["wdT"],
        wqT=P["wqT"], wkT=P["wkT"], wvT=P["wvT"], W2sh=P["W2sh"][c], cvec=P["cvec"],
        gb=P["gb"], lgb=P["lgb"], ipT=P["ipT"], convw=P["convw"], cols=P["cols"],
        xprojT=P["xprojT"], dtprojT=P["dtprojT"], alog=P["alog"][c], dD=P["dD"][c],
        maskbf=masks[c], mask0=masks0[c], opT=P["opT"], gwT=P["gwT"], gbias=P["gbias"], rw=P["rw"],
        rb=P["rb"], w1=P["w1"], b1r=P["b1r"], w2=P["w2"], b2=P["b2"])
        for c in cores], core_ids=cores)
    _res.clear()
    _res.append(res)
    out = np.empty((B, S, E), np.float32)
    for c in cores:
        b, th = c // 2, c % 2
        out[b, th * TOWN:(th + 1) * TOWN, :] = res.results[c]["outT_o"].T
    return out


# version 03
# speedup vs baseline: 1.1539x; 1.1539x over previous
import numpy as np
import ml_dtypes
import concourse.bass as bass
import concourse.mybir as mybir
from concourse.bass_utils import run_bass_kernel_spmd
from contextlib import ExitStack

F32 = mybir.dt.float32
BF16 = mybir.dt.bfloat16
AF = mybir.ActivationFunctionType
ALU = mybir.AluOpType
AXX = mybir.AxisListType.X
BF16NP = ml_dtypes.bfloat16

B, S, E = 4, 4096, 1024
TOWN = 2048
H, DH = 16, 64
HID0 = 3072
DIN, NST, DTR = 2048, 16, 64
NEXP, HIDM = 8, 4096

_DMAQ = {"sync": "qsy", "gpsimd": "qgp", "scalar": "qsc"}


class Op:
    __slots__ = ("stream", "method", "kwargs", "deps", "kind", "ccsem", "snapshot")

    def __init__(self, stream, method, kwargs, deps, kind, ccsem=None):
        self.stream, self.method, self.kwargs = stream, method, kwargs
        self.deps, self.kind, self.ccsem = deps, kind, ccsem


class Buf:
    def __init__(self, pr, h, P, Fr, space):
        self.pr, self.h, self.P, self.F, self.space = pr, h, P, Fr, space
        self.regions = {}

    def _conflicts(self, rect):
        p0, p1, f0, f1 = rect
        return [(r, st) for r, st in self.regions.items()
                if r[0] < p1 and p0 < r[1] and r[2] < f1 and f0 < r[3]]

    def dep_read(self, rect):
        deps = set()
        for _, st in self._conflicts(rect):
            if st[0] is not None:
                deps.add(st[0])
        return deps

    def dep_write(self, rect):
        deps = set()
        for _, st in self._conflicts(rect):
            if st[0] is not None:
                deps.add(st[0])
            deps.update(st[1].items())
        return deps

    def note_read(self, rect, tok):
        st = self.regions.get(rect)
        if st is None:
            st = self.regions.setdefault(rect, [None, {}])
        if st[1].get(tok[0], 0) < tok[1]:
            st[1][tok[0]] = tok[1]

    def note_write(self, rect, tok):
        dead = [r for r in self.regions
                if rect[0] <= r[0] and r[1] <= rect[1] and rect[2] <= r[2] and r[3] <= rect[3]]
        for r in dead:
            del self.regions[r]
        self.regions[rect] = [tok, {}]


class V:
    __slots__ = ("buf", "rect", "ap")

    def __init__(self, buf, p0, p1, f0, f1, ap=None):
        self.buf = buf
        self.rect = (p0, p1, f0, f1)
        self.ap = buf.h[p0:p1, f0:f1] if ap is None else ap


def vw(buf, p0=0, p1=None, f0=0, f1=None):
    return V(buf, p0, buf.P if p1 is None else p1, f0, buf.F if f1 is None else f1)


class FM:

    def __init__(self, pr, name, n, T, dt=F32):
        self.buf = pr.sbuf(name, 128, n * T, dt)
        self.n, self.T = n, T
        self.off = 0

    def v(self, i, f0=0, f1=None):
        f1 = self.T if f1 is None else f1
        return V(self.buf, 0, 128, self.off + i * self.T + f0, self.off + i * self.T + f1)


class FMv:

    def __init__(self, buf, off, n, T):
        self.buf, self.off, self.n, self.T = buf, off, n, T

    def v(self, i, f0=0, f1=None):
        f1 = self.T if f1 is None else f1
        return V(self.buf, 0, 128, self.off + i * self.T + f0, self.off + i * self.T + f1)


class Sub:

    def __init__(self, buf, off, P_, width):
        self.buf, self.off, self.P, self.width = buf, off, P_, width

    def v(self, p0=0, p1=None, f0=0, f1=None):
        return V(self.buf, p0, self.P if p1 is None else p1,
                 self.off + f0, self.off + (self.width if f1 is None else f1))


class Prog:
    def __init__(self):
        self.nc = bass.Bass("TRN2", target_bir_lowering=False, dynamic_dma_scratch_size=16384)
        self.ops = {}
        self.seq = {}
        self.n_cc = 0
        self._uid = 0
        self.pb = [Buf(self, self.nc.alloc_psum_tensor("pb%d" % i, [128, 512], F32),
                       128, 512, "psum") for i in range(8)]

    def uid(self):
        self._uid += 1
        return self._uid

    def dram_in(self, name, shape, dt=F32):
        h = self.nc.dram_tensor(name, list(shape), dt, kind="ExternalInput")
        return Buf(self, h, shape[0], int(np.prod(shape[1:])), "dram")

    def dram_out(self, name, shape, dt=F32):
        h = self.nc.dram_tensor(name, list(shape), dt, kind="ExternalOutput")
        return Buf(self, h, shape[0], int(np.prod(shape[1:])), "dram")

    def dram_tmp(self, name, shape, dt=F32):
        h = self.nc.dram_tensor(name, list(shape), dt, kind="Internal")
        return Buf(self, h, shape[0], int(np.prod(shape[1:])), "dram")

    def sbuf(self, name, P, Fr, dt=F32):
        return Buf(self, self.nc.alloc_sbuf_tensor(name, [P, Fr], dt), P, Fr, "sbuf")

    def psum(self, name, P, Fr, dt=F32):
        return Buf(self, self.nc.alloc_psum_tensor(name, [P, Fr], dt), P, Fr, "psum")

    def _rec(self, stream, issue, method, kwargs, reads, writes, kind, ccsem=None):
        deps = set()
        for x in reads:
            deps |= x.buf.dep_read(x.rect)
        for x in writes:
            deps |= x.buf.dep_write(x.rect)
        self.seq[stream] = self.seq.get(stream, 0) + 1
        tok = (stream, self.seq[stream])
        for x in reads:
            x.buf.note_read(x.rect, tok)
        for x in writes:
            x.buf.note_write(x.rect, tok)
        deps.discard(tok)
        if kind == "dma" and self.seq[stream] > 1:
            deps.add((stream, self.seq[stream] - 1))
        op = Op(stream, method, kwargs, deps, kind, ccsem)
        op.snapshot = {k_: v_ for k_, v_ in self.seq.items() if k_ != stream}
        self.ops.setdefault(issue, []).append(op)
        return tok

    def comp(self, engine, method, kwargs, reads=(), writes=()):
        return self._rec(engine, engine, method, kwargs, reads, writes, "comp")

    def dma(self, q, dst, src):
        qb = _DMAQ[q]
        k = self.seq.get("_cnt_" + qb, 0)
        self.seq["_cnt_" + qb] = k + 1
        return self._rec("%s_%d" % (qb, k % 4), q, "dma_start",
                         dict(out=dst.ap, in_=src.ap), [src], [dst], "dma")

    def cc_op(self, kind, src, dst, groups):
        self.n_cc += 1
        ccs = "cc%d" % self.n_cc
        return self._rec(ccs, "gpsimd", "collective_compute",
                         dict(kind=kind, op=ALU.bypass, replica_groups=groups,
                              ins=[src.h.ap().opt()], outs=[dst.h.ap().opt()]),
                         [vw(src)], [vw(dst)], "cc", ccsem=ccs)

    def emit(self, paranoid=False):
        nc = self.nc
        streams = sorted(s for s in self.seq if not s.startswith("_cnt_"))
        final_dma = {q: 16 * self.seq[q] for q in streams if q.startswith("q")}
        with ExitStack() as stack:
            sems = {s: stack.enter_context(nc.semaphore("s_" + s)) for s in streams}
            block = stack.enter_context(nc.Block())

            def mk(eng_name):
                def body(eng):
                    guar = {}
                    for op in self.ops.get(eng_name, ()):
                        deps = set(op.snapshot.items()) if paranoid == 1 else op.deps
                        for ds, dq in sorted(deps):
                            if ds == op.stream and ds == "tensor":
                                continue
                            thr = dq * (16 if ds.startswith("q") else 1)
                            if guar.get(ds, 0) >= thr:
                                continue
                            eng.wait_ge(sems[ds], thr)
                            guar[ds] = thr
                        if op.kind == "cc":
                            kw = op.kwargs
                            eng.collective_compute(
                                kw["kind"], kw["op"], replica_groups=kw["replica_groups"],
                                ins=kw["ins"], outs=kw["outs"]).then_inc(sems[op.ccsem])
                        else:
                            getattr(eng, op.method)(**op.kwargs).then_inc(
                                sems[op.stream], 16 if op.kind == "dma" else 1)
                    q = _DMAQ.get(eng_name)
                    if q is not None:
                        for cls in range(4):
                            qq = "%s_%d" % (q, cls)
                            if qq in final_dma:
                                eng.wait_ge(sems[qq], final_dma[qq])
                return body

            for eng_name in ("sync", "tensor", "vector", "scalar", "gpsimd"):
                if eng_name in self.ops:
                    getattr(block, eng_name)(mk(eng_name))
        return nc

    def mm(self, out, lhsT, rhs, start=True, stop=True):
        return self.comp("tensor", "matmul",
                         dict(out=out.ap, lhsT=lhsT.ap, rhs=rhs.ap, start=start, stop=stop),
                         [lhsT, rhs], [out])

    def transp(self, out, inp, ident):
        return self.comp("tensor", "transpose",
                         dict(out=out.ap, in_=inp.ap, identity=ident.ap),
                         [inp, ident], [out])

    def act(self, out, inp, func, bias=0.0, scale=1.0, accum=None):
        kw = dict(out=out.ap, in_=inp.ap, func=func)
        reads = [inp]
        for k, val in (("bias", bias), ("scale", scale)):
            if isinstance(val, V):
                kw[k] = val.ap
                reads.append(val)
            else:
                kw[k] = float(val)
        writes = [out]
        if accum is not None:
            kw["accum_out"] = accum.ap
            writes.append(accum)
        return self.comp("scalar", "activation", kw, reads, writes)

    def vtt(self, out, a, b, op, engine="vector"):
        return self.comp(engine, "tensor_tensor",
                         dict(out=out.ap, in0=a.ap, in1=b.ap, op=op), [a, b], [out])

    def vts(self, out, a, s1, s2, op0, op1=None, engine="vector"):
        kw = dict(out=out.ap, in0=a.ap,
                  scalar1=(s1.ap if isinstance(s1, V) else float(s1)),
                  scalar2=(s2.ap if isinstance(s2, V) else (None if s2 is None else float(s2))),
                  op0=op0)
        if op1 is not None:
            kw["op1"] = op1
        reads = [a] + [x for x in (s1, s2) if isinstance(x, V)]
        return self.comp(engine, "tensor_scalar", kw, reads, [out])

    def stt(self, out, in0, scalar, in1, op0, op1, engine="vector"):
        reads = [in0, in1] + ([scalar] if isinstance(scalar, V) else [])
        return self.comp(engine, "scalar_tensor_tensor",
                         dict(out=out.ap, in0=in0.ap,
                              scalar=(scalar.ap if isinstance(scalar, V) else float(scalar)),
                              in1=in1.ap, op0=op0, op1=op1), reads, [out])

    def scan_op(self, out, d0, d1, initial):
        reads = [d0, d1] + ([initial] if isinstance(initial, V) else [])
        return self.comp("vector", "tensor_tensor_scan",
                         dict(out=out.ap, data0=d0.ap, data1=d1.ap,
                              initial=(initial.ap if isinstance(initial, V) else float(initial)),
                              op0=ALU.mult, op1=ALU.add), reads, [out])

    def copy(self, out, inp, engine="vector"):
        meth = "copy" if engine == "scalar" else "tensor_copy"
        return self.comp(engine, meth, dict(out=out.ap, in_=inp.ap), [inp], [out])

    def reduce(self, out, inp, op):
        return self.comp("vector", "tensor_reduce",
                         dict(out=out.ap, in_=inp.ap, axis=AXX, op=op), [inp], [out])

    def recip(self, out, inp):
        return self.comp("vector", "reciprocal", dict(out=out.ap, in_=inp.ap), [inp], [out])

    def memset(self, out, val, engine="vector"):
        return self.comp(engine, "memset", dict(ap=out.ap, constant=val), [], [out])


def dview(dbuf, sl_ap):
    return V(dbuf, 0, dbuf.P, 0, dbuf.F, ap=sl_ap)


def rearr_block(dh, c0, c1, p=128):
    return dh[:, c0:c1].rearrange("(a p) c -> p a c", p=p)



def ph_consts(pr, bf=False):
    cst = pr.dram_in("consts", [128, 2696], F32)
    sb = pr.sbuf("cst", 128, 648)
    pr.dma("sync", vw(sb), dview(Buf(pr, cst.h, 128, 2696, "dram"), cst.h[:, 0:648]))
    d = dict(ones_col=V(sb, 0, 128, 0, 1), ones_row=V(sb, 0, 1, 8, 136),
             ones_row_at=lambda p: V(sb, p, p + 1, 8, 136),
             ones512_at=lambda p: V(sb, p, p + 1, 8, 520),
             eps_at=lambda p: V(sb, p, p + 1, 1, 2),
             ident=V(sb, 0, 128, 520, 648),
             identN=lambda n_: V(sb, 0, n_, 520, 520 + n_), sb=sb)
    if bf:
        sbb = pr.sbuf("cstbf", 128, 8, BF16)
        pr.copy(vw(sbb), V(sb, 0, 128, 0, 8))
        d["ones_col_bf"] = V(sbb, 0, 128, 0, 1)
        selb = pr.sbuf("selbf", 16, 2048, BF16)
        pr.dma("gpsimd", vw(selb), dview(cst, cst.h[0:16, 648:2696]))
        d["sel_bf"] = lambda n_: V(selb, 0, 16, n_ * 128, (n_ + 1) * 128)
        d["sbb"] = sbb
    return d


def ph_ln_stream(pr, tag, getx, n, T, sink, C, banks=(0, 1, 2, 3), gamma=None,
                 beta=None, eps=1e-5, dbg=None, in_bf=False):
    u = pr.uid()
    if getattr(pr, "ln_scratch", None) is None:
        pr.ln_scratch = (pr.sbuf("lnsq%d" % u, 128, 512),
                         pr.sbuf("lnst%d" % u, 128, 2048),
                         pr.sbuf("lntm%d" % u, 128, 512))
    sq, stat, tmp = pr.ln_scratch
    ps_s = V(pr.pb[banks[0]], 0, 1, 0, 512)
    ps_q = V(pr.pb[banks[1]], 0, 1, 0, 512)
    ps_r1 = pr.pb[banks[2]]
    ps_r2 = pr.pb[banks[3]]

    def row(k):
        return V(stat, 0, 1, k * 512, (k + 1) * 512)

    for bi in range(T // 512):
        f0, f1 = bi * 512, (bi + 1) * 512
        oc = C["ones_col_bf"] if in_bf else C["ones_col"]
        for i in range(n):
            xv = getx(i, f0, f1)
            pr.mm(ps_s, oc, xv, start=(i == 0), stop=(i == n - 1))
            pr.act(vw(sq), xv, AF.Square)
            pr.mm(ps_q, C["ones_col"], vw(sq), start=(i == 0), stop=(i == n - 1))
        inv = 1.0 / (128 * n)
        pr.vts(row(0), ps_s, inv, None, ALU.mult)
        pr.vts(row(1), ps_q, inv, None, ALU.mult)
        pr.vtt(row(2), row(0), row(0), ALU.mult)
        pr.vtt(row(1), row(1), row(2), ALU.subtract)
        pr.act(row(1), row(1), AF.Ln, bias=C["eps_at"](0))
        pr.act(row(1), row(1), AF.Exp, scale=-0.5)
        pr.stt(row(3), row(0), -1.0, row(1), ALU.mult, ALU.mult)
        if dbg is not None and bi == 0:
            dbg("stat0", vw(stat))
        pr.mm(vw(ps_r1), C["ones_row"], row(1))
        pr.mm(vw(ps_r2), C["ones_row"], row(3))
        for i in range(n):
            xv = getx(i, f0, f1)
            ov = sink(i, f0, f1)
            if gamma is None:
                pr.vtt(vw(tmp), xv, vw(ps_r1), ALU.mult)
                pr.vtt(ov, vw(tmp), vw(ps_r2), ALU.add)
            else:
                pr.vtt(vw(tmp), xv, vw(ps_r1), ALU.mult)
                pr.vtt(vw(tmp), vw(tmp), vw(ps_r2), ALU.add)
                pr.vts(ov, vw(tmp), gamma(i), beta(i), ALU.mult, ALU.add)


class WLoader:

    def __init__(self, pr, tag, wide, nslots=3, conv_engine="vector"):
        self.pr = pr
        self.wide = wide
        self.nslots = nslots
        self.f32 = pr.sbuf("wl32%s" % tag, 128, wide * nslots)
        self.b16 = pr.sbuf("wlbf%s" % tag, 128, wide * nslots, BF16)
        self.k = 0
        self.conv_engine = conv_engine

    def fetch(self, dram_buf, src_ap, width, q="sync"):
        s = self.k % self.nslots
        self.k += 1
        dst32 = V(self.f32, 0, 128, s * self.wide, s * self.wide + width)
        dstbf = V(self.b16, 0, 128, s * self.wide, s * self.wide + width)
        self.pr.dma(q, dst32, dview(dram_buf, src_ap))
        self.pr.copy(dstbf, dst32, engine=self.conv_engine)
        return dstbf


def ph_l0(pr, C, x0T_d, wgT_d, wuT_d, wdT_d, xn0T, g0T):
    ld = [pr.sbuf("l0x%d" % k, 128, 512) for k in range(4)]
    cache = {}

    def getx(i, f0, f1, _c=cache):
        key = (i, f0)
        if key in _c:
            return _c[key]
        slot = ld[(i + (f0 // 512) * 3) % 4]
        dst = vw(slot, 0, 128, 0, f1 - f0)
        pr.dma("sync", dst, dview(x0T_d, x0T_d.h[i * 128:(i + 1) * 128, f0:f1]))
        _c.clear()
        _c[key] = dst
        return dst

    def sink(i, f0, f1):
        return xn0T.v(i, f0, f1)

    ph_ln_stream(pr, "ln0", getx, 8, S, sink, C)

    wl = WLoader(pr, "f0", 1024, 2)
    psg = [pr.pb[4], pr.pb[5]]
    psu = [pr.pb[6], pr.pb[7]]
    psd = [pr.pb[0], pr.pb[1]]
    sg = pr.sbuf("f0sg", 128, 1024)
    hbuf = FM(pr, "f0h", 24, 1024, BF16)
    wdl = WLoader(pr, "fd", 128, 4)

    for tv in range(2):
        t0 = tv * 1024
        for ht in range(24):
            wg = wl.fetch(wgT_d, rearr_block(wgT_d.h, ht * 128, (ht + 1) * 128), 1024, q="sync")
            wu = wl.fetch(wuT_d, rearr_block(wuT_d.h, ht * 128, (ht + 1) * 128), 1024, q="gpsimd")
            for ec in range(8):
                lg = V(wg.buf, 0, 128, wg.rect[2] + ec * 128, wg.rect[2] + (ec + 1) * 128)
                lu = V(wu.buf, 0, 128, wu.rect[2] + ec * 128, wu.rect[2] + (ec + 1) * 128)
                for tb in range(2):
                    rhs = xn0T.v(ec, t0 + tb * 512, t0 + (tb + 1) * 512)
                    pr.mm(vw(psg[tb]), lg, rhs, start=(ec == 0), stop=(ec == 7))
                    pr.mm(vw(psu[tb]), lu, rhs, start=(ec == 0), stop=(ec == 7))
            for tb in range(2):
                pr.act(V(sg, 0, 128, tb * 512, (tb + 1) * 512), vw(psg[tb]),
                       AF.Silu, scale=1.0 / 127.0)
                pr.vtt(hbuf.v(ht, tb * 512, (tb + 1) * 512),
                       V(sg, 0, 128, tb * 512, (tb + 1) * 512), vw(psu[tb]), ALU.mult)
        for et in range(8):
            for ht in range(24):
                wd = wdl.fetch(wdT_d, wdT_d.h[ht * 128:(ht + 1) * 128, et * 128:(et + 1) * 128],
                               128, q="scalar")
                for tb in range(2):
                    pr.mm(vw(psd[tb]), wd, hbuf.v(ht, tb * 512, (tb + 1) * 512),
                          start=(ht == 0), stop=(ht == 23))
            for tb in range(2):
                pr.act(g0T.v(et, t0 + tb * 512, t0 + (tb + 1) * 512), vw(psd[tb]),
                       AF.Silu, scale=1.0 / (127.0 * 127.0))



def _prep_common(I):
    f32 = np.float32
    P = {}
    x0 = np.asarray(I["L0_attn_x"], f32)
    x0T = np.ascontiguousarray(x0.transpose(0, 2, 1))
    P["x0T_rot"] = [np.ascontiguousarray(
        np.concatenate([x0T[c // 2][:, (c % 2) * TOWN:(c % 2) * TOWN + TOWN],
                        x0T[c // 2][:, ((c % 2) ^ 1) * TOWN:((c % 2) ^ 1) * TOWN + TOWN]],
                       axis=1)) for c in range(8)]
    P["wgT"] = np.ascontiguousarray(np.asarray(I["L0_ffn_w_gate"], np.int32).astype(f32).T)
    P["wuT"] = np.ascontiguousarray(np.asarray(I["L0_ffn_w_up"], np.int32).astype(f32).T)
    P["wdT"] = np.ascontiguousarray(np.asarray(I["L0_ffn_w_down"], np.int32).astype(f32).T)
    consts = np.zeros((128, 2696), f32)
    consts[:, 0] = 1.0
    consts[:, 1] = 1e-5
    for p_ in (0, 32, 64, 96):
        consts[p_, 8:520] = 1.0
    consts[:, 520:648] = np.eye(128, dtype=f32)
    for n_ in range(16):
        consts[n_, 648 + n_ * 128:648 + (n_ + 1) * 128] = 1.0
    P["consts"] = consts
    return P


def build_launch_A():
    pr = Prog()
    C = ph_consts(pr)
    x0T_d = pr.dram_in("x0T", [E, S])
    wgT_d = pr.dram_in("wgT", [E, HID0])
    wuT_d = pr.dram_in("wuT", [E, HID0])
    wdT_d = pr.dram_in("wdT", [HID0, E])
    xn0T_o = pr.dram_out("xn0T_o", [E, S], BF16)
    g0T_o = pr.dram_out("g0T_o", [E, TOWN], BF16)
    xn0T = FM(pr, "xn0T", 8, S, BF16)
    g0T = FM(pr, "g0T", 8, TOWN, BF16)
    ph_l0(pr, C, x0T_d, wgT_d, wuT_d, wdT_d, xn0T, g0T)
    for i in range(8):
        pr.dma("gpsimd", dview(xn0T_o, xn0T_o.h[i * 128:(i + 1) * 128, :]), xn0T.v(i))
        pr.dma("scalar", dview(g0T_o, g0T_o.h[i * 128:(i + 1) * 128, :]), g0T.v(i))
    return pr


def ph_attn(pr, C, xn0T, W2sh_d, cvec_d, wqT_d, wkT_d, wvT_d, s0preT):
    wl = WLoader(pr, "at", 512, 3)
    qT = pr.sbuf("at_q", 64, TOWN, BF16)
    kT = pr.sbuf("at_k", 64, S, BF16)
    vhat = pr.sbuf("at_v", 128, 32 * 64, BF16)
    vtmp = pr.sbuf("at_vt", 64, 512)
    band = pr.sbuf("at_bd", 128, 16 * 512, BF16)
    bb32 = pr.sbuf("at_b32", 128, 2 * 512)
    ccol = pr.sbuf("at_cc", 128, 1)
    expS = pr.sbuf("at_es", 128, 2 * 512, BF16)
    vsum = pr.sbuf("at_vs", 1, 64)
    rzrep = pr.sbuf("at_rr", 128, 512)
    tmp = pr.sbuf("at_tmp", 128, 512)

    psP = pr.pb[0]
    psT = pr.pb[1]
    psS = [pr.pb[6], pr.pb[7]]
    psA = pr.pb[4]
    psB = pr.pb[5]
    psZb = pr.pb[3]
    psR = pr.pb[2]

    for h in range(H):
        p0 = (h % 2) * 64
        et = h // 2
        pr.dma("gpsimd", vw(ccol), dview(cvec_d, bass.AP(cvec_d.h, h, [[0, 128], [1, 1]])))
        for d in range(16):
            dst = V(bb32, 0, 128, (d % 2) * 512, (d % 2) * 512 + 512)
            pr.dma("sync", dst, dview(W2sh_d,
                                      W2sh_d.h[h * 128:(h + 1) * 128,
                                               1920 - 128 * d:2432 - 128 * d]))
            pr.vts(V(band, 0, 128, d * 512, (d + 1) * 512), dst, vw(ccol), None,
                   ALU.subtract)
        wq = wl.fetch(wqT_d, rearr_block(wqT_d.h, h * 64, (h + 1) * 64), 512, q="sync")
        wk = wl.fetch(wkT_d, rearr_block(wkT_d.h, h * 64, (h + 1) * 64), 512, q="gpsimd")
        wv = wl.fetch(wvT_d, rearr_block(wvT_d.h, h * 64, (h + 1) * 64), 512, q="scalar")

        def proj(wbf, nblk, dst_row):
            for qb in range(nblk):
                pp = V(psP, 0, 64, 0, 512)
                for ec in range(8):
                    lw = V(wbf.buf, 0, 128, wbf.rect[2] + ec * 64, wbf.rect[2] + (ec + 1) * 64)
                    pr.mm(pp, lw, xn0T.v(ec, qb * 512, (qb + 1) * 512),
                          start=(ec == 0), stop=(ec == 7))
                pr.act(V(dst_row, 0, 64, qb * 512, (qb + 1) * 512), pp, AF.Copy)

        proj(wq, 4, qT)
        proj(wk, 8, kT)
        for qb in range(8):
            pp = V(psP, 0, 64, 0, 512)
            for ec in range(8):
                lw = V(wv.buf, 0, 128, wv.rect[2] + ec * 64, wv.rect[2] + (ec + 1) * 64)
                pr.mm(pp, lw, xn0T.v(ec, qb * 512, (qb + 1) * 512),
                      start=(ec == 0), stop=(ec == 7))
            pr.copy(vw(vtmp), pp)
            for j in range(4):
                tc = qb * 4 + j
                pt = V(psT, 0, 128, 0, 64)
                pr.transp(pt, V(vtmp, 0, 64, j * 128, (j + 1) * 128), C["identN"](64))
                pr.copy(V(vhat, 0, 128, tc * 64, (tc + 1) * 64), pt)
        pc = V(psZb, 0, 1, 0, 64)
        for tc in range(32):
            pr.mm(pc, C["ones_col_bf"], V(vhat, 0, 128, tc * 64, (tc + 1) * 64),
                  start=(tc == 0), stop=(tc == 31))
        pr.vts(vw(vsum), pc, V(ccol, 0, 1, 0, 1), None, ALU.mult)
        for qb in range(4):
            pA = V(psA, p0, p0 + 64, 0, 512)
            pZ = V(psZb, p0, p0 + 1, 0, 512)
            for kc in range(32):
                pS = vw(psS[kc % 2])
                pr.mm(pS, V(kT, 0, 64, kc * 128, (kc + 1) * 128),
                      V(qT, 0, 64, qb * 512, (qb + 1) * 512))
                es = V(expS, 0, 128, (kc % 2) * 512, (kc % 2) * 512 + 512)
                pr.act(es, pS, AF.Exp, scale=1.0 / 32.0)
                pr.mm(pA, V(vhat, 0, 128, kc * 64, (kc + 1) * 64), es,
                      start=(kc == 0), stop=(kc == 31))
                pr.mm(pZ, C["ones_col_bf"], es, start=(kc == 0), stop=(kc == 31))
            pB = V(psB, p0, p0 + 64, 0, 512)
            bl = list(range(16 + 4 * qb, 32))
            for j, kc in enumerate(bl):
                d = kc - 16 - 4 * qb
                pr.mm(pB, V(vhat, 0, 128, kc * 64, (kc + 1) * 64),
                      V(band, 0, 128, d * 512, (d + 1) * 512),
                      start=(j == 0), stop=False)
            pr.mm(pB, vw(vsum), C["ones512_at"](0), start=False, stop=True)
            rzs = V(tmp, p0, p0 + 1, 0, 512)
            pr.recip(rzs, pZ)
            pr.mm(vw(psR), C["ones_row_at"](p0), rzs)
            pr.copy(vw(rzrep), vw(psR))
            o1 = V(s0preT.buf, p0, p0 + 64,
                   et * TOWN + qb * 512, et * TOWN + (qb + 1) * 512)
            pr.vtt(o1, pA, V(rzrep, p0, p0 + 64, 0, 512), ALU.mult)
            pr.vtt(o1, o1, pB, ALU.add)


def build_launch_B():
    pr = Prog()
    C = ph_consts(pr, bf=True)
    xn0T_d = pr.dram_in("xn0T", [E, S], BF16)
    wqT_d = pr.dram_in("wqT", [E, E])
    wkT_d = pr.dram_in("wkT", [E, E])
    wvT_d = pr.dram_in("wvT", [E, E])
    W2sh_d = pr.dram_in("W2sh", [H * 128, 2432])
    cvec_d = pr.dram_in("cvec", [1, 16])
    gb_d = pr.dram_in("gb", [128, 16])
    s0T_o = pr.dram_out("s0T_o", [E, TOWN])
    xn0T = FM(pr, "xn0Ts", 8, S, BF16)
    for i in range(8):
        pr.dma("sync", xn0T.v(i), dview(xn0T_d, xn0T_d.h[i * 128:(i + 1) * 128, :]))
    gbsb = pr.sbuf("gbsb", 128, 16)
    pr.dma("scalar", vw(gbsb), vw(gb_d))
    s0preT = FM(pr, "s0pre", 8, TOWN, F32)
    ph_attn(pr, C, xn0T, W2sh_d, cvec_d, wqT_d, wkT_d, wvT_d, s0preT)
    ph_ln_stream(pr, "s0ln", lambda i, f0, f1: s0preT.v(i, f0, f1), 8, TOWN,
                 lambda i, f0, f1: s0preT.v(i, f0, f1), C,
                 gamma=lambda i: V(gbsb, 0, 128, i, i + 1),
                 beta=lambda i: V(gbsb, 0, 128, 8 + i, 9 + i))
    for i in range(8):
        pr.dma("gpsimd", dview(s0T_o, s0T_o.h[i * 128:(i + 1) * 128, :]), s0preT.v(i))
    return pr


def ph_combine(pr, C, x0T_d, s0T_d, g0T_d, x1T):
    l0 = pr.sbuf("cb0", 128, 512)
    l1 = pr.sbuf("cb1", 128, 512)
    l2 = pr.sbuf("cb2", 128, 512, BF16)
    for i in range(8):
        for bk in range(4):
            f0, f1 = bk * 512, (bk + 1) * 512
            pr.dma("sync", vw(l0), dview(x0T_d, x0T_d.h[i * 128:(i + 1) * 128, f0:f1]))
            pr.dma("gpsimd", vw(l1), dview(s0T_d, s0T_d.h[i * 128:(i + 1) * 128, f0:f1]))
            pr.dma("scalar", vw(l2), dview(g0T_d, g0T_d.h[i * 128:(i + 1) * 128, f0:f1]))
            ov = x1T.v(i, f0, f1)
            pr.vtt(ov, vw(l1), vw(l2), ALU.mult)
            pr.vtt(ov, ov, vw(l0), ALU.add)


def build_launch_C():
    pr = Prog()
    C = ph_consts(pr, bf=True)
    x0T_d = pr.dram_in("x0T", [E, S])
    s0T_d = pr.dram_in("s0T", [E, TOWN])
    g0T_d = pr.dram_in("g0T", [E, TOWN], BF16)
    lgb_d = pr.dram_in("lgb", [128, 16])
    ipT_d = pr.dram_in("ipT", [E, 2 * DIN])
    x1T_o = pr.dram_out("x1T_o", [E, TOWN])
    xn1T_o = pr.dram_out("xn1T_o", [E, TOWN], BF16)
    xpT_o = pr.dram_out("xpT_o", [DIN, TOWN], BF16)
    zsT_o = pr.dram_out("zsT_o", [DIN, TOWN], BF16)

    x1T = FM(pr, "x1T", 8, TOWN, F32)
    xn1T = FM(pr, "xn1T", 8, TOWN, BF16)
    ln2T = FM(pr, "ln2T", 8, TOWN, BF16)
    lgb = pr.sbuf("lgbs", 128, 16)
    pr.dma("scalar", vw(lgb), vw(lgb_d))

    ph_combine(pr, C, x0T_d, s0T_d, g0T_d, x1T)
    for i in range(8):
        pr.dma("sync", dview(x1T_o, x1T_o.h[i * 128:(i + 1) * 128, :]), x1T.v(i))
    ph_ln_stream(pr, "ln1", lambda i, f0, f1: x1T.v(i, f0, f1), 8, TOWN,
                 lambda i, f0, f1: xn1T.v(i, f0, f1), C)
    for i in range(8):
        pr.dma("gpsimd", dview(xn1T_o, xn1T_o.h[i * 128:(i + 1) * 128, :]), xn1T.v(i))
    ph_ln_stream(pr, "ln2", lambda i, f0, f1: xn1T.v(i, f0, f1), 8, TOWN,
                 lambda i, f0, f1: ln2T.v(i, f0, f1), C,
                 gamma=lambda i: V(lgb, 0, 128, i, i + 1),
                 beta=lambda i: V(lgb, 0, 128, 8 + i, 9 + i), in_bf=True)
    wl = WLoader(pr, "ip", 1024, 2)
    ps4 = [pr.pb[4], pr.pb[5], pr.pb[6], pr.pb[7]]
    stage = pr.sbuf("ipst", 128, 2048, BF16)
    for rt in range(32):
        wb = wl.fetch(ipT_d, rearr_block(ipT_d.h, rt * 128, (rt + 1) * 128), 1024,
                      q=("sync", "gpsimd")[rt % 2])
        for ec in range(8):
            lw = V(wb.buf, 0, 128, wb.rect[2] + ec * 128, wb.rect[2] + (ec + 1) * 128)
            for tb in range(4):
                pr.mm(vw(ps4[tb]), lw, ln2T.v(ec, tb * 512, (tb + 1) * 512),
                      start=(ec == 0), stop=(ec == 7))
        for tb in range(4):
            sv = V(stage, 0, 128, tb * 512, (tb + 1) * 512)
            if rt < 16:
                pr.act(sv, vw(ps4[tb]), AF.Copy)
            else:
                pr.act(sv, vw(ps4[tb]), AF.Silu)
        dst = xpT_o if rt < 16 else zsT_o
        r0 = (rt % 16) * 128
        pr.dma("scalar", dview(dst, dst.h[r0:r0 + 128, :]), vw(stage))
    return pr


def build_launch_D():
    pr = Prog()
    C = ph_consts(pr, bf=True)
    xpT_d = pr.dram_in("xpT", [DIN, TOWN], BF16)
    halo_d = pr.dram_in("halo", [DIN, 3], BF16)
    convw_d = pr.dram_in("convw", [DIN, 4])
    cols_d = pr.dram_in("cols", [128, 32])
    xprojT_d = pr.dram_in("xprojT", [DIN, 96])
    dtprojT_d = pr.dram_in("dtprojT", [DTR, DIN])
    uT_o = pr.dram_out("uT_o", [DIN, TOWN], BF16)
    dtT_o = pr.dram_out("dtT_o", [DIN, TOWN])
    bcT_o = pr.dram_out("bcT_o", [32, TOWN])

    cw = pr.sbuf("cw", 128, 64)
    pr.dma("sync", vw(cw), dview(convw_d, convw_d.h[:, :].rearrange("(a p) k -> p a k", p=128)))
    cols = pr.sbuf("colss", 128, 32)
    pr.dma("sync", vw(cols), vw(cols_d))

    uT = FM(pr, "uT", 16, TOWN, BF16)
    cbuf = pr.sbuf("cbuf", 128, 3 + TOWN, BF16)
    acc = pr.sbuf("acc", 128, TOWN)
    for dt_ in range(16):
        pr.dma("sync", V(cbuf, 0, 128, 0, 3),
               dview(halo_d, halo_d.h[dt_ * 128:(dt_ + 1) * 128, :]))
        pr.dma("sync", V(cbuf, 0, 128, 3, 3 + TOWN),
               dview(xpT_d, xpT_d.h[dt_ * 128:(dt_ + 1) * 128, :]))

        def win(k):
            return V(cbuf, 0, 128, k, k + TOWN)

        pr.vts(vw(acc), win(0), V(cw, 0, 128, dt_ * 4, dt_ * 4 + 1), None, ALU.mult)
        for k in range(1, 4):
            pr.stt(vw(acc), win(k), V(cw, 0, 128, dt_ * 4 + k, dt_ * 4 + k + 1),
                   vw(acc), ALU.mult, ALU.add)
        pr.act(uT.v(dt_), vw(acc), AF.Silu, bias=V(cols, 0, 128, dt_, dt_ + 1))
        pr.dma("gpsimd", dview(uT_o, uT_o.h[dt_ * 128:(dt_ + 1) * 128, :]), uT.v(dt_))
    wx32 = pr.sbuf("wx32", 128, 16 * 96)
    wx = pr.sbuf("wx", 128, 16 * 96, BF16)
    pr.dma("sync", wx32.v(), dview(xprojT_d, xprojT_d.h[:, :].rearrange("(a p) c -> p a c", p=128)))
    pr.copy(vw(wx), vw(wx32))
    xdbl = pr.sbuf("xdbl", 96, TOWN)
    psX = pr.pb[0]
    for tb in range(4):
        px = V(psX, 0, 96, 0, 512)
        for dc in range(16):
            pr.mm(px, V(wx, 0, 128, dc * 96, (dc + 1) * 96),
                  uT.v(dc, tb * 512, (tb + 1) * 512), start=(dc == 0), stop=(dc == 15))
        pr.copy(V(xdbl, 0, 96, tb * 512, (tb + 1) * 512), px)
    pr.dma("gpsimd", dview(bcT_o, bcT_o.h[:, :]), V(xdbl, 64, 96, 0, TOWN))
    dtp = pr.sbuf("dtp", 64, TOWN, BF16)
    pr.copy(vw(dtp), V(xdbl, 0, 64, 0, TOWN))
    wd32 = pr.sbuf("wd32", 64, DIN)
    wdt = pr.sbuf("wdt", 64, DIN, BF16)
    pr.dma("sync", vw(wd32), vw(dtprojT_d))
    pr.copy(vw(wdt), vw(wd32))
    psD = pr.pb[1]
    s1 = pr.sbuf("sp1", 128, 512)
    s2 = pr.sbuf("sp2", 128, 512)
    s3 = pr.sbuf("sp3", 128, 512)
    stg = pr.sbuf("dtstage", 128, TOWN)
    for dt_ in range(16):
        for tb in range(4):
            pd = vw(psD)
            pr.mm(pd, V(wdt, 0, 64, dt_ * 128, (dt_ + 1) * 128),
                  V(dtp, 0, 64, tb * 512, (tb + 1) * 512))
            pr.act(vw(s1), pd, AF.Identity, bias=V(cols, 0, 128, 16 + dt_, 17 + dt_))
            pr.act(vw(s2), vw(s1), AF.Abs)
            pr.act(vw(s2), vw(s2), AF.Exp, scale=-1.0)
            pr.act(vw(s2), vw(s2), AF.Ln, bias=1.0)
            pr.act(vw(s3), vw(s1), AF.Relu)
            pr.vtt(V(stg, 0, 128, tb * 512, (tb + 1) * 512), vw(s3), vw(s2), ALU.add)
        pr.dma("scalar", dview(dtT_o, dtT_o.h[dt_ * 128:(dt_ + 1) * 128, :]), vw(stg))
    return pr


def build_launch_E():
    pr = Prog()
    C = ph_consts(pr)
    dtS_d = pr.dram_in("dtS", [E, S])
    uS_d = pr.dram_in("uS", [E, S], BF16)
    bS_d = pr.dram_in("bS", [NST, S])
    cS_d = pr.dram_in("cS", [NST, S])
    alog_d = pr.dram_in("alog", [128, 8 * NST])
    dD_d = pr.dram_in("dD", [128, 8])
    yS_o = pr.dram_out("yS_o", [E, S], BF16)

    asb = pr.sbuf("asb", 128, 8 * NST)
    pr.dma("sync", vw(asb), vw(alog_d))
    pr.act(vw(asb), vw(asb), AF.Exp)
    pr.vts(vw(asb), vw(asb), -1.0, None, ALU.mult)
    dsb = pr.sbuf("dsb", 128, 8)
    pr.dma("sync", vw(dsb), vw(dD_d))

    dts = pr.sbuf("dts", 128, S)
    us = pr.sbuf("us", 128, S, BF16)
    dub = pr.sbuf("dub", 128, S, BF16)
    yac = pr.sbuf("yac", 128, S)
    hbf = pr.sbuf("hbf", 128, S)
    da = pr.sbuf("da", 128, 2 * 512)
    db = pr.sbuf("db", 128, 2 * 512)
    brow = pr.sbuf("brow", 1, S)
    crow = pr.sbuf("crow", 1, S)
    ys = pr.sbuf("ys", 128, S, BF16)
    psB_ = pr.pb[0]
    psC_ = pr.pb[1]

    for dt_ in range(8):
        pr.dma("sync", vw(dts), dview(dtS_d, dtS_d.h[dt_ * 128:(dt_ + 1) * 128, :]))
        pr.dma("gpsimd", vw(us), dview(uS_d, uS_d.h[dt_ * 128:(dt_ + 1) * 128, :]))
        pr.vtt(vw(dub), vw(dts), vw(us), ALU.mult)
        for n in range(NST):
            pr.dma("sync", vw(brow), dview(bS_d, bS_d.h[n:n + 1, :]))
            pr.dma("gpsimd", vw(crow), dview(cS_d, cS_d.h[n:n + 1, :]))
            acol = V(asb, 0, 128, dt_ * NST + n, dt_ * NST + n + 1)
            for tc in range(8):
                f0, f1 = tc * 512, (tc + 1) * 512
                pB = vw(psB_)
                pC = vw(psC_)
                pr.mm(pB, C["ones_row"], V(brow, 0, 1, f0, f1))
                pr.mm(pC, C["ones_row"], V(crow, 0, 1, f0, f1))
                dav = V(da, 0, 128, (tc % 2) * 512, (tc % 2) * 512 + 512)
                pr.act(dav, V(dts, 0, 128, f0, f1), AF.Exp, scale=acol)
                dbv = V(db, 0, 128, (tc % 2) * 512, (tc % 2) * 512 + 512)
                pr.vtt(dbv, V(dub, 0, 128, f0, f1), pB, ALU.mult)
                ini = 0.0 if tc == 0 else V(hbf, 0, 128, f0 - 1, f0)
                pr.scan_op(V(hbf, 0, 128, f0, f1), dav, dbv, ini)
                if n == 0:
                    pr.vtt(V(yac, 0, 128, f0, f1), V(hbf, 0, 128, f0, f1), pC, ALU.mult)
                else:
                    pr.vtt(dav, V(hbf, 0, 128, f0, f1), pC, ALU.mult)
                    pr.vtt(V(yac, 0, 128, f0, f1), V(yac, 0, 128, f0, f1), dav, ALU.add)
        dcol = V(dsb, 0, 128, dt_, dt_ + 1)
        pr.stt(vw(ys), vw(us), dcol, vw(yac), ALU.mult, ALU.add)
        pr.dma("scalar", dview(yS_o, yS_o.h[dt_ * 128:(dt_ + 1) * 128, :]), vw(ys))
    return pr


def build_launch_F():
    pr = Prog()
    C = ph_consts(pr)
    y2_d = pr.dram_in("y2", [DIN, TOWN], BF16)
    zsT_d = pr.dram_in("zsT", [DIN, TOWN], BF16)
    opT_d = pr.dram_in("opT", [DIN, E])
    gwT_d = pr.dram_in("gwT", [E, DIN])
    gbias_d = pr.dram_in("gbias", [128, 16])
    xn1T_d = pr.dram_in("xn1T", [E, TOWN], BF16)
    s1T_o = pr.dram_out("s1T_o", [E, TOWN])

    gbias = pr.sbuf("gbiass", 128, 16)
    pr.dma("scalar", vw(gbias), vw(gbias_d))
    xn1T = FM(pr, "xn1Ts", 8, TOWN, BF16)
    for i in range(8):
        pr.dma("sync", xn1T.v(i), dview(xn1T_d, xn1T_d.h[i * 128:(i + 1) * 128, :]))

    ygT = FM(pr, "ygT", 16, TOWN, BF16)
    l0 = pr.sbuf("fl0", 128, TOWN, BF16)
    l1 = pr.sbuf("fl1", 128, TOWN, BF16)
    for dc in range(16):
        pr.dma("sync", vw(l0), dview(y2_d, y2_d.h[dc * 128:(dc + 1) * 128, :]))
        pr.dma("gpsimd", vw(l1), dview(zsT_d, zsT_d.h[dc * 128:(dc + 1) * 128, :]))
        pr.vtt(ygT.v(dc), vw(l0), vw(l1), ALU.mult)
    mgT = FM(pr, "mgT", 8, TOWN, BF16)
    wl = WLoader(pr, "op", 1024, 2)
    ps4 = [pr.pb[4], pr.pb[5], pr.pb[6], pr.pb[7]]
    for et in range(8):
        for hf in range(2):
            wb = wl.fetch(opT_d,
                          opT_d.h[hf * 1024:(hf + 1) * 1024, et * 128:(et + 1) * 128]
                          .rearrange("(a p) c -> p a c", p=128), 1024,
                          q=("sync", "gpsimd")[hf])
            for dcl in range(8):
                dc = hf * 8 + dcl
                lw = V(wb.buf, 0, 128, wb.rect[2] + dcl * 128, wb.rect[2] + (dcl + 1) * 128)
                for tb in range(4):
                    pr.mm(vw(ps4[tb]), lw, ygT.v(dc, tb * 512, (tb + 1) * 512),
                          start=(dc == 0), stop=(dc == 15))
        for tb in range(4):
            pr.act(mgT.v(et, tb * 512, (tb + 1) * 512), vw(ps4[tb]), AF.Gelu)
    sgT = FM(pr, "sgT", 8, TOWN, BF16)
    wl2 = WLoader(pr, "gl", 1024, 2)
    tmpf = pr.sbuf("ftmp", 128, 512)
    stage = pr.sbuf("fstage", 128, 512)
    for half in range(2):
        for rt in range(8):
            wb = wl2.fetch(gwT_d, rearr_block(gwT_d.h, (1 - half) * 1024 + rt * 128,
                                              (1 - half) * 1024 + (rt + 1) * 128), 1024,
                           q=("sync", "gpsimd")[rt % 2])
            for ec in range(8):
                lw = V(wb.buf, 0, 128, wb.rect[2] + ec * 128, wb.rect[2] + (ec + 1) * 128)
                for tb in range(4):
                    pr.mm(vw(ps4[tb]), lw, mgT.v(ec, tb * 512, (tb + 1) * 512),
                          start=(ec == 0), stop=(ec == 7))
            if half == 0:
                for tb in range(4):
                    pr.act(sgT.v(rt, tb * 512, (tb + 1) * 512), vw(ps4[tb]), AF.Sigmoid,
                           bias=V(gbias, 0, 128, 8 + rt, 9 + rt))
            else:
                for tb in range(4):
                    pr.act(vw(tmpf), vw(ps4[tb]), AF.Identity,
                           bias=V(gbias, 0, 128, rt, rt + 1))
                    sv = vw(stage)
                    pr.vtt(sv, vw(tmpf), sgT.v(rt, tb * 512, (tb + 1) * 512), ALU.mult)
                    pr.vtt(sv, sv, xn1T.v(rt, tb * 512, (tb + 1) * 512), ALU.add)
                    pr.dma("scalar",
                           dview(s1T_o, s1T_o.h[rt * 128:(rt + 1) * 128,
                                                tb * 512:(tb + 1) * 512]), sv)
    return pr


def build_launch_G():
    pr = Prog()
    C = ph_consts(pr, bf=True)
    xn1T_d = pr.dram_in("xn1T", [E, TOWN], BF16)
    x1T_d = pr.dram_in("x1T", [E, TOWN])
    s1T_d = pr.dram_in("s1T", [E, TOWN])
    rw_d = pr.dram_in("rw", [E, NEXP])
    rb_d = pr.dram_in("rb", [NEXP, 1])
    w1_d = pr.dram_in("w1", [NEXP * E, HIDM])
    b1r_d = pr.dram_in("b1r", [NEXP * 128, 32])
    w2_d = pr.dram_in("w2", [NEXP * HIDM, E])
    b2_d = pr.dram_in("b2", [NEXP, E])
    outT_o = pr.dram_out("outT_o", [E, TOWN])

    xn1T = FM(pr, "xn1Ts", 8, TOWN, BF16)
    for i in range(8):
        pr.dma("sync", xn1T.v(i), dview(xn1T_d, xn1T_d.h[i * 128:(i + 1) * 128, :]))

    rw32 = pr.sbuf("rw32", 128, 64)
    rwbf = pr.sbuf("rwbf", 128, 64, BF16)
    pr.dma("sync", vw(rw32), dview(rw_d, rw_d.h[:, :].rearrange("(a p) c -> p a c", p=128)))
    pr.copy(vw(rwbf), vw(rw32))
    rbsb = pr.sbuf("rbsb", 8, 1)
    pr.dma("sync", vw(rbsb), vw(rb_d))
    lgT = pr.sbuf("lgT", 8, TOWN)
    ps0 = pr.pb[0]
    for tb in range(4):
        p8 = V(ps0, 0, 8, 0, 512)
        for ec in range(8):
            pr.mm(p8, V(rwbf, 0, 128, ec * 8, (ec + 1) * 8),
                  xn1T.v(ec, tb * 512, (tb + 1) * 512), start=(ec == 0), stop=(ec == 7))
        pr.act(V(lgT, 0, 8, tb * 512, (tb + 1) * 512), p8, AF.Identity, bias=vw(rbsb))
    gbuf = pr.sbuf("gbuf", 128, 16 * 8)
    gTb = pr.sbuf("gTb", 8, TOWN, BF16)
    lt = pr.sbuf("lt", 128, 8)
    m1 = pr.sbuf("gm", 128, 4)
    e1 = pr.sbuf("ge", 128, 8)
    ps1 = pr.pb[1]
    ps2 = pr.pb[2]
    for tt in range(16):
        pt = V(ps1, 0, 128, 0, 8)
        pr.transp(pt, V(lgT, 0, 8, tt * 128, (tt + 1) * 128), C["identN"](8))
        pr.copy(vw(lt), pt)
        pr.reduce(V(m1, 0, 128, 0, 1), vw(lt), ALU.max)
        pr.vts(vw(e1), vw(lt), V(m1, 0, 128, 0, 1), None, ALU.is_equal)
        pr.stt(vw(e1), vw(e1), -1e30, vw(lt), ALU.mult, ALU.add)
        pr.reduce(V(m1, 0, 128, 1, 2), vw(e1), ALU.max)
        pr.vts(V(m1, 0, 128, 2, 3), V(m1, 0, 128, 0, 1), -1.0, None, ALU.mult)
        pr.act(vw(e1), vw(lt), AF.Exp, bias=V(m1, 0, 128, 2, 3))
        gv = V(gbuf, 0, 128, tt * 8, (tt + 1) * 8)
        pr.vts(gv, vw(lt), V(m1, 0, 128, 1, 2), None, ALU.is_ge)
        pr.vtt(gv, gv, vw(e1), ALU.mult)
        pr.reduce(V(m1, 0, 128, 3, 4), gv, ALU.add)
        pr.recip(V(m1, 0, 128, 3, 4), V(m1, 0, 128, 3, 4))
        pr.vts(gv, gv, V(m1, 0, 128, 3, 4), None, ALU.mult)
        pg = V(ps2, 0, 8, 0, 128)
        pr.transp(pg, gv, C["ident"])
        pr.copy(V(gTb, 0, 8, tt * 128, (tt + 1) * 128), pg)
    macc = pr.sbuf("macc", 128, 16 * 1024)
    b2bf = pr.sbuf("b2bf", 8, E, BF16)
    b232 = pr.sbuf("b232", 8, E)
    pr.dma("sync", vw(b232), vw(b2_d))
    pr.copy(vw(b2bf), vw(b232))
    ps3 = pr.pb[3]
    for tt in range(16):
        for cb in range(2):
            pb_ = vw(ps3)
            pr.mm(pb_, V(gTb, 0, 8, tt * 128, (tt + 1) * 128),
                  V(b2bf, 0, 8, cb * 512, (cb + 1) * 512))
            pr.copy(V(macc, 0, 128, tt * 1024 + cb * 512, tt * 1024 + (cb + 1) * 512), pb_)
    hbuf = FM(pr, "hq", 8, TOWN, BF16)
    b1sb = pr.sbuf("b1sb", 128, 32)
    wl = WLoader(pr, "w1", 1024, 2)
    w2r = pr.sbuf("w2r", 128, 1024)
    w2b = pr.sbuf("w2b", 128, 8 * 1024, BF16)
    psH = [pr.pb[4], pr.pb[5], pr.pb[6], pr.pb[7]]
    psY = [pr.pb[1], pr.pb[2]]
    for e in range(NEXP):
        pr.dma("scalar", vw(b1sb), dview(b1r_d, b1r_d.h[e * 128:(e + 1) * 128, :]))
        for hc in range(4):
            for htl in range(8):
                ht = hc * 8 + htl
                src = w1_d.h[e * E:(e + 1) * E, ht * 128:(ht + 1) * 128].rearrange("(a p) c -> p a c", p=128)
                wb = wl.fetch(w1_d, src, 1024, q=("sync", "gpsimd")[htl % 2])
                for ec in range(8):
                    lw = V(wb.buf, 0, 128, wb.rect[2] + ec * 128, wb.rect[2] + (ec + 1) * 128)
                    for tb in range(4):
                        pr.mm(vw(psH[tb]), lw, xn1T.v(ec, tb * 512, (tb + 1) * 512),
                              start=(ec == 0), stop=(ec == 7))
                for tb in range(4):
                    pr.act(hbuf.v(htl, tb * 512, (tb + 1) * 512), vw(psH[tb]), AF.Relu,
                           bias=V(b1sb, 0, 128, ht, ht + 1))
                pr.dma("scalar", vw(w2r),
                       dview(w2_d, w2_d.h[e * HIDM + ht * 128:e * HIDM + (ht + 1) * 128, :]))
                pr.copy(V(w2b, 0, 128, htl * 1024, (htl + 1) * 1024), vw(w2r))
            for tt in range(16):
                for cb in range(2):
                    pY = vw(psY[cb])
                    for htl in range(8):
                        pr.mm(pY, hbuf.v(htl, tt * 128, (tt + 1) * 128),
                              V(w2b, 0, 128, htl * 1024 + cb * 512, htl * 1024 + (cb + 1) * 512),
                              start=(htl == 0), stop=(htl == 7))
                    mv = V(macc, 0, 128, tt * 1024 + cb * 512, tt * 1024 + (cb + 1) * 512)
                    pr.stt(mv, pY, V(gbuf, 0, 128, tt * 8 + e, tt * 8 + e + 1), mv,
                           ALU.mult, ALU.add)
    g1t = pr.sbuf("g1t", 128, 1024)
    xl = pr.sbuf("xl", 128, 128)
    sl = pr.sbuf("sl", 128, 128)
    ost = pr.sbuf("ost", 128, 128)
    psG = pr.pb[0]
    for tt in range(16):
        pr.act(vw(g1t), V(macc, 0, 128, tt * 1024, (tt + 1) * 1024), AF.Silu)
        for ec in range(8):
            pg = V(psG, 0, 128, 0, 128)
            pr.transp(pg, V(g1t, 0, 128, ec * 128, (ec + 1) * 128), C["ident"])
            pr.dma("sync", vw(xl),
                   dview(x1T_d, x1T_d.h[ec * 128:(ec + 1) * 128, tt * 128:(tt + 1) * 128]))
            pr.dma("gpsimd", vw(sl),
                   dview(s1T_d, s1T_d.h[ec * 128:(ec + 1) * 128, tt * 128:(tt + 1) * 128]))
            pr.vtt(vw(ost), vw(sl), pg, ALU.mult)
            pr.vtt(vw(ost), vw(ost), vw(xl), ALU.add)
            pr.dma("scalar",
                   dview(outT_o, outT_o.h[ec * 128:(ec + 1) * 128, tt * 128:(tt + 1) * 128]),
                   vw(ost))
    return pr



def _prep_all(I):
    f32 = np.float32
    P = _prep_common(I)
    tab = np.asarray(I["L0_attn_bias_table"], f32)
    P["cvec"] = np.ascontiguousarray(tab[2046:2047, :])
    j = np.arange(2432)[None, :]
    p = np.arange(128)[:, None]
    P["W2sh"] = []
    for c in range(8):
        th = c % 2
        own0 = th * TOWN
        other0 = (1 - th) * TOWN
        idx = np.clip(own0 - other0 + 2175 + j - p, 0, 2046)
        P["W2sh"].append(np.ascontiguousarray(
            tab[idx, :].transpose(2, 0, 1).reshape(H * 128, 2432)))
    P["wqT"] = np.ascontiguousarray(np.asarray(I["L0_attn_Wq"], f32).T)
    P["wkT"] = np.ascontiguousarray(np.asarray(I["L0_attn_Wk"], f32).T)
    P["wvT"] = np.ascontiguousarray(np.asarray(I["L0_attn_Wv"], f32).T)
    gb = np.zeros((128, 16), f32)
    gb[:, 0:8] = np.asarray(I["L0_attn_ln_gamma"], f32).reshape(8, 128).T
    gb[:, 8:16] = np.asarray(I["L0_attn_ln_beta"], f32).reshape(8, 128).T
    P["gb"] = gb
    lgb = np.zeros((128, 16), f32)
    lgb[:, 0:8] = np.asarray(I["L1_attn_ln_g"], f32).reshape(8, 128).T
    lgb[:, 8:16] = np.asarray(I["L1_attn_ln_b"], f32).reshape(8, 128).T
    P["lgb"] = lgb
    P["ipT"] = np.ascontiguousarray(np.asarray(I["L1_attn_in_proj_w"], f32).T)
    P["convw"] = np.ascontiguousarray(np.asarray(I["L1_attn_conv_w"], f32)[:, 0, :])
    cols = np.zeros((128, 32), f32)
    cols[:, 0:16] = np.asarray(I["L1_attn_conv_b"], f32).reshape(16, 128).T
    cols[:, 16:32] = np.asarray(I["L1_attn_dt_proj_b"], f32).reshape(16, 128).T
    P["cols"] = cols
    P["xprojT"] = np.ascontiguousarray(np.asarray(I["L1_attn_x_proj_w"], f32).T)
    P["dtprojT"] = np.ascontiguousarray(np.asarray(I["L1_attn_dt_proj_w"], f32).T)
    alog = np.asarray(I["L1_attn_A_log"], f32)
    P["alog"] = [np.ascontiguousarray(
        alog[(c % 2) * 1024:(c % 2) * 1024 + 1024, :].reshape(8, 128, 16)
        .transpose(1, 0, 2).reshape(128, 8 * 16)) for c in range(8)]
    dd = np.asarray(I["L1_attn_D"], f32)
    P["dD"] = [np.ascontiguousarray(
        dd[(c % 2) * 1024:(c % 2) * 1024 + 1024].reshape(8, 128).T) for c in range(8)]
    P["opT"] = np.ascontiguousarray(np.asarray(I["L1_attn_out_proj_w"], f32).T)
    P["gwT"] = np.ascontiguousarray(np.asarray(I["L1_attn_glu_w"], f32).T)
    glb = np.asarray(I["L1_attn_glu_b"], f32)
    gbias = np.zeros((128, 16), f32)
    gbias[:, 0:8] = glb[0:1024].reshape(8, 128).T
    gbias[:, 8:16] = glb[1024:2048].reshape(8, 128).T
    P["gbias"] = gbias
    P["rw"] = np.ascontiguousarray(np.asarray(I["L1_ffn_router_w"], f32))
    P["rb"] = np.ascontiguousarray(np.asarray(I["L1_ffn_router_b"], f32).reshape(8, 1))
    P["w1"] = np.ascontiguousarray(np.asarray(I["L1_ffn_w1"], f32).reshape(NEXP * E, HIDM))
    P["b1r"] = np.ascontiguousarray(
        np.asarray(I["L1_ffn_b1"], f32).reshape(NEXP, 32, 128).transpose(0, 2, 1)
        .reshape(NEXP * 128, 32))
    P["w2"] = np.ascontiguousarray(np.asarray(I["L1_ffn_w2"], f32).reshape(NEXP * HIDM, E))
    P["b2"] = np.ascontiguousarray(np.asarray(I["L1_ffn_b2"], f32))
    return P


def kernel(**inputs):
    return kernel_fused(inputs)


def kernel_unfused(**inputs):
    P = _prep_all(inputs)
    cores = list(range(8))
    bf = lambda a: np.ascontiguousarray(a)

    prA = build_launch_A()
    resA = run_bass_kernel_spmd(prA.emit(), [dict(
        consts=P["consts"], x0T=P["x0T_rot"][c], wgT=P["wgT"], wuT=P["wuT"],
        wdT=P["wdT"]) for c in cores], core_ids=cores)
    xn0T = [resA.results[c]["xn0T_o"] for c in cores]
    g0T = [resA.results[c]["g0T_o"] for c in cores]

    prB = build_launch_B()
    resB = run_bass_kernel_spmd(prB.emit(), [dict(
        consts=P["consts"], xn0T=xn0T[c], wqT=P["wqT"], wkT=P["wkT"], wvT=P["wvT"],
        W2sh=P["W2sh"][c], cvec=P["cvec"], gb=P["gb"]) for c in cores], core_ids=cores)
    s0T = [resB.results[c]["s0T_o"] for c in cores]

    prC = build_launch_C()
    resC = run_bass_kernel_spmd(prC.emit(), [dict(
        consts=P["consts"], x0T=P["x0T_rot"][c], s0T=s0T[c], g0T=g0T[c],
        lgb=P["lgb"], ipT=P["ipT"]) for c in cores], core_ids=cores)
    x1T = [resC.results[c]["x1T_o"] for c in cores]
    xn1T = [resC.results[c]["xn1T_o"] for c in cores]
    xpT = [resC.results[c]["xpT_o"] for c in cores]
    zsT = [resC.results[c]["zsT_o"] for c in cores]

    zeros_halo = np.zeros((DIN, 3), BF16NP)
    prD = build_launch_D()
    resD = run_bass_kernel_spmd(prD.emit(), [dict(
        consts=P["consts"], xpT=xpT[c],
        halo=(zeros_halo if c % 2 == 0 else np.ascontiguousarray(xpT[c - 1][:, TOWN - 3:])),
        convw=P["convw"], cols=P["cols"], xprojT=P["xprojT"], dtprojT=P["dtprojT"])
        for c in cores], core_ids=cores)
    uT = [resD.results[c]["uT_o"] for c in cores]
    dtT = [resD.results[c]["dtT_o"] for c in cores]
    bcT = [resD.results[c]["bcT_o"] for c in cores]

    prE = build_launch_E()
    in_E = []
    for c in cores:
        b, dh = c // 2, c % 2
        r = slice(dh * 1024, dh * 1024 + 1024)
        in_E.append(dict(
            consts=P["consts"],
            dtS=np.ascontiguousarray(np.concatenate([dtT[2 * b][r], dtT[2 * b + 1][r]], 1)),
            uS=np.ascontiguousarray(np.concatenate([uT[2 * b][r], uT[2 * b + 1][r]], 1)),
            bS=np.ascontiguousarray(np.concatenate([bcT[2 * b][0:16], bcT[2 * b + 1][0:16]], 1)),
            cS=np.ascontiguousarray(np.concatenate([bcT[2 * b][16:32], bcT[2 * b + 1][16:32]], 1)),
            alog=P["alog"][c], dD=P["dD"][c]))
    resE = run_bass_kernel_spmd(prE.emit(), in_E, core_ids=cores)
    yS = [resE.results[c]["yS_o"] for c in cores]

    prF = build_launch_F()
    in_F = []
    for c in cores:
        b, th = c // 2, c % 2
        cs = slice(th * TOWN, th * TOWN + TOWN)
        y2 = np.ascontiguousarray(np.concatenate([yS[2 * b][:, cs], yS[2 * b + 1][:, cs]], 0))
        in_F.append(dict(consts=P["consts"], y2=y2, zsT=zsT[c], opT=P["opT"],
                         gwT=P["gwT"], gbias=P["gbias"], xn1T=xn1T[c]))
    resF = run_bass_kernel_spmd(prF.emit(), in_F, core_ids=cores)
    s1T = [resF.results[c]["s1T_o"] for c in cores]

    prG = build_launch_G()
    resG = run_bass_kernel_spmd(prG.emit(), [dict(
        consts=P["consts"], xn1T=xn1T[c], x1T=x1T[c], s1T=s1T[c], rw=P["rw"],
        rb=P["rb"], w1=P["w1"], b1r=P["b1r"], w2=P["w2"], b2=P["b2"])
        for c in cores], core_ids=cores)

    out = np.empty((B, S, E), np.float32)
    for c in cores:
        b, th = c // 2, c % 2
        out[b, th * TOWN:(th + 1) * TOWN, :] = resG.results[c]["outT_o"].T
    return out



PAIRS = [[0, 1], [2, 3], [4, 5], [6, 7]]


def build_fused(dbg=False):
    pr = Prog()
    C = ph_consts(pr, bf=True)
    pr.ln_scratch = (pr.sbuf("lnsq", 128, 512), pr.sbuf("lnst", 128, 2048),
                     pr.sbuf("lntm", 128, 512))

    SBF = pr.sbuf("SBF", 128, 51200, BF16)
    SF = pr.sbuf("SF", 128, 21120, F32)

    x0T_d = pr.dram_in("x0T", [E, S])
    wgT_d = pr.dram_in("wgT", [E, HID0])
    wuT_d = pr.dram_in("wuT", [E, HID0])
    wdT_d = pr.dram_in("wdT", [HID0, E])
    wqT_d = pr.dram_in("wqT", [E, E])
    wkT_d = pr.dram_in("wkT", [E, E])
    wvT_d = pr.dram_in("wvT", [E, E])
    W2sh_d = pr.dram_in("W2sh", [H * 128, 2432])
    cvec_d = pr.dram_in("cvec", [1, 16])
    gb_d = pr.dram_in("gb", [128, 16])
    lgb_d = pr.dram_in("lgb", [128, 16])
    ipT_d = pr.dram_in("ipT", [E, 2 * DIN])
    convw_d = pr.dram_in("convw", [DIN, 4])
    cols_d = pr.dram_in("cols", [128, 32])
    xprojT_d = pr.dram_in("xprojT", [DIN, 96])
    dtprojT_d = pr.dram_in("dtprojT", [DTR, DIN])
    alog_d = pr.dram_in("alog", [128, 8 * NST])
    dD_d = pr.dram_in("dD", [128, 8])
    maskbf_d = pr.dram_in("maskbf", [128, 1])
    mask0_d = pr.dram_in("mask0", [128, 1])
    opT_d = pr.dram_in("opT", [DIN, E])
    gwT_d = pr.dram_in("gwT", [E, DIN])
    gbias_d = pr.dram_in("gbias", [128, 16])
    rw_d = pr.dram_in("rw", [E, NEXP])
    rb_d = pr.dram_in("rb", [NEXP, 1])
    w1_d = pr.dram_in("w1", [NEXP * E, HIDM])
    b1r_d = pr.dram_in("b1r", [NEXP * 128, 32])
    w2_d = pr.dram_in("w2", [NEXP * HIDM, E])
    b2_d = pr.dram_in("b2", [NEXP, E])
    outT_o = pr.dram_out("outT_o", [E, TOWN])

    mk = pr.dram_out if dbg else pr.dram_tmp
    g0_d = mk("g0_d", [E, TOWN], BF16)
    s0_d = mk("s0_d", [E, TOWN])
    x1_d = mk("x1_d", [E, TOWN])
    xn1_d = mk("xn1_d", [E, TOWN], BF16)
    zs_d = mk("zs_d", [DIN, TOWN], BF16)
    xp_d = mk("xp_d", [DIN, TOWN], BF16)
    s1_d = mk("s1_d", [E, TOWN])
    cc1i = pr.dram_tmp("cc1i", [DIN, 3], BF16)
    cc1o = pr.dram_tmp("cc1o", [2 * DIN, 3], BF16)
    cc2i = pr.dram_tmp("cc2i", [DIN, TOWN])
    cc2o = pr.dram_tmp("cc2o", [2 * DIN, TOWN])
    cc3i = pr.dram_tmp("cc3i", [DIN, TOWN], BF16)
    cc3o = pr.dram_tmp("cc3o", [2 * DIN, TOWN], BF16)
    cc4i = pr.dram_tmp("cc4i", [32, TOWN])
    cc4o = pr.dram_tmp("cc4o", [64, TOWN])
    cc5i = pr.dram_tmp("cc5i", [DIN, TOWN], BF16)
    cc5o = pr.dram_tmp("cc5o", [2 * DIN, TOWN], BF16)

    gbsb = pr.sbuf("gbsb", 128, 16)
    pr.dma("scalar", vw(gbsb), vw(gb_d))
    lgbsb = pr.sbuf("lgbsb", 128, 16)
    pr.dma("scalar", vw(lgbsb), vw(lgb_d))
    gbias = pr.sbuf("gbiasb", 128, 16)
    pr.dma("scalar", vw(gbias), vw(gbias_d))
    colsb = pr.sbuf("colsb", 128, 32)
    pr.dma("scalar", vw(colsb), vw(cols_d))
    maskb = pr.sbuf("maskb", 128, 1)
    pr.dma("scalar", vw(maskb), vw(maskbf_d))
    mask0 = pr.sbuf("mask0b", 128, 1)
    pr.dma("scalar", vw(mask0), vw(mask0_d))

    xn0T = FMv(SBF, 0, 8, S)
    hbuf = FMv(SBF, 32768, 24, 512)
    wlb = Sub(SBF, 45056, 128, 2048)
    g0st = Sub(SBF, 47616, 128, 512)
    wdlb = Sub(SBF, 48128, 128, 3072)
    xls = [Sub(SF, k * 512, 128, 512) for k in range(4)]
    wlf = Sub(SF, 2048, 128, 2048)
    sgq = Sub(SF, 4608, 128, 512)
    wdlf = Sub(SF, 5120, 128, 3072)

    cache = {}

    def getx0(i, f0, f1, _c=cache):
        key = (i, f0)
        if key in _c:
            return _c[key]
        dst = xls[(i + (f0 // 512) * 3) % 4].v()
        pr.dma("sync", dst, dview(x0T_d, x0T_d.h[i * 128:(i + 1) * 128, f0:f1]))
        _c.clear()
        _c[key] = dst
        return dst

    ph_ln_stream(pr, "ln0", getx0, 8, S, lambda i, f0, f1: xn0T.v(i, f0, f1), C)

    kslot = [0]

    def fetch(dbuf, src_ap, fsub, bsub, width, nslots, q, conv="vector"):
        s = kslot[0] % nslots
        kslot[0] += 1
        d32 = fsub.v(0, 128, s * width, (s + 1) * width)
        dbf = bsub.v(0, 128, s * width, (s + 1) * width)
        pr.dma(q, d32, dview(dbuf, src_ap))
        pr.copy(dbf, d32, engine=conv)
        return dbf

    psg, psu, psd = pr.pb[4], pr.pb[5], pr.pb[6]
    for tq in range(4):
        t0 = tq * 512
        for ht in range(24):
            wg = fetch(wgT_d, rearr_block(wgT_d.h, ht * 128, (ht + 1) * 128), wlf, wlb,
                       1024, 2, ("sync", "gpsimd")[ht % 2])
            wu = fetch(wuT_d, rearr_block(wuT_d.h, ht * 128, (ht + 1) * 128), wlf, wlb,
                       1024, 2, ("gpsimd", "sync")[ht % 2])
            for ec in range(8):
                lg = V(wg.buf, 0, 128, wg.rect[2] + ec * 128, wg.rect[2] + (ec + 1) * 128)
                lu = V(wu.buf, 0, 128, wu.rect[2] + ec * 128, wu.rect[2] + (ec + 1) * 128)
                rhs = xn0T.v(ec, t0, t0 + 512)
                pr.mm(vw(psg), lg, rhs, start=(ec == 0), stop=(ec == 7))
                pr.mm(vw(psu), lu, rhs, start=(ec == 0), stop=(ec == 7))
            pr.act(sgq.v(), vw(psg), AF.Silu, scale=1.0 / 127.0)
            pr.vtt(hbuf.v(ht), sgq.v(), vw(psu), ALU.mult)
        for et in range(8):
            pr.dma("scalar", wdlf.v(),
                   dview(wdT_d, rearr_block(wdT_d.h, et * 128, (et + 1) * 128)))
            pr.copy(wdlb.v(), wdlf.v())
            for ht in range(24):
                pr.mm(vw(psd), wdlb.v(0, 128, ht * 128, (ht + 1) * 128), hbuf.v(ht),
                      start=(ht == 0), stop=(ht == 23))
            pr.act(g0st.v(), vw(psd), AF.Silu, scale=1.0 / (127.0 * 127.0))
            pr.dma("scalar", dview(g0_d, g0_d.h[et * 128:(et + 1) * 128, t0:t0 + 512]),
                   g0st.v())

    s0preT = FMv(SF, 0, 8, TOWN)
    qT = Sub(SBF, 32768, 64, 2048)
    kT = Sub(SBF, 34816, 64, 4096)
    vhat = Sub(SBF, 38912, 128, 2048)
    band = Sub(SBF, 40960, 128, 8192)
    awlb = Sub(SBF, 50176, 128, 1024)
    expS = Sub(SBF, 49152, 128, 1024)
    bb32 = Sub(SF, 16384, 128, 1024)
    vtmp = Sub(SF, 17408, 64, 512)
    rzrep = Sub(SF, 17920, 128, 512)
    atmp = Sub(SF, 18432, 128, 512)
    awlf = Sub(SF, 18944, 128, 1536)
    ccol = Sub(SF, 20480, 128, 1)
    vsum = Sub(SF, 20482, 1, 64)

    psP, psT_, psR, psZb, psA, psB = pr.pb[0], pr.pb[1], pr.pb[2], pr.pb[3], pr.pb[4], pr.pb[5]
    psS = [pr.pb[6], pr.pb[7]]
    aslot = [0]

    def afetch(dbuf, src_ap, q):
        s = aslot[0] % 2
        aslot[0] += 1
        d32 = awlf.v(0, 128, s * 512, (s + 1) * 512)
        dbf = awlb.v(0, 128, s * 512, (s + 1) * 512)
        pr.dma(q, d32, dview(dbuf, src_ap))
        pr.copy(dbf, d32)
        return dbf

    for h in range(H):
        p0 = (h % 2) * 64
        et = h // 2
        pr.dma("gpsimd", ccol.v(), dview(cvec_d, bass.AP(cvec_d.h, h, [[0, 128], [1, 1]])))
        for d in range(16):
            dst = bb32.v(0, 128, (d % 2) * 512, (d % 2) * 512 + 512)
            pr.dma("sync", dst, dview(W2sh_d, W2sh_d.h[h * 128:(h + 1) * 128,
                                                       1920 - 128 * d:2432 - 128 * d]))
            pr.vts(band.v(0, 128, d * 512, (d + 1) * 512), dst, ccol.v(), None, ALU.subtract)
        wq = afetch(wqT_d, rearr_block(wqT_d.h, h * 64, (h + 1) * 64), "sync")
        wk = afetch(wkT_d, rearr_block(wkT_d.h, h * 64, (h + 1) * 64), "gpsimd")
        wv = afetch(wvT_d, rearr_block(wvT_d.h, h * 64, (h + 1) * 64), "scalar")

        def proj(wbf, nblk, dst):
            for qb in range(nblk):
                pp = V(psP, 0, 64, 0, 512)
                for ec in range(8):
                    lw = V(wbf.buf, 0, 128, wbf.rect[2] + ec * 64, wbf.rect[2] + (ec + 1) * 64)
                    pr.mm(pp, lw, xn0T.v(ec, qb * 512, (qb + 1) * 512),
                          start=(ec == 0), stop=(ec == 7))
                pr.act(dst.v(0, 64, qb * 512, (qb + 1) * 512), pp, AF.Copy)

        proj(wq, 4, qT)
        proj(wk, 8, kT)
        for qb in range(8):
            pp = V(psP, 0, 64, 0, 512)
            for ec in range(8):
                lw = V(wv.buf, 0, 128, wv.rect[2] + ec * 64, wv.rect[2] + (ec + 1) * 64)
                pr.mm(pp, lw, xn0T.v(ec, qb * 512, (qb + 1) * 512),
                      start=(ec == 0), stop=(ec == 7))
            pr.copy(vtmp.v(), pp)
            for j in range(4):
                tc = qb * 4 + j
                pt = V(psT_, 0, 128, 0, 64)
                pr.transp(pt, vtmp.v(0, 64, j * 128, (j + 1) * 128), C["identN"](64))
                pr.copy(vhat.v(0, 128, tc * 64, (tc + 1) * 64), pt)
        pc = V(psZb, 0, 1, 0, 64)
        for tc in range(32):
            pr.mm(pc, C["ones_col_bf"], vhat.v(0, 128, tc * 64, (tc + 1) * 64),
                  start=(tc == 0), stop=(tc == 31))
        pr.vts(vsum.v(), pc, V(ccol.buf, 0, 1, ccol.off, ccol.off + 1), None, ALU.mult)
        for qb in range(4):
            pA = V(psA, p0, p0 + 64, 0, 512)
            pZ = V(psZb, p0, p0 + 1, 0, 512)
            for kc in range(32):
                pS = vw(psS[kc % 2])
                pr.mm(pS, kT.v(0, 64, kc * 128, (kc + 1) * 128),
                      qT.v(0, 64, qb * 512, (qb + 1) * 512))
                es = expS.v(0, 128, (kc % 2) * 512, (kc % 2) * 512 + 512)
                pr.act(es, pS, AF.Exp, scale=1.0 / 32.0)
                pr.mm(pA, vhat.v(0, 128, kc * 64, (kc + 1) * 64), es,
                      start=(kc == 0), stop=(kc == 31))
                pr.mm(pZ, C["ones_col_bf"], es, start=(kc == 0), stop=(kc == 31))
            pB = V(psB, p0, p0 + 64, 0, 512)
            bl = list(range(16 + 4 * qb, 32))
            for j, kc in enumerate(bl):
                d = kc - 16 - 4 * qb
                pr.mm(pB, vhat.v(0, 128, kc * 64, (kc + 1) * 64),
                      band.v(0, 128, d * 512, (d + 1) * 512), start=(j == 0), stop=False)
            pr.mm(pB, vsum.v(), C["ones512_at"](0), start=False, stop=True)
            rzs = atmp.v(p0, p0 + 1, 0, 512)
            pr.recip(rzs, pZ)
            pr.mm(vw(psR), C["ones_row_at"](p0), rzs)
            pr.copy(rzrep.v(), vw(psR))
            o1 = V(s0preT.buf, p0, p0 + 64, et * TOWN + qb * 512, et * TOWN + (qb + 1) * 512)
            pr.vtt(o1, pA, rzrep.v(p0, p0 + 64, 0, 512), ALU.mult)
            pr.vtt(o1, o1, pB, ALU.add)
    if dbg:
        s0pre_o = pr.dram_out("s0pre_o", [E, TOWN])
        for i in range(8):
            pr.dma("gpsimd", dview(s0pre_o, s0pre_o.h[i * 128:(i + 1) * 128, :]), s0preT.v(i))
        qT_o = pr.dram_out("qT_o", [64, TOWN])
        pr.dma("gpsimd", dview(qT_o, qT_o.h[:, :]), qT.v(0, 64, 0, 2048))
    ph_ln_stream(pr, "s0ln", lambda i, f0, f1: s0preT.v(i, f0, f1), 8, TOWN,
                 lambda i, f0, f1: s0preT.v(i, f0, f1), C,
                 gamma=lambda i: V(gbsb, 0, 128, i, i + 1),
                 beta=lambda i: V(gbsb, 0, 128, 8 + i, 9 + i))
    for i in range(8):
        pr.dma("gpsimd", dview(s0_d, s0_d.h[i * 128:(i + 1) * 128, :]), s0preT.v(i))

    xn1T = FMv(SBF, 0, 8, TOWN)
    ln2T = FMv(SBF, 16384, 8, TOWN)
    g0l = Sub(SBF, 32768, 128, 1024)
    ipb = Sub(SBF, 33792, 128, 2048)
    ipst = Sub(SBF, 35840, 128, 2048)
    x1s = Sub(SF, 0, 128, 1024)
    x0l = Sub(SF, 1024, 128, 1024)
    s0l = Sub(SF, 2048, 128, 1024)
    x1l = [Sub(SF, 3072 + k * 512, 128, 512) for k in range(4)]
    ipf = Sub(SF, 5120, 128, 2048)

    for i in range(8):
        for bk in range(4):
            f0, f1 = bk * 512, (bk + 1) * 512
            par = bk % 2
            xv = x0l.v(0, 128, par * 512, par * 512 + 512)
            sv = s0l.v(0, 128, par * 512, par * 512 + 512)
            gv = g0l.v(0, 128, par * 512, par * 512 + 512)
            pr.dma("sync", xv, dview(x0T_d, x0T_d.h[i * 128:(i + 1) * 128, f0:f1]))
            pr.dma("gpsimd", sv, dview(s0_d, s0_d.h[i * 128:(i + 1) * 128, f0:f1]))
            pr.dma("scalar", gv, dview(g0_d, g0_d.h[i * 128:(i + 1) * 128, f0:f1]))
            ov = x1s.v(0, 128, par * 512, par * 512 + 512)
            pr.vtt(ov, sv, gv, ALU.mult)
            pr.vtt(ov, ov, xv, ALU.add)
            pr.dma("sync", dview(x1_d, x1_d.h[i * 128:(i + 1) * 128, f0:f1]), ov)

    c2 = {}

    def getx1(i, f0, f1, _c=c2):
        key = (i, f0)
        if key in _c:
            return _c[key]
        dst = x1l[(i + (f0 // 512) * 3) % 4].v()
        pr.dma("sync", dst, dview(x1_d, x1_d.h[i * 128:(i + 1) * 128, f0:f1]))
        _c.clear()
        _c[key] = dst
        return dst

    ph_ln_stream(pr, "ln1", getx1, 8, TOWN, lambda i, f0, f1: xn1T.v(i, f0, f1), C)
    for i in range(8):
        pr.dma("gpsimd", dview(xn1_d, xn1_d.h[i * 128:(i + 1) * 128, :]), xn1T.v(i))
    ph_ln_stream(pr, "ln2", lambda i, f0, f1: xn1T.v(i, f0, f1), 8, TOWN,
                 lambda i, f0, f1: ln2T.v(i, f0, f1), C,
                 gamma=lambda i: V(lgbsb, 0, 128, i, i + 1),
                 beta=lambda i: V(lgbsb, 0, 128, 8 + i, 9 + i), in_bf=True)
    ps4 = [pr.pb[4], pr.pb[5], pr.pb[6], pr.pb[7]]
    islot = [0]
    for rt in range(32):
        s = islot[0] % 2
        islot[0] += 1
        d32 = ipf.v(0, 128, s * 1024, (s + 1) * 1024)
        dbf = ipb.v(0, 128, s * 1024, (s + 1) * 1024)
        pr.dma(("sync", "gpsimd")[rt % 2], d32,
               dview(ipT_d, rearr_block(ipT_d.h, rt * 128, (rt + 1) * 128)))
        pr.copy(dbf, d32)
        for ec in range(8):
            lw = V(dbf.buf, 0, 128, dbf.rect[2] + ec * 128, dbf.rect[2] + (ec + 1) * 128)
            for tb in range(4):
                pr.mm(vw(ps4[tb]), lw, ln2T.v(ec, tb * 512, (tb + 1) * 512),
                      start=(ec == 0), stop=(ec == 7))
        for tb in range(4):
            sv = ipst.v(0, 128, tb * 512, (tb + 1) * 512)
            if rt < 16:
                pr.act(sv, vw(ps4[tb]), AF.Copy)
            else:
                pr.act(sv, vw(ps4[tb]), AF.Silu)
        dst = xp_d if rt < 16 else zs_d
        r0 = (rt % 16) * 128
        pr.dma("scalar", dview(dst, dst.h[r0:r0 + 128, :]), ipst.v())
    pr.dma("gpsimd", dview(cc1i, cc1i.h[:, :]), dview(xp_d, xp_d.h[:, TOWN - 3:TOWN]))
    pr.cc_op("AllGather", cc1i, cc1o, PAIRS)

    cw = pr.sbuf("cwf", 128, 64)
    pr.dma("sync", vw(cw), dview(convw_d, convw_d.h[:, :].rearrange("(a p) k -> p a k", p=128)))
    cbuf = Sub(SBF, 32768, 128, 2052)
    ul = Sub(SBF, 34820, 128, 1024)
    dtp = Sub(SBF, 35844, 64, 2048)
    wdt = Sub(SBF, 37892, 64, 2048)
    wx = Sub(SBF, 39940, 128, 1536)
    facc = Sub(SF, 0, 128, 2048)
    xdbl = Sub(SF, 2048, 96, 2048)
    sp1 = Sub(SF, 4096, 128, 512)
    sp2 = Sub(SF, 4608, 128, 512)
    sp3 = Sub(SF, 5120, 128, 512)
    wd32 = Sub(SF, 5632, 64, 2048)
    wx32 = Sub(SF, 7680, 128, 1536)
    dtst = Sub(SF, 9216, 128, 2048)
    ust = Sub(SBF, 41476, 128, 2048)

    for dt_ in range(16):
        pr.dma("sync", cbuf.v(0, 128, 0, 3),
               dview(cc1o, cc1o.h[dt_ * 128:(dt_ + 1) * 128, :]))
        pr.vts(cbuf.v(0, 128, 0, 3), cbuf.v(0, 128, 0, 3), vw(maskb), None, ALU.mult)
        pr.dma("sync", cbuf.v(0, 128, 3, 3 + TOWN),
               dview(xp_d, xp_d.h[dt_ * 128:(dt_ + 1) * 128, :]))

        def win(k):
            return cbuf.v(0, 128, k, k + TOWN)

        pr.vts(facc.v(), win(0), V(cw, 0, 128, dt_ * 4, dt_ * 4 + 1), None, ALU.mult)
        for k in range(1, 4):
            pr.stt(facc.v(), win(k), V(cw, 0, 128, dt_ * 4 + k, dt_ * 4 + k + 1),
                   facc.v(), ALU.mult, ALU.add)
        pr.act(ust.v(), facc.v(), AF.Silu, bias=V(colsb, 0, 128, dt_, dt_ + 1))
        pr.dma("gpsimd", dview(cc3i, cc3i.h[dt_ * 128:(dt_ + 1) * 128, :]), ust.v())
    pr.dma("sync", wx32.v(), dview(xprojT_d, xprojT_d.h[:, :].rearrange("(a p) c -> p a c", p=128)))
    pr.copy(wx.v(0, 128, 0, 1536), wx32.v())
    psX = pr.pb[0]
    for tb in range(4):
        px = V(psX, 0, 96, 0, 512)
        for dc in range(16):
            uv = ul.v(0, 128, (dc % 2) * 512, (dc % 2) * 512 + 512)
            pr.dma("gpsimd", uv, dview(cc3i, cc3i.h[dc * 128:(dc + 1) * 128,
                                                    tb * 512:(tb + 1) * 512]))
            pr.mm(px, wx.v(0, 128, dc * 96, (dc + 1) * 96), uv,
                  start=(dc == 0), stop=(dc == 15))
        pr.copy(xdbl.v(0, 96, tb * 512, (tb + 1) * 512), px)
    pr.dma("gpsimd", dview(cc4i, cc4i.h[:, :]), xdbl.v(64, 96, 0, TOWN))
    pr.copy(dtp.v(0, 64, 0, 2048), xdbl.v(0, 64, 0, TOWN))
    pr.dma("sync", wd32.v(0, 64, 0, 2048), vw(dtprojT_d))
    pr.copy(wdt.v(0, 64, 0, 2048), wd32.v(0, 64, 0, 2048))
    psD = pr.pb[1]
    for dt_ in range(16):
        for tb in range(4):
            pd = vw(psD)
            pr.mm(pd, wdt.v(0, 64, dt_ * 128, (dt_ + 1) * 128),
                  dtp.v(0, 64, tb * 512, (tb + 1) * 512))
            pr.act(sp1.v(), pd, AF.Identity, bias=V(colsb, 0, 128, 16 + dt_, 17 + dt_))
            pr.act(sp2.v(), sp1.v(), AF.Abs)
            pr.act(sp2.v(), sp2.v(), AF.Exp, scale=-1.0)
            pr.act(sp2.v(), sp2.v(), AF.Ln, bias=1.0)
            pr.act(sp3.v(), sp1.v(), AF.Relu)
            pr.vtt(dtst.v(0, 128, tb * 512, (tb + 1) * 512), sp3.v(), sp2.v(), ALU.add)
        pr.dma("scalar", dview(cc2i, cc2i.h[dt_ * 128:(dt_ + 1) * 128, :]), dtst.v())
    pr.cc_op("AllGather", cc2i, cc2o, PAIRS)
    pr.cc_op("AllGather", cc3i, cc3o, PAIRS)
    pr.cc_op("AllGather", cc4i, cc4o, PAIRS)

    rw32 = Sub(SF, 0, 128, 64)
    rwbf = Sub(SBF, 43008, 128, 2048)
    pr.dma("sync", rw32.v(), dview(rw_d, rw_d.h[:, :].rearrange("(a p) c -> p a c", p=128)))
    pr.copy(rwbf.v(0, 128, 0, 64), rw32.v())
    rbsb = Sub(SF, 64, 8, 1)
    pr.dma("sync", rbsb.v(0, 8, 0, 1), vw(rb_d))
    lgT = Sub(SF, 2048, 8, 2048)
    ps0 = pr.pb[0]
    for tb in range(4):
        p8 = V(ps0, 0, 8, 0, 512)
        for ec in range(8):
            pr.mm(p8, rwbf.v(0, 128, ec * 8, (ec + 1) * 8),
                  xn1T.v(ec, tb * 512, (tb + 1) * 512), start=(ec == 0), stop=(ec == 7))
        pr.act(lgT.v(0, 8, tb * 512, (tb + 1) * 512), p8, AF.Identity,
               bias=rbsb.v(0, 8, 0, 1))
    gbuf = pr.sbuf("gbuff", 128, 128)
    gTb = Sub(SBF, 43008, 8, 2048)
    lt = Sub(SF, 128, 128, 8)
    m1 = Sub(SF, 640, 128, 4)
    e1 = Sub(SF, 1024, 128, 8)
    ps1_, ps2_ = pr.pb[1], pr.pb[2]
    for tt in range(16):
        pt = V(ps1_, 0, 128, 0, 8)
        pr.transp(pt, lgT.v(0, 8, tt * 128, (tt + 1) * 128), C["identN"](8))
        pr.copy(lt.v(), pt)
        pr.reduce(m1.v(0, 128, 0, 1), lt.v(), ALU.max)
        pr.vts(e1.v(), lt.v(), m1.v(0, 128, 0, 1), None, ALU.is_equal)
        pr.stt(e1.v(), e1.v(), -1e30, lt.v(), ALU.mult, ALU.add)
        pr.reduce(m1.v(0, 128, 1, 2), e1.v(), ALU.max)
        pr.vts(m1.v(0, 128, 2, 3), m1.v(0, 128, 0, 1), -1.0, None, ALU.mult)
        pr.act(e1.v(), lt.v(), AF.Exp, bias=m1.v(0, 128, 2, 3))
        gv = V(gbuf, 0, 128, tt * 8, (tt + 1) * 8)
        pr.vts(gv, lt.v(), m1.v(0, 128, 1, 2), None, ALU.is_ge)
        pr.vtt(gv, gv, e1.v(), ALU.mult)
        pr.reduce(m1.v(0, 128, 3, 4), gv, ALU.add)
        pr.recip(m1.v(0, 128, 3, 4), m1.v(0, 128, 3, 4))
        pr.vts(gv, gv, m1.v(0, 128, 3, 4), None, ALU.mult)
        pg = V(ps2_, 0, 8, 0, 128)
        pr.transp(pg, gv, C["ident"])
        pr.copy(V(gTb.buf, 0, 8, gTb.off + tt * 128, gTb.off + (tt + 1) * 128), pg)
    macc = Sub(SF, 4736, 128, 16384)
    b232 = Sub(SF, 0, 8, 1024)
    b2bf = Sub(SBF, 46080, 8, 1024)
    pr.dma("sync", b232.v(0, 8, 0, 1024), vw(b2_d))
    pr.copy(b2bf.v(0, 8, 0, 1024), b232.v(0, 8, 0, 1024))
    ps3_ = pr.pb[3]
    for tt in range(16):
        for cb in range(2):
            pb_ = vw(ps3_)
            pr.mm(pb_, V(gTb.buf, 0, 8, gTb.off + tt * 128, gTb.off + (tt + 1) * 128),
                  b2bf.v(0, 8, cb * 512, (cb + 1) * 512))
            pr.copy(macc.v(0, 128, tt * 1024 + cb * 512, tt * 1024 + (cb + 1) * 512), pb_)
    asb = pr.sbuf("asbf", 128, 128)
    pr.dma("sync", vw(asb), vw(alog_d))
    pr.act(vw(asb), vw(asb), AF.Exp)
    pr.vts(vw(asb), vw(asb), -1.0, None, ALU.mult)
    dsb = pr.sbuf("dsbf", 128, 8)
    pr.dma("sync", vw(dsb), vw(dD_d))
    dts = [Sub(SF, 0, 128, 512), Sub(SF, 512, 128, 512)]
    yac = Sub(SF, 1024, 128, 512)
    hbf = Sub(SF, 1536, 128, 512)
    btile = Sub(SBF, 47104, 16, 512)
    ctile = Sub(SBF, 47616, 16, 512)
    da = Sub(SF, 2048, 128, 512)
    db = Sub(SF, 2560, 128, 512)
    carry = Sub(SF, 3072, 128, 16)
    w1f = Sub(SF, 3200, 128, 1024)
    w2rf = Sub(SF, 4224, 128, 512)
    usl = [Sub(SBF, 16384, 128, 512), Sub(SBF, 16896, 128, 512)]
    dub = Sub(SBF, 17408, 128, 512)
    yst = Sub(SBF, 17920, 128, 512)
    hq = FMv(SBF, 18432, 8, 2048)
    w2b = Sub(SBF, 34816, 128, 8192)
    w1b = Sub(SBF, 45056, 128, 1024)
    b1sb = pr.sbuf("b1sbf", 128, 32)
    psBr, psCr = pr.pb[0], pr.pb[1]
    psY = [pr.pb[2], pr.pb[3]]
    psH = [pr.pb[4], pr.pb[5], pr.pb[6], pr.pb[7]]

    def scan_unit(dt_, tc):
        half = tc // 4
        cl0 = (tc % 4) * 512
        sl_ = tc % 2
        r0 = half * 2048 + dt_ * 128
        ca, cb = da.v(), db.v()
        pr.dma("sync", ca, dview(cc2o, cc2o.h[r0:r0 + 128, cl0:cl0 + 512]))
        pr.dma("sync", cb, dview(cc2o, cc2o.h[r0 + 1024:r0 + 1152, cl0:cl0 + 512]))
        pr.act(hbf.v(), ca, AF.Identity, scale=V(mask0, 0, 128, 0, 1))
        pr.stt(dts[sl_].v(), cb, V(maskb, 0, 128, 0, 1), hbf.v(), ALU.mult, ALU.add)
        ua, ub = usl[1 - sl_].v(), yst.v()
        pr.dma("gpsimd", ua, dview(cc3o, cc3o.h[r0:r0 + 128, cl0:cl0 + 512]))
        pr.dma("gpsimd", ub, dview(cc3o, cc3o.h[r0 + 1024:r0 + 1152, cl0:cl0 + 512]))
        pr.act(hbf.v(), ua, AF.Identity, scale=V(mask0, 0, 128, 0, 1))
        pr.stt(usl[sl_].v(), ub, V(maskb, 0, 128, 0, 1), hbf.v(), ALU.mult, ALU.add)
        pr.vtt(dub.v(), dts[sl_].v(), usl[sl_].v(), ALU.mult)
        pr.dma("gpsimd", btile.v(0, 16, 0, 512),
               dview(cc4o, cc4o.h[half * 32:half * 32 + 16, cl0:cl0 + 512]))
        pr.dma("gpsimd", ctile.v(0, 16, 0, 512),
               dview(cc4o, cc4o.h[half * 32 + 16:half * 32 + 32, cl0:cl0 + 512]))
        for n in range(NST):
            acol = V(asb, 0, 128, dt_ * NST + n, dt_ * NST + n + 1)
            pB = vw(psBr)
            pC = vw(psCr)
            pr.mm(pB, C["sel_bf"](n), btile.v(0, 16, 0, 512))
            pr.mm(pC, C["sel_bf"](n), ctile.v(0, 16, 0, 512))
            pr.act(da.v(), dts[sl_].v(), AF.Exp, scale=acol)
            pr.vtt(db.v(), dub.v(), pB, ALU.mult)
            ini = 0.0 if tc == 0 else V(carry.buf, 0, 128, carry.off + n, carry.off + n + 1)
            pr.scan_op(hbf.v(), da.v(), db.v(), ini)
            pr.copy(V(carry.buf, 0, 128, carry.off + n, carry.off + n + 1),
                    hbf.v(0, 128, 511, 512), engine="scalar")
            if n == 0:
                pr.vtt(yac.v(), hbf.v(), pC, ALU.mult)
            else:
                pr.vtt(da.v(), hbf.v(), pC, ALU.mult)
                pr.vtt(yac.v(), yac.v(), da.v(), ALU.add, engine="gpsimd")
        dcol = V(dsb, 0, 128, dt_, dt_ + 1)
        pr.stt(yst.v(), usl[sl_].v(), dcol, yac.v(), ALU.mult, ALU.add)
        r2 = dt_ * 128 + (1 - half) * 0 + half * 1024
        pr.dma("scalar", dview(cc5i, cc5i.h[half * 1024 + dt_ * 128:half * 1024 + (dt_ + 1) * 128,
                                            cl0:cl0 + 512]), yst.v())

    def moe_unit(e, hc):
        if hc == 0:
            pr.dma("scalar", vw(b1sb), dview(b1r_d, b1r_d.h[e * 128:(e + 1) * 128, :]))
        for htl in range(8):
            ht = hc * 8 + htl
            src = w1_d.h[e * E:(e + 1) * E, ht * 128:(ht + 1) * 128].rearrange(
                "(a p) c -> p a c", p=128)
            s = htl % 2
            d32 = w1f.v(0, 128, s * 512, s * 512 + 512)
            dbf = w1b.v(0, 128, 0, 1024)
            pr.dma(("sync", "gpsimd")[htl % 2], w1f.v(0, 128, 0, 1024) if False else d32,
                   dview(w1_d, src)) if False else None
            pr.dma(("sync", "gpsimd")[htl % 2], w1f.v(0, 128, 0, 1024), dview(w1_d, src))
            pr.copy(w1b.v(0, 128, 0, 1024), w1f.v(0, 128, 0, 1024))
            for ec in range(8):
                lw = V(w1b.buf, 0, 128, w1b.off + ec * 128, w1b.off + (ec + 1) * 128)
                for tb in range(4):
                    pr.mm(vw(psH[tb]), lw, xn1T.v(ec, tb * 512, (tb + 1) * 512),
                          start=(ec == 0), stop=(ec == 7))
            for tb in range(4):
                pr.act(hq.v(htl, tb * 512, (tb + 1) * 512), vw(psH[tb]), AF.Relu,
                       bias=V(b1sb, 0, 128, ht, ht + 1))
            for wh in range(2):
                pr.dma("scalar", w2rf.v(),
                       dview(w2_d, w2_d.h[e * HIDM + ht * 128:e * HIDM + (ht + 1) * 128,
                                          wh * 512:(wh + 1) * 512]))
                pr.copy(w2b.v(0, 128, htl * 1024 + wh * 512, htl * 1024 + (wh + 1) * 512),
                        w2rf.v())
        for tt in range(16):
            for cb in range(2):
                pY = vw(psY[cb])
                for htl in range(8):
                    pr.mm(pY, hq.v(htl, tt * 128, (tt + 1) * 128),
                          w2b.v(0, 128, htl * 1024 + cb * 512, htl * 1024 + (cb + 1) * 512),
                          start=(htl == 0), stop=(htl == 7))
                mv = macc.v(0, 128, tt * 1024 + cb * 512, tt * 1024 + (cb + 1) * 512)
                pr.stt(mv, pY, V(gbuf, 0, 128, tt * 8 + e, tt * 8 + e + 1), mv,
                       ALU.mult, ALU.add)

    su = [(d_, t_) for d_ in range(8) for t_ in range(8)]
    mu = [(e_, h_) for e_ in range(NEXP) for h_ in range(4)]
    si = 0
    for k, (e_, h_) in enumerate(mu):
        for _ in range(2):
            if si < len(su):
                scan_unit(*su[si])
                si += 1
        moe_unit(e_, h_)
    while si < len(su):
        scan_unit(*su[si])
        si += 1
    pr.cc_op("AllGather", cc5i, cc5o, PAIRS)
    if dbg:
        ydbg = pr.dram_out("ydbg", [2 * DIN, TOWN], BF16)
        pr.dma("gpsimd", dview(ydbg, ydbg.h[:, :]), dview(cc5o, cc5o.h[:, :]))

    ygT = FMv(SBF, 0, 16, TOWN)
    mgT = FMv(SBF, 32768, 8, TOWN)
    sgT = FMv(SBF, 0, 8, TOWN)
    fwb = Sub(SBF, 49152, 128, 2048)
    fwf = Sub(SF, 0, 128, 2048)
    yl = Sub(SBF, 16384 + 0, 128, 0, ) if False else None
    y0l = Sub(SBF, 49152, 128, 0) if False else None
    ylr = Sub(SBF, 47104, 128, 1024)
    zlr = Sub(SBF, 48128, 128, 1024)
    ftmp = Sub(SF, 2048, 128, 512)
    fstage = Sub(SF, 2560, 128, 512)
    xnl = Sub(SBF, 16384, 128, 1024)

    ybl = Sub(SF, 4096, 128, 512)
    for dc in range(16):
        yva = ylr.v(0, 128, 0, 512)
        yvb = ylr.v(0, 128, 512, 1024)
        zv = zlr.v(0, 128, (dc % 2) * 512, (dc % 2) * 512 + 512)
        hd = dc // 8
        rr = hd * 2048 + (dc % 8) * 128
        for tb in range(4):
            pr.dma("sync", yva, dview(cc5o, cc5o.h[rr:rr + 128, tb * 512:(tb + 1) * 512]))
            pr.dma("sync", yvb, dview(cc5o, cc5o.h[rr + 1024:rr + 1152, tb * 512:(tb + 1) * 512]))
            pr.act(ybl.v(), yva, AF.Identity, scale=V(mask0, 0, 128, 0, 1))
            pr.stt(yva, yvb, V(maskb, 0, 128, 0, 1), ybl.v(), ALU.mult, ALU.add)
            pr.dma("gpsimd", zv, dview(zs_d, zs_d.h[dc * 128:(dc + 1) * 128,
                                                    tb * 512:(tb + 1) * 512]))
            pr.vtt(ygT.v(dc, tb * 512, (tb + 1) * 512), yva, zv, ALU.mult)
    fslot = [0]

    def ffetch(dbuf, src_ap, q):
        s = fslot[0] % 2
        fslot[0] += 1
        d32 = fwf.v(0, 128, s * 1024, (s + 1) * 1024)
        dbf = fwb.v(0, 128, s * 1024, (s + 1) * 1024)
        pr.dma(q, d32, dview(dbuf, src_ap))
        pr.copy(dbf, d32)
        return dbf

    ps4b = [pr.pb[4], pr.pb[5], pr.pb[6], pr.pb[7]]
    for et in range(8):
        for hf in range(2):
            wb = ffetch(opT_d, opT_d.h[hf * 1024:(hf + 1) * 1024, et * 128:(et + 1) * 128]
                        .rearrange("(a p) c -> p a c", p=128), ("sync", "gpsimd")[hf])
            for dcl in range(8):
                dc = hf * 8 + dcl
                lw = V(wb.buf, 0, 128, wb.rect[2] + dcl * 128, wb.rect[2] + (dcl + 1) * 128)
                for tb in range(4):
                    pr.mm(vw(ps4b[tb]), lw, ygT.v(dc, tb * 512, (tb + 1) * 512),
                          start=(dc == 0), stop=(dc == 15))
        for tb in range(4):
            pr.act(mgT.v(et, tb * 512, (tb + 1) * 512), vw(ps4b[tb]), AF.Gelu)
    for half in range(2):
        for rt in range(8):
            wb = ffetch(gwT_d, rearr_block(gwT_d.h, (1 - half) * 1024 + rt * 128,
                                           (1 - half) * 1024 + (rt + 1) * 128),
                        ("sync", "gpsimd")[rt % 2])
            for ec in range(8):
                lw = V(wb.buf, 0, 128, wb.rect[2] + ec * 128, wb.rect[2] + (ec + 1) * 128)
                for tb in range(4):
                    pr.mm(vw(ps4b[tb]), lw, mgT.v(ec, tb * 512, (tb + 1) * 512),
                          start=(ec == 0), stop=(ec == 7))
            if half == 0:
                for tb in range(4):
                    pr.act(sgT.v(rt, tb * 512, (tb + 1) * 512), vw(ps4b[tb]), AF.Sigmoid,
                           bias=V(gbias, 0, 128, 8 + rt, 9 + rt))
            else:
                for tb in range(4):
                    pr.act(ftmp.v(), vw(ps4b[tb]), AF.Identity,
                           bias=V(gbias, 0, 128, rt, rt + 1))
                    xv = xnl.v(0, 128, (tb % 2) * 512, (tb % 2) * 512 + 512)
                    pr.dma("sync", xv, dview(xn1_d, xn1_d.h[rt * 128:(rt + 1) * 128,
                                                            tb * 512:(tb + 1) * 512]))
                    sv = fstage.v()
                    pr.vtt(sv, ftmp.v(), sgT.v(rt, tb * 512, (tb + 1) * 512), ALU.mult)
                    pr.vtt(sv, sv, xv, ALU.add)
                    pr.dma("scalar", dview(s1_d, s1_d.h[rt * 128:(rt + 1) * 128,
                                                        tb * 512:(tb + 1) * 512]), sv)

    g1t = Sub(SF, 0, 128, 1024)
    xl_ = Sub(SF, 1024, 128, 128)
    sl_ = Sub(SF, 1152, 128, 128)
    ost = Sub(SF, 1280, 128, 128)
    psG = pr.pb[0]
    for tt in range(16):
        pr.act(g1t.v(), macc.v(0, 128, tt * 1024, (tt + 1) * 1024), AF.Silu)
        for ec in range(8):
            pg = V(psG, 0, 128, 0, 128)
            pr.transp(pg, g1t.v(0, 128, ec * 128, (ec + 1) * 128), C["ident"])
            pr.dma("sync", xl_.v(), dview(x1_d, x1_d.h[ec * 128:(ec + 1) * 128,
                                                       tt * 128:(tt + 1) * 128]))
            pr.dma("gpsimd", sl_.v(), dview(s1_d, s1_d.h[ec * 128:(ec + 1) * 128,
                                                         tt * 128:(tt + 1) * 128]))
            pr.vtt(ost.v(), sl_.v(), pg, ALU.mult)
            pr.vtt(ost.v(), ost.v(), xl_.v(), ALU.add)
            pr.dma("scalar", dview(outT_o, outT_o.h[ec * 128:(ec + 1) * 128,
                                                    tt * 128:(tt + 1) * 128]), ost.v())
    return pr


def kernel_fused(inputs, dbg=False, _res=[]):
    P = _prep_all(inputs)
    cores = list(range(8))
    prF = build_fused(dbg)
    masks = [np.full((128, 1), float(c % 2), np.float32) for c in cores]
    masks0 = [np.full((128, 1), float(1 - c % 2), np.float32) for c in cores]
    res = run_bass_kernel_spmd(prF.emit(), [dict(
        consts=P["consts"], x0T=P["x0T_rot"][c], wgT=P["wgT"], wuT=P["wuT"], wdT=P["wdT"],
        wqT=P["wqT"], wkT=P["wkT"], wvT=P["wvT"], W2sh=P["W2sh"][c], cvec=P["cvec"],
        gb=P["gb"], lgb=P["lgb"], ipT=P["ipT"], convw=P["convw"], cols=P["cols"],
        xprojT=P["xprojT"], dtprojT=P["dtprojT"], alog=P["alog"][c], dD=P["dD"][c],
        maskbf=masks[c], mask0=masks0[c], opT=P["opT"], gwT=P["gwT"], gbias=P["gbias"], rw=P["rw"],
        rb=P["rb"], w1=P["w1"], b1r=P["b1r"], w2=P["w2"], b2=P["b2"])
        for c in cores], core_ids=cores)
    _res.clear()
    _res.append(res)
    out = np.empty((B, S, E), np.float32)
    for c in cores:
        b, th = c // 2, c % 2
        out[b, th * TOWN:(th + 1) * TOWN, :] = res.results[c]["outT_o"].T
    return out


# version 07
# speedup vs baseline: 1.3050x; 1.1309x over previous
import numpy as np
import ml_dtypes
import concourse.bass as bass
import concourse.mybir as mybir
from concourse.bass_utils import run_bass_kernel_spmd
from contextlib import ExitStack

F32 = mybir.dt.float32
BF16 = mybir.dt.bfloat16
AF = mybir.ActivationFunctionType
ALU = mybir.AluOpType
AXX = mybir.AxisListType.X
BF16NP = ml_dtypes.bfloat16

B, S, E = 4, 4096, 1024
TOWN = 2048
H, DH = 16, 64
HID0 = 3072
DIN, NST, DTR = 2048, 16, 64
NEXP, HIDM = 8, 4096

_DMAQ = {"sync": "qsy", "gpsimd": "qgp", "scalar": "qsc"}


class Op:
    __slots__ = ("stream", "method", "kwargs", "deps", "kind", "ccsem", "snapshot")

    def __init__(self, stream, method, kwargs, deps, kind, ccsem=None):
        self.stream, self.method, self.kwargs = stream, method, kwargs
        self.deps, self.kind, self.ccsem = deps, kind, ccsem


class Buf:
    def __init__(self, pr, h, P, Fr, space):
        self.pr, self.h, self.P, self.F, self.space = pr, h, P, Fr, space
        self.regions = {}

    def _conflicts(self, rect):
        p0, p1, f0, f1 = rect
        return [(r, st) for r, st in self.regions.items()
                if r[0] < p1 and p0 < r[1] and r[2] < f1 and f0 < r[3]]

    def dep_read(self, rect):
        deps = set()
        for _, st in self._conflicts(rect):
            if st[0] is not None:
                deps.add(st[0])
        return deps

    def dep_write(self, rect):
        deps = set()
        for _, st in self._conflicts(rect):
            if st[0] is not None:
                deps.add(st[0])
            deps.update(st[1].items())
        return deps

    def note_read(self, rect, tok):
        st = self.regions.get(rect)
        if st is None:
            st = self.regions.setdefault(rect, [None, {}])
        if st[1].get(tok[0], 0) < tok[1]:
            st[1][tok[0]] = tok[1]

    def note_write(self, rect, tok):
        dead = [r for r in self.regions
                if rect[0] <= r[0] and r[1] <= rect[1] and rect[2] <= r[2] and r[3] <= rect[3]]
        for r in dead:
            del self.regions[r]
        self.regions[rect] = [tok, {}]


class V:
    __slots__ = ("buf", "rect", "ap")

    def __init__(self, buf, p0, p1, f0, f1, ap=None):
        self.buf = buf
        self.rect = (p0, p1, f0, f1)
        self.ap = buf.h[p0:p1, f0:f1] if ap is None else ap


def vw(buf, p0=0, p1=None, f0=0, f1=None):
    return V(buf, p0, buf.P if p1 is None else p1, f0, buf.F if f1 is None else f1)


class FM:

    def __init__(self, pr, name, n, T, dt=F32):
        self.buf = pr.sbuf(name, 128, n * T, dt)
        self.n, self.T = n, T
        self.off = 0

    def v(self, i, f0=0, f1=None):
        f1 = self.T if f1 is None else f1
        return V(self.buf, 0, 128, self.off + i * self.T + f0, self.off + i * self.T + f1)


class FMv:

    def __init__(self, buf, off, n, T):
        self.buf, self.off, self.n, self.T = buf, off, n, T

    def v(self, i, f0=0, f1=None):
        f1 = self.T if f1 is None else f1
        return V(self.buf, 0, 128, self.off + i * self.T + f0, self.off + i * self.T + f1)


class Sub:

    def __init__(self, buf, off, P_, width):
        self.buf, self.off, self.P, self.width = buf, off, P_, width

    def v(self, p0=0, p1=None, f0=0, f1=None):
        return V(self.buf, p0, self.P if p1 is None else p1,
                 self.off + f0, self.off + (self.width if f1 is None else f1))


class Prog:
    def __init__(self):
        self.nc = bass.Bass("TRN2", target_bir_lowering=False, dynamic_dma_scratch_size=16384)
        self.ops = {}
        self.seq = {}
        self.n_cc = 0
        self._uid = 0
        self.pb = [Buf(self, self.nc.alloc_psum_tensor("pb%d" % i, [128, 512], F32),
                       128, 512, "psum") for i in range(8)]

    def uid(self):
        self._uid += 1
        return self._uid

    def dram_in(self, name, shape, dt=F32):
        h = self.nc.dram_tensor(name, list(shape), dt, kind="ExternalInput")
        return Buf(self, h, shape[0], int(np.prod(shape[1:])), "dram")

    def dram_out(self, name, shape, dt=F32):
        h = self.nc.dram_tensor(name, list(shape), dt, kind="ExternalOutput")
        return Buf(self, h, shape[0], int(np.prod(shape[1:])), "dram")

    def dram_tmp(self, name, shape, dt=F32):
        h = self.nc.dram_tensor(name, list(shape), dt, kind="Internal")
        return Buf(self, h, shape[0], int(np.prod(shape[1:])), "dram")

    def sbuf(self, name, P, Fr, dt=F32):
        return Buf(self, self.nc.alloc_sbuf_tensor(name, [P, Fr], dt), P, Fr, "sbuf")

    def psum(self, name, P, Fr, dt=F32):
        return Buf(self, self.nc.alloc_psum_tensor(name, [P, Fr], dt), P, Fr, "psum")

    def _rec(self, stream, issue, method, kwargs, reads, writes, kind, ccsem=None):
        deps = set()
        for x in reads:
            deps |= x.buf.dep_read(x.rect)
        for x in writes:
            deps |= x.buf.dep_write(x.rect)
        self.seq[stream] = self.seq.get(stream, 0) + 1
        tok = (stream, self.seq[stream])
        for x in reads:
            x.buf.note_read(x.rect, tok)
        for x in writes:
            x.buf.note_write(x.rect, tok)
        deps.discard(tok)
        if kind == "dma" and self.seq[stream] > 1:
            deps.add((stream, self.seq[stream] - 1))
        op = Op(stream, method, kwargs, deps, kind, ccsem)
        op.snapshot = {k_: v_ for k_, v_ in self.seq.items() if k_ != stream}
        self.ops.setdefault(issue, []).append(op)
        return tok

    def comp(self, engine, method, kwargs, reads=(), writes=()):
        return self._rec(engine, engine, method, kwargs, reads, writes, "comp")

    def dma(self, q, dst, src):
        qb = _DMAQ[q]
        k = self.seq.get("_cnt_" + qb, 0)
        self.seq["_cnt_" + qb] = k + 1
        return self._rec("%s_%d" % (qb, k % 4), q, "dma_start",
                         dict(out=dst.ap, in_=src.ap), [src], [dst], "dma")

    def cc_op(self, kind, src, dst, groups):
        self.n_cc += 1
        ccs = "cc%d" % self.n_cc
        return self._rec(ccs, "gpsimd", "collective_compute",
                         dict(kind=kind, op=ALU.bypass, replica_groups=groups,
                              ins=[src.h.ap().opt()], outs=[dst.h.ap().opt()]),
                         [vw(src)], [vw(dst)], "cc", ccsem=ccs)

    def emit(self, paranoid=False):
        nc = self.nc
        streams = sorted(s for s in self.seq if not s.startswith("_cnt_"))
        final_dma = {q: 16 * self.seq[q] for q in streams if q.startswith("q")}
        with ExitStack() as stack:
            sems = {s: stack.enter_context(nc.semaphore("s_" + s)) for s in streams}
            block = stack.enter_context(nc.Block())

            def mk(eng_name):
                def body(eng):
                    guar = {}
                    for op in self.ops.get(eng_name, ()):
                        deps = set(op.snapshot.items()) if paranoid == 1 else op.deps
                        for ds, dq in sorted(deps):
                            if ds == op.stream and ds == "tensor":
                                continue
                            thr = dq * (16 if ds.startswith("q") else 1)
                            if guar.get(ds, 0) >= thr:
                                continue
                            eng.wait_ge(sems[ds], thr)
                            guar[ds] = thr
                        if op.kind == "cc":
                            kw = op.kwargs
                            eng.collective_compute(
                                kw["kind"], kw["op"], replica_groups=kw["replica_groups"],
                                ins=kw["ins"], outs=kw["outs"]).then_inc(sems[op.ccsem])
                        else:
                            getattr(eng, op.method)(**op.kwargs).then_inc(
                                sems[op.stream], 16 if op.kind == "dma" else 1)
                    q = _DMAQ.get(eng_name)
                    if q is not None:
                        for cls in range(4):
                            qq = "%s_%d" % (q, cls)
                            if qq in final_dma:
                                eng.wait_ge(sems[qq], final_dma[qq])
                return body

            for eng_name in ("sync", "tensor", "vector", "scalar", "gpsimd"):
                if eng_name in self.ops:
                    getattr(block, eng_name)(mk(eng_name))
        return nc

    def mm(self, out, lhsT, rhs, start=True, stop=True):
        return self.comp("tensor", "matmul",
                         dict(out=out.ap, lhsT=lhsT.ap, rhs=rhs.ap, start=start, stop=stop),
                         [lhsT, rhs], [out])

    def transp(self, out, inp, ident):
        return self.comp("tensor", "transpose",
                         dict(out=out.ap, in_=inp.ap, identity=ident.ap),
                         [inp, ident], [out])

    def act(self, out, inp, func, bias=0.0, scale=1.0, accum=None):
        kw = dict(out=out.ap, in_=inp.ap, func=func)
        reads = [inp]
        for k, val in (("bias", bias), ("scale", scale)):
            if isinstance(val, V):
                kw[k] = val.ap
                reads.append(val)
            else:
                kw[k] = float(val)
        writes = [out]
        if accum is not None:
            kw["accum_out"] = accum.ap
            writes.append(accum)
        return self.comp("scalar", "activation", kw, reads, writes)

    def vtt(self, out, a, b, op, engine="vector"):
        return self.comp(engine, "tensor_tensor",
                         dict(out=out.ap, in0=a.ap, in1=b.ap, op=op), [a, b], [out])

    def vts(self, out, a, s1, s2, op0, op1=None, engine="vector"):
        kw = dict(out=out.ap, in0=a.ap,
                  scalar1=(s1.ap if isinstance(s1, V) else float(s1)),
                  scalar2=(s2.ap if isinstance(s2, V) else (None if s2 is None else float(s2))),
                  op0=op0)
        if op1 is not None:
            kw["op1"] = op1
        reads = [a] + [x for x in (s1, s2) if isinstance(x, V)]
        return self.comp(engine, "tensor_scalar", kw, reads, [out])

    def stt(self, out, in0, scalar, in1, op0, op1, engine="vector"):
        reads = [in0, in1] + ([scalar] if isinstance(scalar, V) else [])
        return self.comp(engine, "scalar_tensor_tensor",
                         dict(out=out.ap, in0=in0.ap,
                              scalar=(scalar.ap if isinstance(scalar, V) else float(scalar)),
                              in1=in1.ap, op0=op0, op1=op1), reads, [out])

    def scan_op(self, out, d0, d1, initial):
        reads = [d0, d1] + ([initial] if isinstance(initial, V) else [])
        return self.comp("vector", "tensor_tensor_scan",
                         dict(out=out.ap, data0=d0.ap, data1=d1.ap,
                              initial=(initial.ap if isinstance(initial, V) else float(initial)),
                              op0=ALU.mult, op1=ALU.add), reads, [out])

    def copy(self, out, inp, engine="vector"):
        meth = "copy" if engine == "scalar" else "tensor_copy"
        return self.comp(engine, meth, dict(out=out.ap, in_=inp.ap), [inp], [out])

    def reduce(self, out, inp, op):
        return self.comp("vector", "tensor_reduce",
                         dict(out=out.ap, in_=inp.ap, axis=AXX, op=op), [inp], [out])

    def recip(self, out, inp):
        return self.comp("vector", "reciprocal", dict(out=out.ap, in_=inp.ap), [inp], [out])

    def memset(self, out, val, engine="vector"):
        return self.comp(engine, "memset", dict(ap=out.ap, constant=val), [], [out])


def dview(dbuf, sl_ap):
    return V(dbuf, 0, dbuf.P, 0, dbuf.F, ap=sl_ap)


def rearr_block(dh, c0, c1, p=128):
    return dh[:, c0:c1].rearrange("(a p) c -> p a c", p=p)



def ph_consts(pr, bf=False):
    cst = pr.dram_in("consts", [128, 2696], F32)
    sb = pr.sbuf("cst", 128, 648)
    pr.dma("sync", vw(sb), dview(Buf(pr, cst.h, 128, 2696, "dram"), cst.h[:, 0:648]))
    d = dict(ones_col=V(sb, 0, 128, 0, 1), ones_row=V(sb, 0, 1, 8, 136),
             ones_row_at=lambda p: V(sb, p, p + 1, 8, 136),
             ones512_at=lambda p: V(sb, p, p + 1, 8, 520),
             eps_at=lambda p: V(sb, p, p + 1, 1, 2),
             ident=V(sb, 0, 128, 520, 648),
             identN=lambda n_: V(sb, 0, n_, 520, 520 + n_), sb=sb)
    if bf:
        sbb = pr.sbuf("cstbf", 128, 8, BF16)
        pr.copy(vw(sbb), V(sb, 0, 128, 0, 8))
        d["ones_col_bf"] = V(sbb, 0, 128, 0, 1)
        selb = pr.sbuf("selbf", 16, 2048, BF16)
        pr.dma("gpsimd", vw(selb), dview(cst, cst.h[0:16, 648:2696]))
        d["sel_bf"] = lambda n_: V(selb, 0, 16, n_ * 128, (n_ + 1) * 128)
        d["sbb"] = sbb
    return d


def ph_ln_stream(pr, tag, getx, n, T, sink, C, banks=(0, 1, 2, 3), gamma=None,
                 beta=None, eps=1e-5, dbg=None, in_bf=False):
    u = pr.uid()
    if getattr(pr, "ln_scratch", None) is None:
        pr.ln_scratch = (pr.sbuf("lnsq%d" % u, 128, 512),
                         pr.sbuf("lnst%d" % u, 128, 2048),
                         pr.sbuf("lntm%d" % u, 128, 512))
    sq, stat, tmp = pr.ln_scratch
    ps_s = V(pr.pb[banks[0]], 0, 1, 0, 512)
    ps_q = V(pr.pb[banks[1]], 0, 1, 0, 512)
    ps_r1 = pr.pb[banks[2]]
    ps_r2 = pr.pb[banks[3]]

    def row(k):
        return V(stat, 0, 1, k * 512, (k + 1) * 512)

    for bi in range(T // 512):
        f0, f1 = bi * 512, (bi + 1) * 512
        oc = C["ones_col_bf"] if in_bf else C["ones_col"]
        for i in range(n):
            xv = getx(i, f0, f1)
            pr.mm(ps_s, oc, xv, start=(i == 0), stop=(i == n - 1))
            pr.act(vw(sq), xv, AF.Square)
            pr.mm(ps_q, C["ones_col"], vw(sq), start=(i == 0), stop=(i == n - 1))
        inv = 1.0 / (128 * n)
        pr.vts(row(0), ps_s, inv, None, ALU.mult)
        pr.vts(row(1), ps_q, inv, None, ALU.mult)
        pr.vtt(row(2), row(0), row(0), ALU.mult)
        pr.vtt(row(1), row(1), row(2), ALU.subtract)
        pr.act(row(1), row(1), AF.Ln, bias=C["eps_at"](0))
        pr.act(row(1), row(1), AF.Exp, scale=-0.5)
        pr.stt(row(3), row(0), -1.0, row(1), ALU.mult, ALU.mult)
        if dbg is not None and bi == 0:
            dbg("stat0", vw(stat))
        pr.mm(vw(ps_r1), C["ones_row"], row(1))
        pr.mm(vw(ps_r2), C["ones_row"], row(3))
        for i in range(n):
            xv = getx(i, f0, f1)
            ov = sink(i, f0, f1)
            if gamma is None:
                pr.vtt(vw(tmp), xv, vw(ps_r1), ALU.mult)
                pr.vtt(ov, vw(tmp), vw(ps_r2), ALU.add)
            else:
                pr.vtt(vw(tmp), xv, vw(ps_r1), ALU.mult)
                pr.vtt(vw(tmp), vw(tmp), vw(ps_r2), ALU.add)
                pr.vts(ov, vw(tmp), gamma(i), beta(i), ALU.mult, ALU.add)


class WLoader:

    def __init__(self, pr, tag, wide, nslots=3, conv_engine="vector"):
        self.pr = pr
        self.wide = wide
        self.nslots = nslots
        self.f32 = pr.sbuf("wl32%s" % tag, 128, wide * nslots)
        self.b16 = pr.sbuf("wlbf%s" % tag, 128, wide * nslots, BF16)
        self.k = 0
        self.conv_engine = conv_engine

    def fetch(self, dram_buf, src_ap, width, q="sync"):
        s = self.k % self.nslots
        self.k += 1
        dst32 = V(self.f32, 0, 128, s * self.wide, s * self.wide + width)
        dstbf = V(self.b16, 0, 128, s * self.wide, s * self.wide + width)
        self.pr.dma(q, dst32, dview(dram_buf, src_ap))
        self.pr.copy(dstbf, dst32, engine=self.conv_engine)
        return dstbf


def ph_l0(pr, C, x0T_d, wgT_d, wuT_d, wdT_d, xn0T, g0T):
    ld = [pr.sbuf("l0x%d" % k, 128, 512) for k in range(4)]
    cache = {}

    def getx(i, f0, f1, _c=cache):
        key = (i, f0)
        if key in _c:
            return _c[key]
        slot = ld[(i + (f0 // 512) * 3) % 4]
        dst = vw(slot, 0, 128, 0, f1 - f0)
        pr.dma("sync", dst, dview(x0T_d, x0T_d.h[i * 128:(i + 1) * 128, f0:f1]))
        _c.clear()
        _c[key] = dst
        return dst

    def sink(i, f0, f1):
        return xn0T.v(i, f0, f1)

    ph_ln_stream(pr, "ln0", getx, 8, S, sink, C)

    wl = WLoader(pr, "f0", 1024, 2)
    psg = [pr.pb[4], pr.pb[5]]
    psu = [pr.pb[6], pr.pb[7]]
    psd = [pr.pb[0], pr.pb[1]]
    sg = pr.sbuf("f0sg", 128, 1024)
    hbuf = FM(pr, "f0h", 24, 1024, BF16)
    wdl = WLoader(pr, "fd", 128, 4)

    for tv in range(2):
        t0 = tv * 1024
        for ht in range(24):
            wg = wl.fetch(wgT_d, rearr_block(wgT_d.h, ht * 128, (ht + 1) * 128), 1024, q="sync")
            wu = wl.fetch(wuT_d, rearr_block(wuT_d.h, ht * 128, (ht + 1) * 128), 1024, q="gpsimd")
            for ec in range(8):
                lg = V(wg.buf, 0, 128, wg.rect[2] + ec * 128, wg.rect[2] + (ec + 1) * 128)
                lu = V(wu.buf, 0, 128, wu.rect[2] + ec * 128, wu.rect[2] + (ec + 1) * 128)
                for tb in range(2):
                    rhs = xn0T.v(ec, t0 + tb * 512, t0 + (tb + 1) * 512)
                    pr.mm(vw(psg[tb]), lg, rhs, start=(ec == 0), stop=(ec == 7))
                    pr.mm(vw(psu[tb]), lu, rhs, start=(ec == 0), stop=(ec == 7))
            for tb in range(2):
                pr.act(V(sg, 0, 128, tb * 512, (tb + 1) * 512), vw(psg[tb]),
                       AF.Silu, scale=1.0 / 127.0)
                pr.vtt(hbuf.v(ht, tb * 512, (tb + 1) * 512),
                       V(sg, 0, 128, tb * 512, (tb + 1) * 512), vw(psu[tb]), ALU.mult)
        for et in range(8):
            for ht in range(24):
                wd = wdl.fetch(wdT_d, wdT_d.h[ht * 128:(ht + 1) * 128, et * 128:(et + 1) * 128],
                               128, q="scalar")
                for tb in range(2):
                    pr.mm(vw(psd[tb]), wd, hbuf.v(ht, tb * 512, (tb + 1) * 512),
                          start=(ht == 0), stop=(ht == 23))
            for tb in range(2):
                pr.act(g0T.v(et, t0 + tb * 512, t0 + (tb + 1) * 512), vw(psd[tb]),
                       AF.Silu, scale=1.0 / (127.0 * 127.0))



def _prep_common(I):
    f32 = np.float32
    P = {}
    x0 = np.asarray(I["L0_attn_x"], f32)
    x0T = np.ascontiguousarray(x0.transpose(0, 2, 1))
    P["x0T_rot"] = [np.ascontiguousarray(
        np.concatenate([x0T[c // 2][:, (c % 2) * TOWN:(c % 2) * TOWN + TOWN],
                        x0T[c // 2][:, ((c % 2) ^ 1) * TOWN:((c % 2) ^ 1) * TOWN + TOWN]],
                       axis=1)) for c in range(8)]
    P["wgT"] = np.ascontiguousarray(np.asarray(I["L0_ffn_w_gate"], np.int32).astype(f32).T)
    P["wuT"] = np.ascontiguousarray(np.asarray(I["L0_ffn_w_up"], np.int32).astype(f32).T)
    P["wdT"] = np.ascontiguousarray(np.asarray(I["L0_ffn_w_down"], np.int32).astype(f32).T)
    consts = np.zeros((128, 2696), f32)
    consts[:, 0] = 1.0
    consts[:, 1] = 1e-5
    for p_ in (0, 32, 64, 96):
        consts[p_, 8:520] = 1.0
    consts[:, 520:648] = np.eye(128, dtype=f32)
    for n_ in range(16):
        consts[n_, 648 + n_ * 128:648 + (n_ + 1) * 128] = 1.0
    P["consts"] = consts
    return P


def build_launch_A():
    pr = Prog()
    C = ph_consts(pr)
    x0T_d = pr.dram_in("x0T", [E, S])
    wgT_d = pr.dram_in("wgT", [E, HID0])
    wuT_d = pr.dram_in("wuT", [E, HID0])
    wdT_d = pr.dram_in("wdT", [HID0, E])
    xn0T_o = pr.dram_out("xn0T_o", [E, S], BF16)
    g0T_o = pr.dram_out("g0T_o", [E, TOWN], BF16)
    xn0T = FM(pr, "xn0T", 8, S, BF16)
    g0T = FM(pr, "g0T", 8, TOWN, BF16)
    ph_l0(pr, C, x0T_d, wgT_d, wuT_d, wdT_d, xn0T, g0T)
    for i in range(8):
        pr.dma("gpsimd", dview(xn0T_o, xn0T_o.h[i * 128:(i + 1) * 128, :]), xn0T.v(i))
        pr.dma("scalar", dview(g0T_o, g0T_o.h[i * 128:(i + 1) * 128, :]), g0T.v(i))
    return pr


def ph_attn(pr, C, xn0T, W2sh_d, cvec_d, wqT_d, wkT_d, wvT_d, s0preT):
    wl = WLoader(pr, "at", 512, 3)
    qT = pr.sbuf("at_q", 64, TOWN, BF16)
    kT = pr.sbuf("at_k", 64, S, BF16)
    vhat = pr.sbuf("at_v", 128, 32 * 64, BF16)
    vtmp = pr.sbuf("at_vt", 64, 512)
    band = pr.sbuf("at_bd", 128, 16 * 512, BF16)
    bb32 = pr.sbuf("at_b32", 128, 2 * 512)
    ccol = pr.sbuf("at_cc", 128, 1)
    expS = pr.sbuf("at_es", 128, 2 * 512, BF16)
    vsum = pr.sbuf("at_vs", 1, 64)
    rzrep = pr.sbuf("at_rr", 128, 512)
    tmp = pr.sbuf("at_tmp", 128, 512)

    psP = pr.pb[0]
    psT = pr.pb[1]
    psS = [pr.pb[6], pr.pb[7]]
    psA = pr.pb[4]
    psB = pr.pb[5]
    psZb = pr.pb[3]
    psR = pr.pb[2]

    for h in range(H):
        p0 = (h % 2) * 64
        et = h // 2
        pr.dma("gpsimd", vw(ccol), dview(cvec_d, bass.AP(cvec_d.h, h, [[0, 128], [1, 1]])))
        for d in range(16):
            dst = V(bb32, 0, 128, (d % 2) * 512, (d % 2) * 512 + 512)
            pr.dma("sync", dst, dview(W2sh_d,
                                      W2sh_d.h[h * 128:(h + 1) * 128,
                                               1920 - 128 * d:2432 - 128 * d]))
            pr.vts(V(band, 0, 128, d * 512, (d + 1) * 512), dst, vw(ccol), None,
                   ALU.subtract)
        wq = wl.fetch(wqT_d, rearr_block(wqT_d.h, h * 64, (h + 1) * 64), 512, q="sync")
        wk = wl.fetch(wkT_d, rearr_block(wkT_d.h, h * 64, (h + 1) * 64), 512, q="gpsimd")
        wv = wl.fetch(wvT_d, rearr_block(wvT_d.h, h * 64, (h + 1) * 64), 512, q="scalar")

        def proj(wbf, nblk, dst_row):
            for qb in range(nblk):
                pp = V(psP, 0, 64, 0, 512)
                for ec in range(8):
                    lw = V(wbf.buf, 0, 128, wbf.rect[2] + ec * 64, wbf.rect[2] + (ec + 1) * 64)
                    pr.mm(pp, lw, xn0T.v(ec, qb * 512, (qb + 1) * 512),
                          start=(ec == 0), stop=(ec == 7))
                pr.act(V(dst_row, 0, 64, qb * 512, (qb + 1) * 512), pp, AF.Copy)

        proj(wq, 4, qT)
        proj(wk, 8, kT)
        for qb in range(8):
            pp = V(psP, 0, 64, 0, 512)
            for ec in range(8):
                lw = V(wv.buf, 0, 128, wv.rect[2] + ec * 64, wv.rect[2] + (ec + 1) * 64)
                pr.mm(pp, lw, xn0T.v(ec, qb * 512, (qb + 1) * 512),
                      start=(ec == 0), stop=(ec == 7))
            pr.copy(vw(vtmp), pp)
            for j in range(4):
                tc = qb * 4 + j
                pt = V(psT, 0, 128, 0, 64)
                pr.transp(pt, V(vtmp, 0, 64, j * 128, (j + 1) * 128), C["identN"](64))
                pr.copy(V(vhat, 0, 128, tc * 64, (tc + 1) * 64), pt)
        pc = V(psZb, 0, 1, 0, 64)
        for tc in range(32):
            pr.mm(pc, C["ones_col_bf"], V(vhat, 0, 128, tc * 64, (tc + 1) * 64),
                  start=(tc == 0), stop=(tc == 31))
        pr.vts(vw(vsum), pc, V(ccol, 0, 1, 0, 1), None, ALU.mult)
        for qb in range(4):
            pA = V(psA, p0, p0 + 64, 0, 512)
            pZ = V(psZb, p0, p0 + 1, 0, 512)
            for kc in range(32):
                pS = vw(psS[kc % 2])
                pr.mm(pS, V(kT, 0, 64, kc * 128, (kc + 1) * 128),
                      V(qT, 0, 64, qb * 512, (qb + 1) * 512))
                es = V(expS, 0, 128, (kc % 2) * 512, (kc % 2) * 512 + 512)
                pr.act(es, pS, AF.Exp, scale=1.0 / 32.0)
                pr.mm(pA, V(vhat, 0, 128, kc * 64, (kc + 1) * 64), es,
                      start=(kc == 0), stop=(kc == 31))
                pr.mm(pZ, C["ones_col_bf"], es, start=(kc == 0), stop=(kc == 31))
            pB = V(psB, p0, p0 + 64, 0, 512)
            bl = list(range(16 + 4 * qb, 32))
            for j, kc in enumerate(bl):
                d = kc - 16 - 4 * qb
                pr.mm(pB, V(vhat, 0, 128, kc * 64, (kc + 1) * 64),
                      V(band, 0, 128, d * 512, (d + 1) * 512),
                      start=(j == 0), stop=False)
            pr.mm(pB, vw(vsum), C["ones512_at"](0), start=False, stop=True)
            rzs = V(tmp, p0, p0 + 1, 0, 512)
            pr.recip(rzs, pZ)
            pr.mm(vw(psR), C["ones_row_at"](p0), rzs)
            pr.copy(vw(rzrep), vw(psR))
            o1 = V(s0preT.buf, p0, p0 + 64,
                   et * TOWN + qb * 512, et * TOWN + (qb + 1) * 512)
            pr.vtt(o1, pA, V(rzrep, p0, p0 + 64, 0, 512), ALU.mult)
            pr.vtt(o1, o1, pB, ALU.add)


def build_launch_B():
    pr = Prog()
    C = ph_consts(pr, bf=True)
    xn0T_d = pr.dram_in("xn0T", [E, S], BF16)
    wqT_d = pr.dram_in("wqT", [E, E])
    wkT_d = pr.dram_in("wkT", [E, E])
    wvT_d = pr.dram_in("wvT", [E, E])
    W2sh_d = pr.dram_in("W2sh", [H * 128, 2432])
    cvec_d = pr.dram_in("cvec", [1, 16])
    gb_d = pr.dram_in("gb", [128, 16])
    s0T_o = pr.dram_out("s0T_o", [E, TOWN])
    xn0T = FM(pr, "xn0Ts", 8, S, BF16)
    for i in range(8):
        pr.dma("sync", xn0T.v(i), dview(xn0T_d, xn0T_d.h[i * 128:(i + 1) * 128, :]))
    gbsb = pr.sbuf("gbsb", 128, 16)
    pr.dma("scalar", vw(gbsb), vw(gb_d))
    s0preT = FM(pr, "s0pre", 8, TOWN, F32)
    ph_attn(pr, C, xn0T, W2sh_d, cvec_d, wqT_d, wkT_d, wvT_d, s0preT)
    ph_ln_stream(pr, "s0ln", lambda i, f0, f1: s0preT.v(i, f0, f1), 8, TOWN,
                 lambda i, f0, f1: s0preT.v(i, f0, f1), C,
                 gamma=lambda i: V(gbsb, 0, 128, i, i + 1),
                 beta=lambda i: V(gbsb, 0, 128, 8 + i, 9 + i))
    for i in range(8):
        pr.dma("gpsimd", dview(s0T_o, s0T_o.h[i * 128:(i + 1) * 128, :]), s0preT.v(i))
    return pr


def ph_combine(pr, C, x0T_d, s0T_d, g0T_d, x1T):
    l0 = pr.sbuf("cb0", 128, 512)
    l1 = pr.sbuf("cb1", 128, 512)
    l2 = pr.sbuf("cb2", 128, 512, BF16)
    for i in range(8):
        for bk in range(4):
            f0, f1 = bk * 512, (bk + 1) * 512
            pr.dma("sync", vw(l0), dview(x0T_d, x0T_d.h[i * 128:(i + 1) * 128, f0:f1]))
            pr.dma("gpsimd", vw(l1), dview(s0T_d, s0T_d.h[i * 128:(i + 1) * 128, f0:f1]))
            pr.dma("scalar", vw(l2), dview(g0T_d, g0T_d.h[i * 128:(i + 1) * 128, f0:f1]))
            ov = x1T.v(i, f0, f1)
            pr.vtt(ov, vw(l1), vw(l2), ALU.mult)
            pr.vtt(ov, ov, vw(l0), ALU.add)


def build_launch_C():
    pr = Prog()
    C = ph_consts(pr, bf=True)
    x0T_d = pr.dram_in("x0T", [E, S])
    s0T_d = pr.dram_in("s0T", [E, TOWN])
    g0T_d = pr.dram_in("g0T", [E, TOWN], BF16)
    lgb_d = pr.dram_in("lgb", [128, 16])
    ipT_d = pr.dram_in("ipT", [E, 2 * DIN])
    x1T_o = pr.dram_out("x1T_o", [E, TOWN])
    xn1T_o = pr.dram_out("xn1T_o", [E, TOWN], BF16)
    xpT_o = pr.dram_out("xpT_o", [DIN, TOWN], BF16)
    zsT_o = pr.dram_out("zsT_o", [DIN, TOWN], BF16)

    x1T = FM(pr, "x1T", 8, TOWN, F32)
    xn1T = FM(pr, "xn1T", 8, TOWN, BF16)
    ln2T = FM(pr, "ln2T", 8, TOWN, BF16)
    lgb = pr.sbuf("lgbs", 128, 16)
    pr.dma("scalar", vw(lgb), vw(lgb_d))

    ph_combine(pr, C, x0T_d, s0T_d, g0T_d, x1T)
    for i in range(8):
        pr.dma("sync", dview(x1T_o, x1T_o.h[i * 128:(i + 1) * 128, :]), x1T.v(i))
    ph_ln_stream(pr, "ln1", lambda i, f0, f1: x1T.v(i, f0, f1), 8, TOWN,
                 lambda i, f0, f1: xn1T.v(i, f0, f1), C)
    for i in range(8):
        pr.dma("gpsimd", dview(xn1T_o, xn1T_o.h[i * 128:(i + 1) * 128, :]), xn1T.v(i))
    ph_ln_stream(pr, "ln2", lambda i, f0, f1: xn1T.v(i, f0, f1), 8, TOWN,
                 lambda i, f0, f1: ln2T.v(i, f0, f1), C,
                 gamma=lambda i: V(lgb, 0, 128, i, i + 1),
                 beta=lambda i: V(lgb, 0, 128, 8 + i, 9 + i), in_bf=True)
    wl = WLoader(pr, "ip", 1024, 2)
    ps4 = [pr.pb[4], pr.pb[5], pr.pb[6], pr.pb[7]]
    stage = pr.sbuf("ipst", 128, 2048, BF16)
    for rt in range(32):
        wb = wl.fetch(ipT_d, rearr_block(ipT_d.h, rt * 128, (rt + 1) * 128), 1024,
                      q=("sync", "gpsimd")[rt % 2])
        for ec in range(8):
            lw = V(wb.buf, 0, 128, wb.rect[2] + ec * 128, wb.rect[2] + (ec + 1) * 128)
            for tb in range(4):
                pr.mm(vw(ps4[tb]), lw, ln2T.v(ec, tb * 512, (tb + 1) * 512),
                      start=(ec == 0), stop=(ec == 7))
        for tb in range(4):
            sv = V(stage, 0, 128, tb * 512, (tb + 1) * 512)
            if rt < 16:
                pr.act(sv, vw(ps4[tb]), AF.Copy)
            else:
                pr.act(sv, vw(ps4[tb]), AF.Silu)
        dst = xpT_o if rt < 16 else zsT_o
        r0 = (rt % 16) * 128
        pr.dma("scalar", dview(dst, dst.h[r0:r0 + 128, :]), vw(stage))
    return pr


def build_launch_D():
    pr = Prog()
    C = ph_consts(pr, bf=True)
    xpT_d = pr.dram_in("xpT", [DIN, TOWN], BF16)
    halo_d = pr.dram_in("halo", [DIN, 3], BF16)
    convw_d = pr.dram_in("convw", [DIN, 4])
    cols_d = pr.dram_in("cols", [128, 32])
    xprojT_d = pr.dram_in("xprojT", [DIN, 96])
    dtprojT_d = pr.dram_in("dtprojT", [DTR, DIN])
    uT_o = pr.dram_out("uT_o", [DIN, TOWN], BF16)
    dtT_o = pr.dram_out("dtT_o", [DIN, TOWN])
    bcT_o = pr.dram_out("bcT_o", [32, TOWN])

    cw = pr.sbuf("cw", 128, 64)
    pr.dma("sync", vw(cw), dview(convw_d, convw_d.h[:, :].rearrange("(a p) k -> p a k", p=128)))
    cols = pr.sbuf("colss", 128, 32)
    pr.dma("sync", vw(cols), vw(cols_d))

    uT = FM(pr, "uT", 16, TOWN, BF16)
    cbuf = pr.sbuf("cbuf", 128, 3 + TOWN, BF16)
    acc = pr.sbuf("acc", 128, TOWN)
    for dt_ in range(16):
        pr.dma("sync", V(cbuf, 0, 128, 0, 3),
               dview(halo_d, halo_d.h[dt_ * 128:(dt_ + 1) * 128, :]))
        pr.dma("sync", V(cbuf, 0, 128, 3, 3 + TOWN),
               dview(xpT_d, xpT_d.h[dt_ * 128:(dt_ + 1) * 128, :]))

        def win(k):
            return V(cbuf, 0, 128, k, k + TOWN)

        pr.vts(vw(acc), win(0), V(cw, 0, 128, dt_ * 4, dt_ * 4 + 1), None, ALU.mult)
        for k in range(1, 4):
            pr.stt(vw(acc), win(k), V(cw, 0, 128, dt_ * 4 + k, dt_ * 4 + k + 1),
                   vw(acc), ALU.mult, ALU.add)
        pr.act(uT.v(dt_), vw(acc), AF.Silu, bias=V(cols, 0, 128, dt_, dt_ + 1))
        pr.dma("gpsimd", dview(uT_o, uT_o.h[dt_ * 128:(dt_ + 1) * 128, :]), uT.v(dt_))
    wx32 = pr.sbuf("wx32", 128, 16 * 96)
    wx = pr.sbuf("wx", 128, 16 * 96, BF16)
    pr.dma("sync", wx32.v(), dview(xprojT_d, xprojT_d.h[:, :].rearrange("(a p) c -> p a c", p=128)))
    pr.copy(vw(wx), vw(wx32))
    xdbl = pr.sbuf("xdbl", 96, TOWN)
    psX = pr.pb[0]
    for tb in range(4):
        px = V(psX, 0, 96, 0, 512)
        for dc in range(16):
            pr.mm(px, V(wx, 0, 128, dc * 96, (dc + 1) * 96),
                  uT.v(dc, tb * 512, (tb + 1) * 512), start=(dc == 0), stop=(dc == 15))
        pr.copy(V(xdbl, 0, 96, tb * 512, (tb + 1) * 512), px)
    pr.dma("gpsimd", dview(bcT_o, bcT_o.h[:, :]), V(xdbl, 64, 96, 0, TOWN))
    dtp = pr.sbuf("dtp", 64, TOWN, BF16)
    pr.copy(vw(dtp), V(xdbl, 0, 64, 0, TOWN))
    wd32 = pr.sbuf("wd32", 64, DIN)
    wdt = pr.sbuf("wdt", 64, DIN, BF16)
    pr.dma("sync", vw(wd32), vw(dtprojT_d))
    pr.copy(vw(wdt), vw(wd32))
    psD = pr.pb[1]
    s1 = pr.sbuf("sp1", 128, 512)
    s2 = pr.sbuf("sp2", 128, 512)
    s3 = pr.sbuf("sp3", 128, 512)
    stg = pr.sbuf("dtstage", 128, TOWN)
    for dt_ in range(16):
        for tb in range(4):
            pd = vw(psD)
            pr.mm(pd, V(wdt, 0, 64, dt_ * 128, (dt_ + 1) * 128),
                  V(dtp, 0, 64, tb * 512, (tb + 1) * 512))
            pr.act(vw(s1), pd, AF.Identity, bias=V(cols, 0, 128, 16 + dt_, 17 + dt_))
            pr.act(vw(s2), vw(s1), AF.Abs)
            pr.act(vw(s2), vw(s2), AF.Exp, scale=-1.0)
            pr.act(vw(s2), vw(s2), AF.Ln, bias=1.0)
            pr.act(vw(s3), vw(s1), AF.Relu)
            pr.vtt(V(stg, 0, 128, tb * 512, (tb + 1) * 512), vw(s3), vw(s2), ALU.add)
        pr.dma("scalar", dview(dtT_o, dtT_o.h[dt_ * 128:(dt_ + 1) * 128, :]), vw(stg))
    return pr


def build_launch_E():
    pr = Prog()
    C = ph_consts(pr)
    dtS_d = pr.dram_in("dtS", [E, S])
    uS_d = pr.dram_in("uS", [E, S], BF16)
    bS_d = pr.dram_in("bS", [NST, S])
    cS_d = pr.dram_in("cS", [NST, S])
    alog_d = pr.dram_in("alog", [128, 8 * NST])
    dD_d = pr.dram_in("dD", [128, 8])
    yS_o = pr.dram_out("yS_o", [E, S], BF16)

    asb = pr.sbuf("asb", 128, 8 * NST)
    pr.dma("sync", vw(asb), vw(alog_d))
    pr.act(vw(asb), vw(asb), AF.Exp)
    pr.vts(vw(asb), vw(asb), -1.0, None, ALU.mult)
    dsb = pr.sbuf("dsb", 128, 8)
    pr.dma("sync", vw(dsb), vw(dD_d))

    dts = pr.sbuf("dts", 128, S)
    us = pr.sbuf("us", 128, S, BF16)
    dub = pr.sbuf("dub", 128, S, BF16)
    yac = pr.sbuf("yac", 128, S)
    hbf = pr.sbuf("hbf", 128, S)
    da = pr.sbuf("da", 128, 2 * 512)
    db = pr.sbuf("db", 128, 2 * 512)
    brow = pr.sbuf("brow", 1, S)
    crow = pr.sbuf("crow", 1, S)
    ys = pr.sbuf("ys", 128, S, BF16)
    psB_ = pr.pb[0]
    psC_ = pr.pb[1]

    for dt_ in range(8):
        pr.dma("sync", vw(dts), dview(dtS_d, dtS_d.h[dt_ * 128:(dt_ + 1) * 128, :]))
        pr.dma("gpsimd", vw(us), dview(uS_d, uS_d.h[dt_ * 128:(dt_ + 1) * 128, :]))
        pr.vtt(vw(dub), vw(dts), vw(us), ALU.mult)
        for n in range(NST):
            pr.dma("sync", vw(brow), dview(bS_d, bS_d.h[n:n + 1, :]))
            pr.dma("gpsimd", vw(crow), dview(cS_d, cS_d.h[n:n + 1, :]))
            acol = V(asb, 0, 128, dt_ * NST + n, dt_ * NST + n + 1)
            for tc in range(8):
                f0, f1 = tc * 512, (tc + 1) * 512
                pB = vw(psB_)
                pC = vw(psC_)
                pr.mm(pB, C["ones_row"], V(brow, 0, 1, f0, f1))
                pr.mm(pC, C["ones_row"], V(crow, 0, 1, f0, f1))
                dav = V(da, 0, 128, (tc % 2) * 512, (tc % 2) * 512 + 512)
                pr.act(dav, V(dts, 0, 128, f0, f1), AF.Exp, scale=acol)
                dbv = V(db, 0, 128, (tc % 2) * 512, (tc % 2) * 512 + 512)
                pr.vtt(dbv, V(dub, 0, 128, f0, f1), pB, ALU.mult)
                ini = 0.0 if tc == 0 else V(hbf, 0, 128, f0 - 1, f0)
                pr.scan_op(V(hbf, 0, 128, f0, f1), dav, dbv, ini)
                if n == 0:
                    pr.vtt(V(yac, 0, 128, f0, f1), V(hbf, 0, 128, f0, f1), pC, ALU.mult)
                else:
                    pr.vtt(dav, V(hbf, 0, 128, f0, f1), pC, ALU.mult)
                    pr.vtt(V(yac, 0, 128, f0, f1), V(yac, 0, 128, f0, f1), dav, ALU.add)
        dcol = V(dsb, 0, 128, dt_, dt_ + 1)
        pr.stt(vw(ys), vw(us), dcol, vw(yac), ALU.mult, ALU.add)
        pr.dma("scalar", dview(yS_o, yS_o.h[dt_ * 128:(dt_ + 1) * 128, :]), vw(ys))
    return pr


def build_launch_F():
    pr = Prog()
    C = ph_consts(pr)
    y2_d = pr.dram_in("y2", [DIN, TOWN], BF16)
    zsT_d = pr.dram_in("zsT", [DIN, TOWN], BF16)
    opT_d = pr.dram_in("opT", [DIN, E])
    gwT_d = pr.dram_in("gwT", [E, DIN])
    gbias_d = pr.dram_in("gbias", [128, 16])
    xn1T_d = pr.dram_in("xn1T", [E, TOWN], BF16)
    s1T_o = pr.dram_out("s1T_o", [E, TOWN])

    gbias = pr.sbuf("gbiass", 128, 16)
    pr.dma("scalar", vw(gbias), vw(gbias_d))
    xn1T = FM(pr, "xn1Ts", 8, TOWN, BF16)
    for i in range(8):
        pr.dma("sync", xn1T.v(i), dview(xn1T_d, xn1T_d.h[i * 128:(i + 1) * 128, :]))

    ygT = FM(pr, "ygT", 16, TOWN, BF16)
    l0 = pr.sbuf("fl0", 128, TOWN, BF16)
    l1 = pr.sbuf("fl1", 128, TOWN, BF16)
    for dc in range(16):
        pr.dma("sync", vw(l0), dview(y2_d, y2_d.h[dc * 128:(dc + 1) * 128, :]))
        pr.dma("gpsimd", vw(l1), dview(zsT_d, zsT_d.h[dc * 128:(dc + 1) * 128, :]))
        pr.vtt(ygT.v(dc), vw(l0), vw(l1), ALU.mult)
    mgT = FM(pr, "mgT", 8, TOWN, BF16)
    wl = WLoader(pr, "op", 1024, 2)
    ps4 = [pr.pb[4], pr.pb[5], pr.pb[6], pr.pb[7]]
    for et in range(8):
        for hf in range(2):
            wb = wl.fetch(opT_d,
                          opT_d.h[hf * 1024:(hf + 1) * 1024, et * 128:(et + 1) * 128]
                          .rearrange("(a p) c -> p a c", p=128), 1024,
                          q=("sync", "gpsimd")[hf])
            for dcl in range(8):
                dc = hf * 8 + dcl
                lw = V(wb.buf, 0, 128, wb.rect[2] + dcl * 128, wb.rect[2] + (dcl + 1) * 128)
                for tb in range(4):
                    pr.mm(vw(ps4[tb]), lw, ygT.v(dc, tb * 512, (tb + 1) * 512),
                          start=(dc == 0), stop=(dc == 15))
        for tb in range(4):
            pr.act(mgT.v(et, tb * 512, (tb + 1) * 512), vw(ps4[tb]), AF.Gelu)
    sgT = FM(pr, "sgT", 8, TOWN, BF16)
    wl2 = WLoader(pr, "gl", 1024, 2)
    tmpf = pr.sbuf("ftmp", 128, 512)
    stage = pr.sbuf("fstage", 128, 512)
    for half in range(2):
        for rt in range(8):
            wb = wl2.fetch(gwT_d, rearr_block(gwT_d.h, (1 - half) * 1024 + rt * 128,
                                              (1 - half) * 1024 + (rt + 1) * 128), 1024,
                           q=("sync", "gpsimd")[rt % 2])
            for ec in range(8):
                lw = V(wb.buf, 0, 128, wb.rect[2] + ec * 128, wb.rect[2] + (ec + 1) * 128)
                for tb in range(4):
                    pr.mm(vw(ps4[tb]), lw, mgT.v(ec, tb * 512, (tb + 1) * 512),
                          start=(ec == 0), stop=(ec == 7))
            if half == 0:
                for tb in range(4):
                    pr.act(sgT.v(rt, tb * 512, (tb + 1) * 512), vw(ps4[tb]), AF.Sigmoid,
                           bias=V(gbias, 0, 128, 8 + rt, 9 + rt))
            else:
                for tb in range(4):
                    pr.act(vw(tmpf), vw(ps4[tb]), AF.Identity,
                           bias=V(gbias, 0, 128, rt, rt + 1))
                    sv = vw(stage)
                    pr.vtt(sv, vw(tmpf), sgT.v(rt, tb * 512, (tb + 1) * 512), ALU.mult)
                    pr.vtt(sv, sv, xn1T.v(rt, tb * 512, (tb + 1) * 512), ALU.add)
                    pr.dma("scalar",
                           dview(s1T_o, s1T_o.h[rt * 128:(rt + 1) * 128,
                                                tb * 512:(tb + 1) * 512]), sv)
    return pr


def build_launch_G():
    pr = Prog()
    C = ph_consts(pr, bf=True)
    xn1T_d = pr.dram_in("xn1T", [E, TOWN], BF16)
    x1T_d = pr.dram_in("x1T", [E, TOWN])
    s1T_d = pr.dram_in("s1T", [E, TOWN])
    rw_d = pr.dram_in("rw", [E, NEXP])
    rb_d = pr.dram_in("rb", [NEXP, 1])
    w1_d = pr.dram_in("w1", [NEXP * E, HIDM])
    b1r_d = pr.dram_in("b1r", [NEXP * 128, 32])
    w2_d = pr.dram_in("w2", [NEXP * HIDM, E])
    b2_d = pr.dram_in("b2", [NEXP, E])
    outT_o = pr.dram_out("outT_o", [E, TOWN])

    xn1T = FM(pr, "xn1Ts", 8, TOWN, BF16)
    for i in range(8):
        pr.dma("sync", xn1T.v(i), dview(xn1T_d, xn1T_d.h[i * 128:(i + 1) * 128, :]))

    rw32 = pr.sbuf("rw32", 128, 64)
    rwbf = pr.sbuf("rwbf", 128, 64, BF16)
    pr.dma("sync", vw(rw32), dview(rw_d, rw_d.h[:, :].rearrange("(a p) c -> p a c", p=128)))
    pr.copy(vw(rwbf), vw(rw32))
    rbsb = pr.sbuf("rbsb", 8, 1)
    pr.dma("sync", vw(rbsb), vw(rb_d))
    lgT = pr.sbuf("lgT", 8, TOWN)
    ps0 = pr.pb[0]
    for tb in range(4):
        p8 = V(ps0, 0, 8, 0, 512)
        for ec in range(8):
            pr.mm(p8, V(rwbf, 0, 128, ec * 8, (ec + 1) * 8),
                  xn1T.v(ec, tb * 512, (tb + 1) * 512), start=(ec == 0), stop=(ec == 7))
        pr.act(V(lgT, 0, 8, tb * 512, (tb + 1) * 512), p8, AF.Identity, bias=vw(rbsb))
    gbuf = pr.sbuf("gbuf", 128, 16 * 8)
    gTb = pr.sbuf("gTb", 8, TOWN, BF16)
    lt = pr.sbuf("lt", 128, 8)
    m1 = pr.sbuf("gm", 128, 4)
    e1 = pr.sbuf("ge", 128, 8)
    ps1 = pr.pb[1]
    ps2 = pr.pb[2]
    for tt in range(16):
        pt = V(ps1, 0, 128, 0, 8)
        pr.transp(pt, V(lgT, 0, 8, tt * 128, (tt + 1) * 128), C["identN"](8))
        pr.copy(vw(lt), pt)
        pr.reduce(V(m1, 0, 128, 0, 1), vw(lt), ALU.max)
        pr.vts(vw(e1), vw(lt), V(m1, 0, 128, 0, 1), None, ALU.is_equal)
        pr.stt(vw(e1), vw(e1), -1e30, vw(lt), ALU.mult, ALU.add)
        pr.reduce(V(m1, 0, 128, 1, 2), vw(e1), ALU.max)
        pr.vts(V(m1, 0, 128, 2, 3), V(m1, 0, 128, 0, 1), -1.0, None, ALU.mult)
        pr.act(vw(e1), vw(lt), AF.Exp, bias=V(m1, 0, 128, 2, 3))
        gv = V(gbuf, 0, 128, tt * 8, (tt + 1) * 8)
        pr.vts(gv, vw(lt), V(m1, 0, 128, 1, 2), None, ALU.is_ge)
        pr.vtt(gv, gv, vw(e1), ALU.mult)
        pr.reduce(V(m1, 0, 128, 3, 4), gv, ALU.add)
        pr.recip(V(m1, 0, 128, 3, 4), V(m1, 0, 128, 3, 4))
        pr.vts(gv, gv, V(m1, 0, 128, 3, 4), None, ALU.mult)
        pg = V(ps2, 0, 8, 0, 128)
        pr.transp(pg, gv, C["ident"])
        pr.copy(V(gTb, 0, 8, tt * 128, (tt + 1) * 128), pg)
    macc = pr.sbuf("macc", 128, 16 * 1024)
    b2bf = pr.sbuf("b2bf", 8, E, BF16)
    b232 = pr.sbuf("b232", 8, E)
    pr.dma("sync", vw(b232), vw(b2_d))
    pr.copy(vw(b2bf), vw(b232))
    ps3 = pr.pb[3]
    for tt in range(16):
        for cb in range(2):
            pb_ = vw(ps3)
            pr.mm(pb_, V(gTb, 0, 8, tt * 128, (tt + 1) * 128),
                  V(b2bf, 0, 8, cb * 512, (cb + 1) * 512))
            pr.copy(V(macc, 0, 128, tt * 1024 + cb * 512, tt * 1024 + (cb + 1) * 512), pb_)
    hbuf = FM(pr, "hq", 8, TOWN, BF16)
    b1sb = pr.sbuf("b1sb", 128, 32)
    wl = WLoader(pr, "w1", 1024, 2)
    w2r = pr.sbuf("w2r", 128, 1024)
    w2b = pr.sbuf("w2b", 128, 8 * 1024, BF16)
    psH = [pr.pb[4], pr.pb[5], pr.pb[6], pr.pb[7]]
    psY = [pr.pb[1], pr.pb[2]]
    for e in range(NEXP):
        pr.dma("scalar", vw(b1sb), dview(b1r_d, b1r_d.h[e * 128:(e + 1) * 128, :]))
        for hc in range(4):
            for htl in range(8):
                ht = hc * 8 + htl
                src = w1_d.h[e * E:(e + 1) * E, ht * 128:(ht + 1) * 128].rearrange("(a p) c -> p a c", p=128)
                wb = wl.fetch(w1_d, src, 1024, q=("sync", "gpsimd")[htl % 2])
                for ec in range(8):
                    lw = V(wb.buf, 0, 128, wb.rect[2] + ec * 128, wb.rect[2] + (ec + 1) * 128)
                    for tb in range(4):
                        pr.mm(vw(psH[tb]), lw, xn1T.v(ec, tb * 512, (tb + 1) * 512),
                              start=(ec == 0), stop=(ec == 7))
                for tb in range(4):
                    pr.act(hbuf.v(htl, tb * 512, (tb + 1) * 512), vw(psH[tb]), AF.Relu,
                           bias=V(b1sb, 0, 128, ht, ht + 1))
                pr.dma("scalar", vw(w2r),
                       dview(w2_d, w2_d.h[e * HIDM + ht * 128:e * HIDM + (ht + 1) * 128, :]))
                pr.copy(V(w2b, 0, 128, htl * 1024, (htl + 1) * 1024), vw(w2r))
            for tt in range(16):
                for cb in range(2):
                    pY = vw(psY[cb])
                    for htl in range(8):
                        pr.mm(pY, hbuf.v(htl, tt * 128, (tt + 1) * 128),
                              V(w2b, 0, 128, htl * 1024 + cb * 512, htl * 1024 + (cb + 1) * 512),
                              start=(htl == 0), stop=(htl == 7))
                    mv = V(macc, 0, 128, tt * 1024 + cb * 512, tt * 1024 + (cb + 1) * 512)
                    pr.stt(mv, pY, V(gbuf, 0, 128, tt * 8 + e, tt * 8 + e + 1), mv,
                           ALU.mult, ALU.add)
    g1t = pr.sbuf("g1t", 128, 1024)
    xl = pr.sbuf("xl", 128, 128)
    sl = pr.sbuf("sl", 128, 128)
    ost = pr.sbuf("ost", 128, 128)
    psG = pr.pb[0]
    for tt in range(16):
        pr.act(vw(g1t), V(macc, 0, 128, tt * 1024, (tt + 1) * 1024), AF.Silu)
        for ec in range(8):
            pg = V(psG, 0, 128, 0, 128)
            pr.transp(pg, V(g1t, 0, 128, ec * 128, (ec + 1) * 128), C["ident"])
            pr.dma("sync", vw(xl),
                   dview(x1T_d, x1T_d.h[ec * 128:(ec + 1) * 128, tt * 128:(tt + 1) * 128]))
            pr.dma("gpsimd", vw(sl),
                   dview(s1T_d, s1T_d.h[ec * 128:(ec + 1) * 128, tt * 128:(tt + 1) * 128]))
            pr.vtt(vw(ost), vw(sl), pg, ALU.mult)
            pr.vtt(vw(ost), vw(ost), vw(xl), ALU.add)
            pr.dma("scalar",
                   dview(outT_o, outT_o.h[ec * 128:(ec + 1) * 128, tt * 128:(tt + 1) * 128]),
                   vw(ost))
    return pr



def _prep_all(I):
    f32 = np.float32
    P = _prep_common(I)
    tab = np.asarray(I["L0_attn_bias_table"], f32)
    P["cvec"] = np.ascontiguousarray(tab[2046:2047, :])
    j = np.arange(2432)[None, :]
    p = np.arange(128)[:, None]
    P["W2sh"] = []
    for c in range(8):
        th = c % 2
        own0 = th * TOWN
        other0 = (1 - th) * TOWN
        idx = np.clip(own0 - other0 + 2175 + j - p, 0, 2046)
        P["W2sh"].append(np.ascontiguousarray(
            tab[idx, :].transpose(2, 0, 1).reshape(H * 128, 2432)))
    P["wqT"] = np.ascontiguousarray(np.asarray(I["L0_attn_Wq"], f32).T)
    P["wkT"] = np.ascontiguousarray(np.asarray(I["L0_attn_Wk"], f32).T)
    P["wvT"] = np.ascontiguousarray(np.asarray(I["L0_attn_Wv"], f32).T)
    gb = np.zeros((128, 16), f32)
    gb[:, 0:8] = np.asarray(I["L0_attn_ln_gamma"], f32).reshape(8, 128).T
    gb[:, 8:16] = np.asarray(I["L0_attn_ln_beta"], f32).reshape(8, 128).T
    P["gb"] = gb
    lgb = np.zeros((128, 16), f32)
    lgb[:, 0:8] = np.asarray(I["L1_attn_ln_g"], f32).reshape(8, 128).T
    lgb[:, 8:16] = np.asarray(I["L1_attn_ln_b"], f32).reshape(8, 128).T
    P["lgb"] = lgb
    P["ipT"] = np.ascontiguousarray(np.asarray(I["L1_attn_in_proj_w"], f32).T)
    P["convw"] = np.ascontiguousarray(np.asarray(I["L1_attn_conv_w"], f32)[:, 0, :])
    cols = np.zeros((128, 32), f32)
    cols[:, 0:16] = np.asarray(I["L1_attn_conv_b"], f32).reshape(16, 128).T
    cols[:, 16:32] = np.asarray(I["L1_attn_dt_proj_b"], f32).reshape(16, 128).T
    P["cols"] = cols
    P["xprojT"] = np.ascontiguousarray(np.asarray(I["L1_attn_x_proj_w"], f32).T)
    P["dtprojT"] = np.ascontiguousarray(np.asarray(I["L1_attn_dt_proj_w"], f32).T)
    alog = np.asarray(I["L1_attn_A_log"], f32)
    P["alog"] = [np.ascontiguousarray(
        alog[(c % 2) * 1024:(c % 2) * 1024 + 1024, :].reshape(8, 128, 16)
        .transpose(1, 0, 2).reshape(128, 8 * 16)) for c in range(8)]
    dd = np.asarray(I["L1_attn_D"], f32)
    P["dD"] = [np.ascontiguousarray(
        dd[(c % 2) * 1024:(c % 2) * 1024 + 1024].reshape(8, 128).T) for c in range(8)]
    P["opT"] = np.ascontiguousarray(np.asarray(I["L1_attn_out_proj_w"], f32).T)
    P["gwT"] = np.ascontiguousarray(np.asarray(I["L1_attn_glu_w"], f32).T)
    glb = np.asarray(I["L1_attn_glu_b"], f32)
    gbias = np.zeros((128, 16), f32)
    gbias[:, 0:8] = glb[0:1024].reshape(8, 128).T
    gbias[:, 8:16] = glb[1024:2048].reshape(8, 128).T
    P["gbias"] = gbias
    P["rw"] = np.ascontiguousarray(np.asarray(I["L1_ffn_router_w"], f32))
    P["rb"] = np.ascontiguousarray(np.asarray(I["L1_ffn_router_b"], f32).reshape(8, 1))
    P["w1"] = np.ascontiguousarray(np.asarray(I["L1_ffn_w1"], f32).reshape(NEXP * E, HIDM))
    P["b1r"] = np.ascontiguousarray(
        np.asarray(I["L1_ffn_b1"], f32).reshape(NEXP, 32, 128).transpose(0, 2, 1)
        .reshape(NEXP * 128, 32))
    P["w2"] = np.ascontiguousarray(np.asarray(I["L1_ffn_w2"], f32).reshape(NEXP * HIDM, E))
    P["b2"] = np.ascontiguousarray(np.asarray(I["L1_ffn_b2"], f32))
    return P


def kernel(**inputs):
    return kernel_fused(inputs)


def kernel_unfused(**inputs):
    P = _prep_all(inputs)
    cores = list(range(8))
    bf = lambda a: np.ascontiguousarray(a)

    prA = build_launch_A()
    resA = run_bass_kernel_spmd(prA.emit(), [dict(
        consts=P["consts"], x0T=P["x0T_rot"][c], wgT=P["wgT"], wuT=P["wuT"],
        wdT=P["wdT"]) for c in cores], core_ids=cores)
    xn0T = [resA.results[c]["xn0T_o"] for c in cores]
    g0T = [resA.results[c]["g0T_o"] for c in cores]

    prB = build_launch_B()
    resB = run_bass_kernel_spmd(prB.emit(), [dict(
        consts=P["consts"], xn0T=xn0T[c], wqT=P["wqT"], wkT=P["wkT"], wvT=P["wvT"],
        W2sh=P["W2sh"][c], cvec=P["cvec"], gb=P["gb"]) for c in cores], core_ids=cores)
    s0T = [resB.results[c]["s0T_o"] for c in cores]

    prC = build_launch_C()
    resC = run_bass_kernel_spmd(prC.emit(), [dict(
        consts=P["consts"], x0T=P["x0T_rot"][c], s0T=s0T[c], g0T=g0T[c],
        lgb=P["lgb"], ipT=P["ipT"]) for c in cores], core_ids=cores)
    x1T = [resC.results[c]["x1T_o"] for c in cores]
    xn1T = [resC.results[c]["xn1T_o"] for c in cores]
    xpT = [resC.results[c]["xpT_o"] for c in cores]
    zsT = [resC.results[c]["zsT_o"] for c in cores]

    zeros_halo = np.zeros((DIN, 3), BF16NP)
    prD = build_launch_D()
    resD = run_bass_kernel_spmd(prD.emit(), [dict(
        consts=P["consts"], xpT=xpT[c],
        halo=(zeros_halo if c % 2 == 0 else np.ascontiguousarray(xpT[c - 1][:, TOWN - 3:])),
        convw=P["convw"], cols=P["cols"], xprojT=P["xprojT"], dtprojT=P["dtprojT"])
        for c in cores], core_ids=cores)
    uT = [resD.results[c]["uT_o"] for c in cores]
    dtT = [resD.results[c]["dtT_o"] for c in cores]
    bcT = [resD.results[c]["bcT_o"] for c in cores]

    prE = build_launch_E()
    in_E = []
    for c in cores:
        b, dh = c // 2, c % 2
        r = slice(dh * 1024, dh * 1024 + 1024)
        in_E.append(dict(
            consts=P["consts"],
            dtS=np.ascontiguousarray(np.concatenate([dtT[2 * b][r], dtT[2 * b + 1][r]], 1)),
            uS=np.ascontiguousarray(np.concatenate([uT[2 * b][r], uT[2 * b + 1][r]], 1)),
            bS=np.ascontiguousarray(np.concatenate([bcT[2 * b][0:16], bcT[2 * b + 1][0:16]], 1)),
            cS=np.ascontiguousarray(np.concatenate([bcT[2 * b][16:32], bcT[2 * b + 1][16:32]], 1)),
            alog=P["alog"][c], dD=P["dD"][c]))
    resE = run_bass_kernel_spmd(prE.emit(), in_E, core_ids=cores)
    yS = [resE.results[c]["yS_o"] for c in cores]

    prF = build_launch_F()
    in_F = []
    for c in cores:
        b, th = c // 2, c % 2
        cs = slice(th * TOWN, th * TOWN + TOWN)
        y2 = np.ascontiguousarray(np.concatenate([yS[2 * b][:, cs], yS[2 * b + 1][:, cs]], 0))
        in_F.append(dict(consts=P["consts"], y2=y2, zsT=zsT[c], opT=P["opT"],
                         gwT=P["gwT"], gbias=P["gbias"], xn1T=xn1T[c]))
    resF = run_bass_kernel_spmd(prF.emit(), in_F, core_ids=cores)
    s1T = [resF.results[c]["s1T_o"] for c in cores]

    prG = build_launch_G()
    resG = run_bass_kernel_spmd(prG.emit(), [dict(
        consts=P["consts"], xn1T=xn1T[c], x1T=x1T[c], s1T=s1T[c], rw=P["rw"],
        rb=P["rb"], w1=P["w1"], b1r=P["b1r"], w2=P["w2"], b2=P["b2"])
        for c in cores], core_ids=cores)

    out = np.empty((B, S, E), np.float32)
    for c in cores:
        b, th = c // 2, c % 2
        out[b, th * TOWN:(th + 1) * TOWN, :] = resG.results[c]["outT_o"].T
    return out



PAIRS = [[0, 1], [2, 3], [4, 5], [6, 7]]


def build_fused(dbg=False):
    pr = Prog()
    C = ph_consts(pr, bf=True)
    pr.ln_scratch = (pr.sbuf("lnsq", 128, 512), pr.sbuf("lnst", 128, 2048),
                     pr.sbuf("lntm", 128, 512))

    SBF = pr.sbuf("SBF", 128, 51200, BF16)
    SF = pr.sbuf("SF", 128, 21120, F32)

    x0T_d = pr.dram_in("x0T", [E, S])
    wgT_d = pr.dram_in("wgT", [E, HID0])
    wuT_d = pr.dram_in("wuT", [E, HID0])
    wdT_d = pr.dram_in("wdT", [HID0, E])
    wqT_d = pr.dram_in("wqT", [E, E])
    wkT_d = pr.dram_in("wkT", [E, E])
    wvT_d = pr.dram_in("wvT", [E, E])
    W2sh_d = pr.dram_in("W2sh", [H * 128, 2432])
    cvec_d = pr.dram_in("cvec", [1, 16])
    gb_d = pr.dram_in("gb", [128, 16])
    lgb_d = pr.dram_in("lgb", [128, 16])
    ipT_d = pr.dram_in("ipT", [E, 2 * DIN])
    convw_d = pr.dram_in("convw", [DIN, 4])
    cols_d = pr.dram_in("cols", [128, 32])
    xprojT_d = pr.dram_in("xprojT", [DIN, 96])
    dtprojT_d = pr.dram_in("dtprojT", [DTR, DIN])
    alog_d = pr.dram_in("alog", [128, 8 * NST])
    dD_d = pr.dram_in("dD", [128, 8])
    maskbf_d = pr.dram_in("maskbf", [128, 1])
    mask0_d = pr.dram_in("mask0", [128, 1])
    opT_d = pr.dram_in("opT", [DIN, E])
    gwT_d = pr.dram_in("gwT", [E, DIN])
    gbias_d = pr.dram_in("gbias", [128, 16])
    rw_d = pr.dram_in("rw", [E, NEXP])
    rb_d = pr.dram_in("rb", [NEXP, 1])
    w1_d = pr.dram_in("w1", [NEXP * E, HIDM])
    b1r_d = pr.dram_in("b1r", [NEXP * 128, 32])
    w2_d = pr.dram_in("w2", [NEXP * HIDM, E])
    b2_d = pr.dram_in("b2", [NEXP, E])
    outT_o = pr.dram_out("outT_o", [E, TOWN])

    mk = pr.dram_out if dbg else pr.dram_tmp
    g0_d = mk("g0_d", [E, TOWN], BF16)
    s0_d = mk("s0_d", [E, TOWN])
    x1_d = mk("x1_d", [E, TOWN])
    xn1_d = mk("xn1_d", [E, TOWN], BF16)
    zs_d = mk("zs_d", [DIN, TOWN], BF16)
    xp_d = mk("xp_d", [DIN, TOWN], BF16)
    s1_d = mk("s1_d", [E, TOWN])
    cc1i = pr.dram_tmp("cc1i", [DIN, 3], BF16)
    cc1o = pr.dram_tmp("cc1o", [2 * DIN, 3], BF16)
    cc2i = pr.dram_tmp("cc2i", [DIN, TOWN])
    cc2o = pr.dram_tmp("cc2o", [2 * DIN, TOWN])
    cc3i = pr.dram_tmp("cc3i", [DIN, TOWN], BF16)
    cc3o = pr.dram_tmp("cc3o", [2 * DIN, TOWN], BF16)
    cc4i = pr.dram_tmp("cc4i", [32, TOWN])
    cc4o = pr.dram_tmp("cc4o", [64, TOWN])
    brep_d = pr.dram_tmp("brep_d", [32 * 128, TOWN], BF16)
    crep_d = pr.dram_tmp("crep_d", [32 * 128, TOWN], BF16)
    cc5i = pr.dram_tmp("cc5i", [DIN, TOWN], BF16)
    cc5o = pr.dram_tmp("cc5o", [2 * DIN, TOWN], BF16)

    gbsb = pr.sbuf("gbsb", 128, 16)
    pr.dma("scalar", vw(gbsb), vw(gb_d))
    lgbsb = pr.sbuf("lgbsb", 128, 16)
    pr.dma("scalar", vw(lgbsb), vw(lgb_d))
    gbias = pr.sbuf("gbiasb", 128, 16)
    pr.dma("scalar", vw(gbias), vw(gbias_d))
    colsb = pr.sbuf("colsb", 128, 32)
    pr.dma("scalar", vw(colsb), vw(cols_d))
    maskb = pr.sbuf("maskb", 128, 1)
    pr.dma("scalar", vw(maskb), vw(maskbf_d))
    mask0 = pr.sbuf("mask0b", 128, 1)
    pr.dma("scalar", vw(mask0), vw(mask0_d))

    xn0T = FMv(SBF, 0, 8, S)
    hbuf = FMv(SBF, 32768, 24, 512)
    wlb = Sub(SBF, 45056, 128, 2048)
    g0st = Sub(SBF, 47616, 128, 512)
    wdlb = Sub(SBF, 48128, 128, 3072)
    xls = [Sub(SF, k * 512, 128, 512) for k in range(4)]
    wlf = Sub(SF, 2048, 128, 2048)
    sgq = Sub(SF, 4608, 128, 512)
    wdlf = Sub(SF, 5120, 128, 3072)

    cache = {}

    def getx0(i, f0, f1, _c=cache):
        key = (i, f0)
        if key in _c:
            return _c[key]
        dst = xls[(i + (f0 // 512) * 3) % 4].v()
        pr.dma("sync", dst, dview(x0T_d, x0T_d.h[i * 128:(i + 1) * 128, f0:f1]))
        _c.clear()
        _c[key] = dst
        return dst

    ph_ln_stream(pr, "ln0", getx0, 8, S, lambda i, f0, f1: xn0T.v(i, f0, f1), C)

    kslot = [0]

    def fetch(dbuf, src_ap, fsub, bsub, width, nslots, q, conv="vector"):
        s = kslot[0] % nslots
        kslot[0] += 1
        dbf = bsub.v(0, 128, s * width, (s + 1) * width)
        pr.dma("gpsimd", dbf, dview(dbuf, src_ap))
        return dbf

    psg, psu, psd = pr.pb[4], pr.pb[5], pr.pb[6]
    for tq in range(4):
        t0 = tq * 512
        for ht in range(24):
            wg = fetch(wgT_d, rearr_block(wgT_d.h, ht * 128, (ht + 1) * 128), wlf, wlb,
                       1024, 2, ("sync", "gpsimd")[ht % 2])
            wu = fetch(wuT_d, rearr_block(wuT_d.h, ht * 128, (ht + 1) * 128), wlf, wlb,
                       1024, 2, ("gpsimd", "sync")[ht % 2])
            for ec in range(8):
                lg = V(wg.buf, 0, 128, wg.rect[2] + ec * 128, wg.rect[2] + (ec + 1) * 128)
                lu = V(wu.buf, 0, 128, wu.rect[2] + ec * 128, wu.rect[2] + (ec + 1) * 128)
                rhs = xn0T.v(ec, t0, t0 + 512)
                pr.mm(vw(psg), lg, rhs, start=(ec == 0), stop=(ec == 7))
                pr.mm(vw(psu), lu, rhs, start=(ec == 0), stop=(ec == 7))
            pr.act(sgq.v(), vw(psg), AF.Silu, scale=1.0 / 127.0)
            pr.vtt(hbuf.v(ht), sgq.v(), vw(psu), ALU.mult)
        for et in range(8):
            pr.dma("scalar", wdlf.v(),
                   dview(wdT_d, rearr_block(wdT_d.h, et * 128, (et + 1) * 128)))
            pr.copy(wdlb.v(), wdlf.v())
            for ht in range(24):
                pr.mm(vw(psd), wdlb.v(0, 128, ht * 128, (ht + 1) * 128), hbuf.v(ht),
                      start=(ht == 0), stop=(ht == 23))
            pr.act(g0st.v(), vw(psd), AF.Silu, scale=1.0 / (127.0 * 127.0))
            pr.dma("scalar", dview(g0_d, g0_d.h[et * 128:(et + 1) * 128, t0:t0 + 512]),
                   g0st.v())

    s0preT = FMv(SF, 0, 8, TOWN)
    qT = Sub(SBF, 32768, 64, 2048)
    kT = Sub(SBF, 34816, 64, 4096)
    vhat = Sub(SBF, 38912, 128, 2048)
    band = Sub(SBF, 40960, 128, 8192)
    awlb = Sub(SBF, 50176, 128, 1024)
    expS = Sub(SBF, 49152, 128, 1024)
    bb32 = Sub(SF, 16384, 128, 1024)
    vtmp = Sub(SF, 17408, 64, 512)
    rzrep = Sub(SF, 17920, 128, 512)
    atmp = Sub(SF, 18432, 128, 512)
    awlf = Sub(SF, 18944, 128, 1536)
    ccol = Sub(SF, 20480, 128, 1)
    vsum = Sub(SF, 20482, 1, 64)

    psP, psT_, psR, psZb, psA, psB = pr.pb[0], pr.pb[1], pr.pb[2], pr.pb[3], pr.pb[4], pr.pb[5]
    psS = [pr.pb[6], pr.pb[7]]
    aslot = [0]

    def afetch(dbuf, src_ap, q):
        s = aslot[0] % 2
        aslot[0] += 1
        d32 = awlf.v(0, 128, s * 512, (s + 1) * 512)
        dbf = awlb.v(0, 128, s * 512, (s + 1) * 512)
        pr.dma(q, d32, dview(dbuf, src_ap))
        pr.copy(dbf, d32)
        return dbf

    for h in range(H):
        p0 = (h % 2) * 64
        et = h // 2
        pr.dma("gpsimd", ccol.v(), dview(cvec_d, bass.AP(cvec_d.h, h, [[0, 128], [1, 1]])))
        for d in range(16):
            dst = bb32.v(0, 128, (d % 2) * 512, (d % 2) * 512 + 512)
            pr.dma("sync", dst, dview(W2sh_d, W2sh_d.h[h * 128:(h + 1) * 128,
                                                       1920 - 128 * d:2432 - 128 * d]))
            pr.vts(band.v(0, 128, d * 512, (d + 1) * 512), dst, ccol.v(), None, ALU.subtract)
        wq = afetch(wqT_d, rearr_block(wqT_d.h, h * 64, (h + 1) * 64), "sync")
        wk = afetch(wkT_d, rearr_block(wkT_d.h, h * 64, (h + 1) * 64), "gpsimd")
        wv = afetch(wvT_d, rearr_block(wvT_d.h, h * 64, (h + 1) * 64), "scalar")

        def proj(wbf, nblk, dst):
            for qb in range(nblk):
                pp = V(psP, 0, 64, 0, 512)
                for ec in range(8):
                    lw = V(wbf.buf, 0, 128, wbf.rect[2] + ec * 64, wbf.rect[2] + (ec + 1) * 64)
                    pr.mm(pp, lw, xn0T.v(ec, qb * 512, (qb + 1) * 512),
                          start=(ec == 0), stop=(ec == 7))
                pr.act(dst.v(0, 64, qb * 512, (qb + 1) * 512), pp, AF.Copy)

        proj(wq, 4, qT)
        proj(wk, 8, kT)
        for qb in range(8):
            pp = V(psP, 0, 64, 0, 512)
            for ec in range(8):
                lw = V(wv.buf, 0, 128, wv.rect[2] + ec * 64, wv.rect[2] + (ec + 1) * 64)
                pr.mm(pp, lw, xn0T.v(ec, qb * 512, (qb + 1) * 512),
                      start=(ec == 0), stop=(ec == 7))
            pr.copy(vtmp.v(), pp)
            for j in range(4):
                tc = qb * 4 + j
                pt = V(psT_, 0, 128, 0, 64)
                pr.transp(pt, vtmp.v(0, 64, j * 128, (j + 1) * 128), C["identN"](64))
                pr.copy(vhat.v(0, 128, tc * 64, (tc + 1) * 64), pt)
        pc = V(psZb, 0, 1, 0, 64)
        for tc in range(32):
            pr.mm(pc, C["ones_col_bf"], vhat.v(0, 128, tc * 64, (tc + 1) * 64),
                  start=(tc == 0), stop=(tc == 31))
        pr.vts(vsum.v(), pc, V(ccol.buf, 0, 1, ccol.off, ccol.off + 1), None, ALU.mult)
        for qb in range(4):
            pA = V(psA, p0, p0 + 64, 0, 512)
            pZ = V(psZb, p0, p0 + 1, 0, 512)
            for kc2 in range(16):
                for j2 in range(2):
                    kc = kc2 * 2 + j2
                    pr.mm(vw(psS[j2]), kT.v(0, 64, kc * 128, (kc + 1) * 128),
                          qT.v(0, 64, qb * 512, (qb + 1) * 512))
                for j2 in range(2):
                    kc = kc2 * 2 + j2
                    es = expS.v(0, 128, j2 * 512, j2 * 512 + 512)
                    pr.act(es, vw(psS[j2]), AF.Exp, scale=1.0 / 32.0)
                for j2 in range(2):
                    kc = kc2 * 2 + j2
                    es = expS.v(0, 128, j2 * 512, j2 * 512 + 512)
                    pr.mm(pA, vhat.v(0, 128, kc * 64, (kc + 1) * 64), es,
                          start=(kc == 0), stop=(kc == 31))
                    pr.mm(pZ, C["ones_col_bf"], es, start=(kc == 0), stop=(kc == 31))
            pB = V(psB, p0, p0 + 64, 0, 512)
            bl = list(range(16 + 4 * qb, 32))
            for j, kc in enumerate(bl):
                d = kc - 16 - 4 * qb
                pr.mm(pB, vhat.v(0, 128, kc * 64, (kc + 1) * 64),
                      band.v(0, 128, d * 512, (d + 1) * 512), start=(j == 0), stop=False)
            pr.mm(pB, vsum.v(), C["ones512_at"](0), start=False, stop=True)
            rzs = atmp.v(p0, p0 + 1, 0, 512)
            pr.recip(rzs, pZ)
            pr.mm(vw(psR), C["ones_row_at"](p0), rzs)
            pr.copy(rzrep.v(), vw(psR))
            o1 = V(s0preT.buf, p0, p0 + 64, et * TOWN + qb * 512, et * TOWN + (qb + 1) * 512)
            pr.vtt(o1, pA, rzrep.v(p0, p0 + 64, 0, 512), ALU.mult)
            pr.vtt(o1, o1, pB, ALU.add)
    if dbg:
        s0pre_o = pr.dram_out("s0pre_o", [E, TOWN])
        for i in range(8):
            pr.dma("gpsimd", dview(s0pre_o, s0pre_o.h[i * 128:(i + 1) * 128, :]), s0preT.v(i))
        qT_o = pr.dram_out("qT_o", [64, TOWN])
        pr.dma("gpsimd", dview(qT_o, qT_o.h[:, :]), qT.v(0, 64, 0, 2048))
    ph_ln_stream(pr, "s0ln", lambda i, f0, f1: s0preT.v(i, f0, f1), 8, TOWN,
                 lambda i, f0, f1: s0preT.v(i, f0, f1), C,
                 gamma=lambda i: V(gbsb, 0, 128, i, i + 1),
                 beta=lambda i: V(gbsb, 0, 128, 8 + i, 9 + i))
    for i in range(8):
        pr.dma("gpsimd", dview(s0_d, s0_d.h[i * 128:(i + 1) * 128, :]), s0preT.v(i))

    xn1T = FMv(SBF, 0, 8, TOWN)
    ln2T = FMv(SBF, 16384, 8, TOWN)
    g0l = Sub(SBF, 32768, 128, 1024)
    ipb = Sub(SBF, 33792, 128, 2048)
    ipst = Sub(SBF, 35840, 128, 2048)
    x1s = Sub(SF, 0, 128, 1024)
    x0l = Sub(SF, 1024, 128, 1024)
    s0l = Sub(SF, 2048, 128, 1024)
    x1l = [Sub(SF, 3072 + k * 512, 128, 512) for k in range(4)]
    ipf = Sub(SF, 5120, 128, 2048)

    for i in range(8):
        for bk in range(4):
            f0, f1 = bk * 512, (bk + 1) * 512
            par = bk % 2
            xv = x0l.v(0, 128, par * 512, par * 512 + 512)
            sv = s0l.v(0, 128, par * 512, par * 512 + 512)
            gv = g0l.v(0, 128, par * 512, par * 512 + 512)
            pr.dma("sync", xv, dview(x0T_d, x0T_d.h[i * 128:(i + 1) * 128, f0:f1]))
            pr.dma("gpsimd", sv, dview(s0_d, s0_d.h[i * 128:(i + 1) * 128, f0:f1]))
            pr.dma("scalar", gv, dview(g0_d, g0_d.h[i * 128:(i + 1) * 128, f0:f1]))
            ov = x1s.v(0, 128, par * 512, par * 512 + 512)
            pr.vtt(ov, sv, gv, ALU.mult)
            pr.vtt(ov, ov, xv, ALU.add)
            pr.dma("sync", dview(x1_d, x1_d.h[i * 128:(i + 1) * 128, f0:f1]), ov)

    c2 = {}

    def getx1(i, f0, f1, _c=c2):
        key = (i, f0)
        if key in _c:
            return _c[key]
        dst = x1l[(i + (f0 // 512) * 3) % 4].v()
        pr.dma("sync", dst, dview(x1_d, x1_d.h[i * 128:(i + 1) * 128, f0:f1]))
        _c.clear()
        _c[key] = dst
        return dst

    ph_ln_stream(pr, "ln1", getx1, 8, TOWN, lambda i, f0, f1: xn1T.v(i, f0, f1), C)
    for i in range(8):
        pr.dma("gpsimd", dview(xn1_d, xn1_d.h[i * 128:(i + 1) * 128, :]), xn1T.v(i))
    ph_ln_stream(pr, "ln2", lambda i, f0, f1: xn1T.v(i, f0, f1), 8, TOWN,
                 lambda i, f0, f1: ln2T.v(i, f0, f1), C,
                 gamma=lambda i: V(lgbsb, 0, 128, i, i + 1),
                 beta=lambda i: V(lgbsb, 0, 128, 8 + i, 9 + i), in_bf=True)
    ps4 = [pr.pb[4], pr.pb[5], pr.pb[6], pr.pb[7]]
    islot = [0]
    for rt in range(32):
        s = islot[0] % 2
        islot[0] += 1
        d32 = ipf.v(0, 128, s * 1024, (s + 1) * 1024)
        dbf = ipb.v(0, 128, s * 1024, (s + 1) * 1024)
        pr.dma(("sync", "gpsimd")[rt % 2], d32,
               dview(ipT_d, rearr_block(ipT_d.h, rt * 128, (rt + 1) * 128)))
        pr.copy(dbf, d32)
        for ec in range(8):
            lw = V(dbf.buf, 0, 128, dbf.rect[2] + ec * 128, dbf.rect[2] + (ec + 1) * 128)
            for tb in range(4):
                pr.mm(vw(ps4[tb]), lw, ln2T.v(ec, tb * 512, (tb + 1) * 512),
                      start=(ec == 0), stop=(ec == 7))
        for tb in range(4):
            sv = ipst.v(0, 128, tb * 512, (tb + 1) * 512)
            if rt < 16:
                pr.act(sv, vw(ps4[tb]), AF.Copy)
            else:
                pr.act(sv, vw(ps4[tb]), AF.Silu)
        dst = xp_d if rt < 16 else zs_d
        r0 = (rt % 16) * 128
        pr.dma("scalar", dview(dst, dst.h[r0:r0 + 128, :]), ipst.v())
        if rt == 15:
            pr.dma("gpsimd", dview(cc1i, cc1i.h[:, :]), dview(xp_d, xp_d.h[:, TOWN - 3:TOWN]))
            pr.cc_op("AllGather", cc1i, cc1o, PAIRS)

    cw = pr.sbuf("cwf", 128, 64)
    pr.dma("sync", vw(cw), dview(convw_d, convw_d.h[:, :].rearrange("(a p) k -> p a k", p=128)))
    cbuf = Sub(SBF, 32768, 128, 2052)
    ul = Sub(SBF, 34820, 128, 1024)
    dtp = Sub(SBF, 35844, 64, 2048)
    wdt = Sub(SBF, 37892, 64, 2048)
    wx = Sub(SBF, 39940, 128, 1536)
    facc = Sub(SF, 0, 128, 2048)
    xdbl = Sub(SF, 2048, 96, 2048)
    sp1 = Sub(SF, 4096, 128, 512)
    sp2 = Sub(SF, 4608, 128, 512)
    sp3 = Sub(SF, 5120, 128, 512)
    wd32 = Sub(SF, 5632, 64, 2048)
    wx32 = Sub(SF, 7680, 128, 1536)
    dtst = Sub(SF, 9216, 128, 2048)
    ust = Sub(SBF, 41476, 128, 2048)

    for dt_ in range(16):
        pr.dma("sync", cbuf.v(0, 128, 0, 3),
               dview(cc1o, cc1o.h[dt_ * 128:(dt_ + 1) * 128, :]))
        pr.vts(cbuf.v(0, 128, 0, 3), cbuf.v(0, 128, 0, 3), vw(maskb), None, ALU.mult)
        pr.dma("sync", cbuf.v(0, 128, 3, 3 + TOWN),
               dview(xp_d, xp_d.h[dt_ * 128:(dt_ + 1) * 128, :]))

        def win(k):
            return cbuf.v(0, 128, k, k + TOWN)

        pr.vts(facc.v(), win(0), V(cw, 0, 128, dt_ * 4, dt_ * 4 + 1), None, ALU.mult)
        for k in range(1, 4):
            pr.stt(facc.v(), win(k), V(cw, 0, 128, dt_ * 4 + k, dt_ * 4 + k + 1),
                   facc.v(), ALU.mult, ALU.add)
        pr.act(ust.v(), facc.v(), AF.Silu, bias=V(colsb, 0, 128, dt_, dt_ + 1))
        pr.dma("gpsimd", dview(cc3i, cc3i.h[dt_ * 128:(dt_ + 1) * 128, :]), ust.v())
    pr.dma("sync", wx32.v(), dview(xprojT_d, xprojT_d.h[:, :].rearrange("(a p) c -> p a c", p=128)))
    pr.copy(wx.v(0, 128, 0, 1536), wx32.v())
    psX = pr.pb[0]
    for tb in range(4):
        px = V(psX, 0, 96, 0, 512)
        for dc in range(16):
            uv = ul.v(0, 128, (dc % 2) * 512, (dc % 2) * 512 + 512)
            pr.dma("gpsimd", uv, dview(cc3i, cc3i.h[dc * 128:(dc + 1) * 128,
                                                    tb * 512:(tb + 1) * 512]))
            pr.mm(px, wx.v(0, 128, dc * 96, (dc + 1) * 96), uv,
                  start=(dc == 0), stop=(dc == 15))
        pr.copy(xdbl.v(0, 96, tb * 512, (tb + 1) * 512), px)
    pr.dma("gpsimd", dview(cc4i, cc4i.h[:, :]), xdbl.v(64, 96, 0, TOWN))
    pr.copy(dtp.v(0, 64, 0, 2048), xdbl.v(0, 64, 0, TOWN))
    pr.dma("sync", wd32.v(0, 64, 0, 2048), vw(dtprojT_d))
    pr.copy(wdt.v(0, 64, 0, 2048), wd32.v(0, 64, 0, 2048))
    psD = pr.pb[1]
    for dt_ in range(16):
        for tb in range(4):
            pd = vw(psD)
            pr.mm(pd, wdt.v(0, 64, dt_ * 128, (dt_ + 1) * 128),
                  dtp.v(0, 64, tb * 512, (tb + 1) * 512))
            pr.act(sp1.v(), pd, AF.Identity, bias=V(colsb, 0, 128, 16 + dt_, 17 + dt_))
            pr.act(sp2.v(), sp1.v(), AF.Abs)
            pr.act(sp2.v(), sp2.v(), AF.Exp, scale=-1.0)
            pr.act(sp2.v(), sp2.v(), AF.Ln, bias=1.0)
            pr.act(sp3.v(), sp1.v(), AF.Relu)
            pr.vtt(dtst.v(0, 128, tb * 512, (tb + 1) * 512), sp3.v(), sp2.v(), ALU.add)
        pr.dma("scalar", dview(cc2i, cc2i.h[dt_ * 128:(dt_ + 1) * 128, :]), dtst.v())
    pr.cc_op("AllGather", cc2i, cc2o, PAIRS)
    pr.cc_op("AllGather", cc3i, cc3o, PAIRS)
    pr.cc_op("AllGather", cc4i, cc4o, PAIRS)
    bcst = Sub(SBF, 49152, 128, 2048)
    for hn in range(32):
        pr.dma("gpsimd", bcst.v(),
               dview(cc4o, bass.AP(cc4o.h, (hn // 16) * 32 * TOWN + (hn % 16) * TOWN,
                                   [[0, 128], [1, TOWN]])))
        pr.dma("sync", dview(brep_d, brep_d.h[hn * 128:(hn + 1) * 128, :]), bcst.v())
        pr.dma("gpsimd", bcst.v(),
               dview(cc4o, bass.AP(cc4o.h, (hn // 16) * 32 * TOWN + (16 + hn % 16) * TOWN,
                                   [[0, 128], [1, TOWN]])))
        pr.dma("sync", dview(crep_d, crep_d.h[hn * 128:(hn + 1) * 128, :]), bcst.v())

    rw32 = Sub(SF, 0, 128, 64)
    rwbf = Sub(SBF, 43008, 128, 2048)
    pr.dma("sync", rw32.v(), dview(rw_d, rw_d.h[:, :].rearrange("(a p) c -> p a c", p=128)))
    pr.copy(rwbf.v(0, 128, 0, 64), rw32.v())
    rbsb = Sub(SF, 64, 8, 1)
    pr.dma("sync", rbsb.v(0, 8, 0, 1), vw(rb_d))
    lgT = Sub(SF, 2048, 8, 2048)
    ps0 = pr.pb[0]
    for tb in range(4):
        p8 = V(ps0, 0, 8, 0, 512)
        for ec in range(8):
            pr.mm(p8, rwbf.v(0, 128, ec * 8, (ec + 1) * 8),
                  xn1T.v(ec, tb * 512, (tb + 1) * 512), start=(ec == 0), stop=(ec == 7))
        pr.act(lgT.v(0, 8, tb * 512, (tb + 1) * 512), p8, AF.Identity,
               bias=rbsb.v(0, 8, 0, 1))
    gbuf = pr.sbuf("gbuff", 128, 128)
    gTb = Sub(SBF, 43008, 8, 2048)
    lt = Sub(SF, 128, 128, 8)
    m1 = Sub(SF, 640, 128, 4)
    e1 = Sub(SF, 1024, 128, 8)
    ps1_, ps2_ = pr.pb[1], pr.pb[2]
    for tt in range(16):
        pt = V(ps1_, 0, 128, 0, 8)
        pr.transp(pt, lgT.v(0, 8, tt * 128, (tt + 1) * 128), C["identN"](8))
        pr.copy(lt.v(), pt)
        pr.reduce(m1.v(0, 128, 0, 1), lt.v(), ALU.max)
        pr.vts(e1.v(), lt.v(), m1.v(0, 128, 0, 1), None, ALU.is_equal)
        pr.stt(e1.v(), e1.v(), -1e30, lt.v(), ALU.mult, ALU.add)
        pr.reduce(m1.v(0, 128, 1, 2), e1.v(), ALU.max)
        pr.vts(m1.v(0, 128, 2, 3), m1.v(0, 128, 0, 1), -1.0, None, ALU.mult)
        pr.act(e1.v(), lt.v(), AF.Exp, bias=m1.v(0, 128, 2, 3))
        gv = V(gbuf, 0, 128, tt * 8, (tt + 1) * 8)
        pr.vts(gv, lt.v(), m1.v(0, 128, 1, 2), None, ALU.is_ge)
        pr.vtt(gv, gv, e1.v(), ALU.mult)
        pr.reduce(m1.v(0, 128, 3, 4), gv, ALU.add)
        pr.recip(m1.v(0, 128, 3, 4), m1.v(0, 128, 3, 4))
        pr.vts(gv, gv, m1.v(0, 128, 3, 4), None, ALU.mult)
        pg = V(ps2_, 0, 8, 0, 128)
        pr.transp(pg, gv, C["ident"])
        pr.copy(V(gTb.buf, 0, 8, gTb.off + tt * 128, gTb.off + (tt + 1) * 128), pg)
    macc = Sub(SF, 4736, 128, 16384)
    b232 = Sub(SF, 0, 8, 1024)
    b2bf = Sub(SBF, 46080, 8, 1024)
    pr.dma("sync", b232.v(0, 8, 0, 1024), vw(b2_d))
    pr.copy(b2bf.v(0, 8, 0, 1024), b232.v(0, 8, 0, 1024))
    ps3_ = pr.pb[3]
    for tt in range(16):
        for cb in range(2):
            pb_ = vw(ps3_)
            pr.mm(pb_, V(gTb.buf, 0, 8, gTb.off + tt * 128, gTb.off + (tt + 1) * 128),
                  b2bf.v(0, 8, cb * 512, (cb + 1) * 512))
            pr.copy(macc.v(0, 128, tt * 1024 + cb * 512, tt * 1024 + (cb + 1) * 512), pb_)
    asb = pr.sbuf("asbf", 128, 128)
    pr.dma("sync", vw(asb), vw(alog_d))
    pr.act(vw(asb), vw(asb), AF.Exp)
    pr.vts(vw(asb), vw(asb), -1.0, None, ALU.mult)
    dsb = pr.sbuf("dsbf", 128, 8)
    pr.dma("sync", vw(dsb), vw(dD_d))
    dts = [Sub(SF, 0, 128, 512), Sub(SF, 512, 128, 512)]
    yac = Sub(SF, 1024, 128, 512)
    brch = [Sub(SBF, 47104, 128, 512), Sub(SBF, 47616, 128, 512)]
    crch = [Sub(SBF, 48128, 128, 512), Sub(SBF, 48640, 128, 512)]
    hbf2 = [Sub(SF, 1536, 128, 512), Sub(SF, 3200, 128, 512)]
    da2 = [Sub(SF, 2048, 128, 512), Sub(SF, 3712, 128, 512)]
    db = Sub(SF, 2560, 128, 512)
    carry = Sub(SF, 3072, 128, 16)
    hc2 = [Sub(SBF, 49152, 128, 512), Sub(SBF, 49664, 128, 512)]
    usl = [Sub(SBF, 16384, 128, 512), Sub(SBF, 16896, 128, 512)]
    dub = Sub(SBF, 17408, 128, 512)
    yst = Sub(SBF, 17920, 128, 512)
    hq = FMv(SBF, 18432, 8, 2048)
    w2b = Sub(SBF, 34816, 128, 8192)
    w1b = Sub(SBF, 45056, 128, 1024)
    w1b2 = Sub(SBF, 50176, 128, 1024)
    b1sb = pr.sbuf("b1sbf", 128, 32)
    psBr, psCr = pr.pb[0], pr.pb[1]
    psY = [pr.pb[2], pr.pb[3]]
    psH = [pr.pb[4], pr.pb[5], pr.pb[6], pr.pb[7]]

    def scan_unit(dt_, tc):
        half = tc // 4
        cl0 = (tc % 4) * 512
        sl_ = tc % 2
        r0 = half * 2048 + dt_ * 128
        ca, cb = da2[0].v(), db.v()
        pr.dma("sync", ca, dview(cc2o, cc2o.h[r0:r0 + 128, cl0:cl0 + 512]))
        pr.dma("sync", cb, dview(cc2o, cc2o.h[r0 + 1024:r0 + 1152, cl0:cl0 + 512]))
        pr.act(hbf2[0].v(), ca, AF.Identity, scale=V(mask0, 0, 128, 0, 1))
        pr.stt(dts[sl_].v(), cb, V(maskb, 0, 128, 0, 1), hbf2[0].v(), ALU.mult, ALU.add)
        ua, ub = usl[1 - sl_].v(), yst.v()
        pr.dma("gpsimd", ua, dview(cc3o, cc3o.h[r0:r0 + 128, cl0:cl0 + 512]))
        pr.dma("gpsimd", ub, dview(cc3o, cc3o.h[r0 + 1024:r0 + 1152, cl0:cl0 + 512]))
        pr.act(hbf2[1].v(), ua, AF.Identity, scale=V(mask0, 0, 128, 0, 1))
        pr.stt(usl[sl_].v(), ub, V(maskb, 0, 128, 0, 1), hbf2[1].v(), ALU.mult, ALU.add)
        pr.vtt(dub.v(), dts[sl_].v(), usl[sl_].v(), ALU.mult)
        for n in range(NST):
            acol = V(asb, 0, 128, dt_ * NST + n, dt_ * NST + n + 1)
            bv = brch[n % 2].v()
            cv = crch[n % 2].v()
            rr2 = (half * 16 + n) * 128
            pr.dma("sync", bv, dview(brep_d, brep_d.h[rr2:rr2 + 128, cl0:cl0 + 512]))
            pr.dma("scalar", cv, dview(crep_d, crep_d.h[rr2:rr2 + 128, cl0:cl0 + 512]))
            da = da2[n % 2]
            hbf = hbf2[n % 2]
            hc = hc2[n % 2]
            pr.act(da.v(), dts[sl_].v(), AF.Exp, scale=acol)
            eng1 = "gpsimd" if n % 2 == 0 else "vector"
            eng2 = "vector" if n % 2 == 0 else "gpsimd"
            pr.vtt(db.v(), dub.v(), bv, ALU.mult, engine=eng1)
            ini = 0.0 if tc == 0 else V(carry.buf, 0, 128, carry.off + n, carry.off + n + 1)
            pr.scan_op(hbf.v(), da.v(), db.v(), ini)
            pr.copy(V(carry.buf, 0, 128, carry.off + n, carry.off + n + 1),
                    hbf.v(0, 128, 511, 512), engine="scalar")
            if n == 0:
                pr.vtt(yac.v(), hbf.v(), cv, ALU.mult)
            else:
                pr.vtt(hc.v(), hbf.v(), cv, ALU.mult, engine=eng2)
                pr.vtt(yac.v(), yac.v(), hc.v(), ALU.add, engine="gpsimd")
        dcol = V(dsb, 0, 128, dt_, dt_ + 1)
        pr.stt(yst.v(), usl[sl_].v(), dcol, yac.v(), ALU.mult, ALU.add)
        r2 = dt_ * 128 + (1 - half) * 0 + half * 1024
        pr.dma("scalar", dview(cc5i, cc5i.h[half * 1024 + dt_ * 128:half * 1024 + (dt_ + 1) * 128,
                                            cl0:cl0 + 512]), yst.v())

    def moe_unit(e, hc):
        if hc == 0:
            pr.dma("scalar", vw(b1sb), dview(b1r_d, b1r_d.h[e * 128:(e + 1) * 128, :]))
        for htl in range(8):
            ht = hc * 8 + htl
            src = w1_d.h[e * E:(e + 1) * E, ht * 128:(ht + 1) * 128].rearrange(
                "(a p) c -> p a c", p=128)
            w1bs = w1b if htl % 2 == 0 else w1b2
            pr.dma("gpsimd", w1bs.v(0, 128, 0, 1024), dview(w1_d, src))
            for ec in range(8):
                lw = V(w1bs.buf, 0, 128, w1bs.off + ec * 128, w1bs.off + (ec + 1) * 128)
                for tb in range(4):
                    pr.mm(vw(psH[tb]), lw, xn1T.v(ec, tb * 512, (tb + 1) * 512),
                          start=(ec == 0), stop=(ec == 7))
            for tb in range(4):
                pr.act(hq.v(htl, tb * 512, (tb + 1) * 512), vw(psH[tb]), AF.Relu,
                       bias=V(b1sb, 0, 128, ht, ht + 1))
            pr.dma("gpsimd", w2b.v(0, 128, htl * 1024, (htl + 1) * 1024),
                   dview(w2_d, w2_d.h[e * HIDM + ht * 128:e * HIDM + (ht + 1) * 128, :]))
        for tt in range(16):
            for cb in range(2):
                pY = vw(psY[cb])
                for htl in range(8):
                    pr.mm(pY, hq.v(htl, tt * 128, (tt + 1) * 128),
                          w2b.v(0, 128, htl * 1024 + cb * 512, htl * 1024 + (cb + 1) * 512),
                          start=(htl == 0), stop=(htl == 7))
                mv = macc.v(0, 128, tt * 1024 + cb * 512, tt * 1024 + (cb + 1) * 512)
                pr.stt(mv, pY, V(gbuf, 0, 128, tt * 8 + e, tt * 8 + e + 1), mv,
                       ALU.mult, ALU.add)

    su = [(d_, t_) for d_ in range(8) for t_ in range(8)]
    mu = [(e_, h_) for e_ in range(NEXP) for h_ in range(4)]
    si = 0
    for k, (e_, h_) in enumerate(mu):
        for _ in range(2):
            if si < len(su):
                scan_unit(*su[si])
                si += 1
        moe_unit(e_, h_)
    while si < len(su):
        scan_unit(*su[si])
        si += 1
    pr.cc_op("AllGather", cc5i, cc5o, PAIRS)
    if dbg:
        ydbg = pr.dram_out("ydbg", [2 * DIN, TOWN], BF16)
        pr.dma("gpsimd", dview(ydbg, ydbg.h[:, :]), dview(cc5o, cc5o.h[:, :]))

    ygT = FMv(SBF, 0, 16, TOWN)
    mgT = FMv(SBF, 32768, 8, TOWN)
    sgT = FMv(SBF, 0, 8, TOWN)
    fwb = Sub(SBF, 49152, 128, 2048)
    fwf = Sub(SF, 0, 128, 2048)
    yl = Sub(SBF, 16384 + 0, 128, 0, ) if False else None
    y0l = Sub(SBF, 49152, 128, 0) if False else None
    ylr = Sub(SBF, 47104, 128, 1024)
    zlr = Sub(SBF, 48128, 128, 1024)
    ftmp = Sub(SF, 2048, 128, 512)
    fstage = Sub(SF, 2560, 128, 512)
    xnl = Sub(SBF, 16384, 128, 1024)

    ybl = Sub(SF, 4096, 128, 512)
    for dc in range(16):
        yva = ylr.v(0, 128, 0, 512)
        yvb = ylr.v(0, 128, 512, 1024)
        zv = zlr.v(0, 128, (dc % 2) * 512, (dc % 2) * 512 + 512)
        hd = dc // 8
        rr = hd * 2048 + (dc % 8) * 128
        for tb in range(4):
            pr.dma("sync", yva, dview(cc5o, cc5o.h[rr:rr + 128, tb * 512:(tb + 1) * 512]))
            pr.dma("sync", yvb, dview(cc5o, cc5o.h[rr + 1024:rr + 1152, tb * 512:(tb + 1) * 512]))
            pr.act(ybl.v(), yva, AF.Identity, scale=V(mask0, 0, 128, 0, 1))
            pr.stt(yva, yvb, V(maskb, 0, 128, 0, 1), ybl.v(), ALU.mult, ALU.add)
            pr.dma("gpsimd", zv, dview(zs_d, zs_d.h[dc * 128:(dc + 1) * 128,
                                                    tb * 512:(tb + 1) * 512]))
            pr.vtt(ygT.v(dc, tb * 512, (tb + 1) * 512), yva, zv, ALU.mult)
    fslot = [0]

    def ffetch(dbuf, src_ap, q):
        s = fslot[0] % 2
        fslot[0] += 1
        d32 = fwf.v(0, 128, s * 1024, (s + 1) * 1024)
        dbf = fwb.v(0, 128, s * 1024, (s + 1) * 1024)
        pr.dma(q, d32, dview(dbuf, src_ap))
        pr.copy(dbf, d32)
        return dbf

    ps4b = [pr.pb[4], pr.pb[5], pr.pb[6], pr.pb[7]]
    for et in range(8):
        for hf in range(2):
            wb = ffetch(opT_d, opT_d.h[hf * 1024:(hf + 1) * 1024, et * 128:(et + 1) * 128]
                        .rearrange("(a p) c -> p a c", p=128), ("sync", "gpsimd")[hf])
            for dcl in range(8):
                dc = hf * 8 + dcl
                lw = V(wb.buf, 0, 128, wb.rect[2] + dcl * 128, wb.rect[2] + (dcl + 1) * 128)
                for tb in range(4):
                    pr.mm(vw(ps4b[tb]), lw, ygT.v(dc, tb * 512, (tb + 1) * 512),
                          start=(dc == 0), stop=(dc == 15))
        for tb in range(4):
            pr.act(mgT.v(et, tb * 512, (tb + 1) * 512), vw(ps4b[tb]), AF.Gelu)
    for half in range(2):
        for rt in range(8):
            wb = ffetch(gwT_d, rearr_block(gwT_d.h, (1 - half) * 1024 + rt * 128,
                                           (1 - half) * 1024 + (rt + 1) * 128),
                        ("sync", "gpsimd")[rt % 2])
            for ec in range(8):
                lw = V(wb.buf, 0, 128, wb.rect[2] + ec * 128, wb.rect[2] + (ec + 1) * 128)
                for tb in range(4):
                    pr.mm(vw(ps4b[tb]), lw, mgT.v(ec, tb * 512, (tb + 1) * 512),
                          start=(ec == 0), stop=(ec == 7))
            if half == 0:
                for tb in range(4):
                    pr.act(sgT.v(rt, tb * 512, (tb + 1) * 512), vw(ps4b[tb]), AF.Sigmoid,
                           bias=V(gbias, 0, 128, 8 + rt, 9 + rt))
            else:
                for tb in range(4):
                    pr.act(ftmp.v(), vw(ps4b[tb]), AF.Identity,
                           bias=V(gbias, 0, 128, rt, rt + 1))
                    xv = xnl.v(0, 128, (tb % 2) * 512, (tb % 2) * 512 + 512)
                    pr.dma("sync", xv, dview(xn1_d, xn1_d.h[rt * 128:(rt + 1) * 128,
                                                            tb * 512:(tb + 1) * 512]))
                    sv = fstage.v()
                    pr.vtt(sv, ftmp.v(), sgT.v(rt, tb * 512, (tb + 1) * 512), ALU.mult)
                    pr.vtt(sv, sv, xv, ALU.add)
                    pr.dma("scalar", dview(s1_d, s1_d.h[rt * 128:(rt + 1) * 128,
                                                        tb * 512:(tb + 1) * 512]), sv)

    g1t = Sub(SF, 0, 128, 1024)
    xl_ = Sub(SF, 1024, 128, 128)
    sl_ = Sub(SF, 1152, 128, 128)
    ost = Sub(SF, 1280, 128, 128)
    psG = pr.pb[0]
    for tt in range(16):
        pr.act(g1t.v(), macc.v(0, 128, tt * 1024, (tt + 1) * 1024), AF.Silu)
        for ec in range(8):
            pg = V(psG, 0, 128, 0, 128)
            pr.transp(pg, g1t.v(0, 128, ec * 128, (ec + 1) * 128), C["ident"])
            pr.dma("sync", xl_.v(), dview(x1_d, x1_d.h[ec * 128:(ec + 1) * 128,
                                                       tt * 128:(tt + 1) * 128]))
            pr.dma("gpsimd", sl_.v(), dview(s1_d, s1_d.h[ec * 128:(ec + 1) * 128,
                                                         tt * 128:(tt + 1) * 128]))
            pr.vtt(ost.v(), sl_.v(), pg, ALU.mult)
            pr.vtt(ost.v(), ost.v(), xl_.v(), ALU.add)
            pr.dma("scalar", dview(outT_o, outT_o.h[ec * 128:(ec + 1) * 128,
                                                    tt * 128:(tt + 1) * 128]), ost.v())
    return pr


def kernel_fused(inputs, dbg=False, _res=[]):
    P = _prep_all(inputs)
    cores = list(range(8))
    prF = build_fused(dbg)
    masks = [np.full((128, 1), float(c % 2), np.float32) for c in cores]
    masks0 = [np.full((128, 1), float(1 - c % 2), np.float32) for c in cores]
    res = run_bass_kernel_spmd(prF.emit(), [dict(
        consts=P["consts"], x0T=P["x0T_rot"][c], wgT=P["wgT"], wuT=P["wuT"], wdT=P["wdT"],
        wqT=P["wqT"], wkT=P["wkT"], wvT=P["wvT"], W2sh=P["W2sh"][c], cvec=P["cvec"],
        gb=P["gb"], lgb=P["lgb"], ipT=P["ipT"], convw=P["convw"], cols=P["cols"],
        xprojT=P["xprojT"], dtprojT=P["dtprojT"], alog=P["alog"][c], dD=P["dD"][c],
        maskbf=masks[c], mask0=masks0[c], opT=P["opT"], gwT=P["gwT"], gbias=P["gbias"], rw=P["rw"],
        rb=P["rb"], w1=P["w1"], b1r=P["b1r"], w2=P["w2"], b2=P["b2"])
        for c in cores], core_ids=cores)
    _res.clear()
    _res.append(res)
    out = np.empty((B, S, E), np.float32)
    for c in cores:
        b, th = c // 2, c % 2
        out[b, th * TOWN:(th + 1) * TOWN, :] = res.results[c]["outT_o"].T
    return out
